# version 29
# baseline (speedup 1.0000x reference)
import contextlib
import numpy as np
import concourse.bass as bass
import concourse.mybir as mybir
from concourse.bass_utils import run_bass_kernel_spmd

F32 = mybir.dt.float32
BF16 = mybir.dt.bfloat16
AF = mybir.ActivationFunctionType
ALU = mybir.AluOpType
AX = mybir.AxisListType

ENGINES = ("pe", "act", "dve", "pool", "sp")
DMA_QUEUES = ("sp", "pool", "act")


class TV:
    __slots__ = ("t", "p0", "p1", "f0", "f1")

    def __init__(self, t, p=None, f=None):
        self.t = t
        shape = t.shape
        self.p0, self.p1 = (p.start or 0, p.stop if p.stop is not None else shape[0]) if p else (0, shape[0])
        self.f0, self.f1 = (f.start or 0, f.stop if f.stop is not None else shape[1]) if f else (0, shape[1])

    @property
    def ap(self):
        return self.t[self.p0:self.p1, self.f0:self.f1]

    @property
    def name(self):
        return self.t.name

    def v(self, p=None, f=None):
        pp = slice(self.p0 + (p.start or 0), self.p0 + p.stop) if p else slice(self.p0, self.p1)
        ff = slice(self.f0 + (f.start or 0), self.f0 + f.stop) if f else slice(self.f0, self.f1)
        return TV(self.t, pp, ff)


def _overlap(a, b):
    return not (a[1] <= b[0] or b[1] <= a[0] or a[3] <= b[2] or b[3] <= a[2])


def _covers(a, b):
    return a[0] <= b[0] and a[1] >= b[1] and a[2] <= b[2] and a[3] >= b[3]


def _reg(tv):
    return (tv.p0, tv.p1, tv.f0, tv.f1)


class _DramReg:
    def __init__(self, name, shape=(1, 1)):
        self.name = name
        self.shape = list(shape)

    def __getitem__(self, _):
        return None


def dreg(name, shape, rows=None, cols=None):
    return TV(_DramReg(name, shape), rows, cols)


class FW:
    def __init__(self, nc, stack, n_dma_slots=4):
        self.nc = nc
        self.stack = stack
        self.q = {e: [] for e in ENGINES}
        self.sems = {}
        self.cnt = {}
        self.waited = {e: {} for e in ENGINES}
        for e in ENGINES:
            self._mksem("s_" + e)
        self.dma_slots = {}
        self.dma_rr = {}
        self.slot_uses = {}
        for qn in DMA_QUEUES:
            names = []
            for i in range(n_dma_slots):
                sn = f"dq_{qn}_{i}"
                self._mksem(sn)
                names.append(sn)
                self.slot_uses[sn] = 0
            self.dma_slots[qn] = names
            self.dma_rr[qn] = 0
        self.writes = {}
        self.reads = {}
        self.pe_pend = []
        self.psum_bank_state = {}
        self.psum_width = {}
        self.n_instr = 0

    def reg_psum(self, t, dtype):
        self.psum_width[t.name] = 512 if dtype == F32 else 1024

    def _mksem(self, name):
        self.sems[name] = self.stack.enter_context(self.nc.semaphore(name))
        self.cnt[name] = 0

    def _edge(self, eng, sem_name, val):
        if val <= 0:
            return
        if self.waited[eng].get(sem_name, 0) >= val:
            return
        self.waited[eng][sem_name] = val
        s = self.sems[sem_name]
        self.q[eng].append(lambda E, s=s, v=val: E.wait_ge(s, v))

    def barrier(self):
        for e in ENGINES:
            for sn, v in self.cnt.items():
                self._edge(e, sn, v)

    def _pend_guard(self, eng, tv, writing):
        if eng == "pe" or not self.pe_pend:
            return
        reg = _reg(tv)
        for kind, nm, r in self.pe_pend:
            if nm == tv.name and (writing or kind == "w") and _overlap(reg, r):
                raise RuntimeError(
                    f"{eng} touches {tv.name}{reg} while an open PE accumulation "
                    f"group still {'reads' if kind == 'r' else 'writes'} {r} — close "
                    f"the group (stop=True) first")

    def _breg(self, tv):
        w = self.psum_width.get(tv.name)
        if w is None:
            return _reg(tv)
        return (tv.p0, tv.p1, (tv.f0 // w) * w, ((tv.f1 + w - 1) // w) * w)

    def _haz_read(self, eng, tv):
        self._pend_guard(eng, tv, False)
        reg = self._breg(tv)
        for r in self.writes.get(tv.name, ()):
            if _overlap(reg, r):
                self._edge(eng, r[4], r[5])

    def _haz_write(self, eng, tv):
        self._pend_guard(eng, tv, True)
        reg = self._breg(tv)
        for r in self.writes.get(tv.name, ()):
            if _overlap(reg, r):
                self._edge(eng, r[4], r[5])
        for r in self.reads.get(tv.name, ()):
            if _overlap(reg, r):
                self._edge(eng, r[4], r[5])

    def _note_read(self, name, reg, sem, val):
        lst = self.reads.setdefault(name, [])
        for r in lst:
            if r[4] == sem and r[0] == reg[0] and r[1] == reg[1] and r[2] == reg[2] and r[3] == reg[3]:
                r[5] = max(r[5], val)
                return
        lst.append([reg[0], reg[1], reg[2], reg[3], sem, val])

    def _note_write(self, name, reg, sem, val):
        wl = self.writes.setdefault(name, [])
        wl[:] = [r for r in wl if not _covers(reg, r)]
        wl.append([reg[0], reg[1], reg[2], reg[3], sem, val])
        rl = self.reads.get(name)
        if rl:
            rl[:] = [r for r in rl if not _covers(reg, r)]

    def _emit(self, eng, fn, reads, writes):
        for tv in reads:
            self._haz_read(eng, tv)
        for tv in writes:
            self._haz_write(eng, tv)
        sem = "s_" + eng
        self.cnt[sem] += 1
        val = self.cnt[sem]
        s = self.sems[sem]
        self.q[eng].append(lambda E, fn=fn, s=s: fn(E).then_inc(s, 1))
        for tv in reads:
            self._note_read(tv.name, self._breg(tv), sem, val)
        for tv in writes:
            self._note_write(tv.name, self._breg(tv), sem, val)
        self.n_instr += 1
        return val

    def _pe_flush(self, val):
        for kind, nm, reg in self.pe_pend:
            if kind == "r":
                self._note_read(nm, reg, "s_pe", val)
            else:
                self._note_write(nm, reg, "s_pe", val)
        self.pe_pend.clear()

    def _psum_base_guard(self, out_tv, base, is_group_start):
        w = self.psum_width.get(out_tv.name)
        assert w is not None, f"psum tensor {out_tv.name} not registered"
        for bank in range(out_tv.f0 // w, (out_tv.f1 + w - 1) // w):
            key = (out_tv.name, bank)
            st = self.psum_bank_state.get(key)
            if st is not None and st[0] != base:
                if st[1] is None:
                    raise RuntimeError("mixed-base write into bank of an open PE group")
                self._edge("pe", "s_pe", st[1])
            self.psum_bank_state[key] = (base, None)

    def _psum_close(self, out_tvs, val):
        for out_tv in out_tvs:
            w = self.psum_width[out_tv.name]
            for bank in range(out_tv.f0 // w, (out_tv.f1 + w - 1) // w):
                key = (out_tv.name, bank)
                base = self.psum_bank_state[key][0]
                self.psum_bank_state[key] = (base, val)

    def mm(self, out_tv, lhsT_tv, rhs_tv, start, stop, skip_check=False):
        assert lhsT_tv.p0 == rhs_tv.p0, "lhsT/rhs partition bases must match"
        assert lhsT_tv.p1 - lhsT_tv.p0 == rhs_tv.p1 - rhs_tv.p0
        self._psum_base_guard(out_tv, lhsT_tv.p0, start)
        self._haz_read("pe", lhsT_tv)
        self._haz_read("pe", rhs_tv)
        self._haz_write("pe", out_tv)
        self.pe_pend.append(("r", lhsT_tv.name, self._breg(lhsT_tv)))
        self.pe_pend.append(("r", rhs_tv.name, self._breg(rhs_tv)))
        self.pe_pend.append(("w", out_tv.name, self._breg(out_tv)))
        o, l, r = out_tv.ap, lhsT_tv.ap, rhs_tv.ap
        if stop:
            sem = "s_pe"
            self.cnt[sem] += 1
            val = self.cnt[sem]
            s = self.sems[sem]
            self.q["pe"].append(lambda E, o=o, l=l, r=r, st=start, s=s, sk=skip_check:
                                E.matmul(o, l, r, start=st, stop=True, skip_group_check=sk).then_inc(s, 1))
            self._pe_flush(val)
            self._psum_close([out_tv], val)
        else:
            self.q["pe"].append(lambda E, o=o, l=l, r=r, st=start, sk=skip_check:
                                E.matmul(o, l, r, start=st, stop=False, skip_group_check=sk))
        self.n_instr += 1

    def transpose(self, out_tv, in_tv, ident_tv):
        self._psum_base_guard(out_tv, in_tv.p0, True)
        self._haz_read("pe", in_tv)
        self._haz_read("pe", ident_tv)
        self._haz_write("pe", out_tv)
        self.pe_pend.append(("r", in_tv.name, self._breg(in_tv)))
        self.pe_pend.append(("r", ident_tv.name, self._breg(ident_tv)))
        self.pe_pend.append(("w", out_tv.name, self._breg(out_tv)))
        o, i, idn = out_tv.ap, in_tv.ap, ident_tv.ap
        sem = "s_pe"
        self.cnt[sem] += 1
        val = self.cnt[sem]
        s = self.sems[sem]
        self.q["pe"].append(lambda E, o=o, i=i, idn=idn, s=s:
                            E.transpose(o, i, idn).then_inc(s, 1))
        self._pe_flush(val)
        self._psum_close([out_tv], val)
        self.n_instr += 1

    def act(self, out_tv, in_tv, func, bias=0.0, scale=1.0, accum=None):
        reads = [in_tv]
        writes = [out_tv]
        b = bias.ap if isinstance(bias, TV) else bias
        sc = scale.ap if isinstance(scale, TV) else scale
        if isinstance(bias, TV):
            reads.append(bias)
        if isinstance(scale, TV):
            reads.append(scale)
        acc_ap = None
        if accum is not None:
            writes.append(accum)
            acc_ap = accum.ap
        o, i = out_tv.ap, in_tv.ap
        self._emit("act", lambda E, o=o, i=i, f=func, b=b, sc=sc, a=acc_ap:
                   E.activation(o, i, f, bias=b, scale=sc, accum_out=a), reads, writes)

    def tt(self, op, out_tv, a_tv, b_tv, eng="dve"):
        o, a, b = out_tv.ap, a_tv.ap, b_tv.ap
        self._emit(eng, lambda E, o=o, a=a, b=b, op=op: E.tensor_tensor(o, a, b, op),
                   [a_tv, b_tv], [out_tv])

    def ts(self, out_tv, in_tv, s1, op0, s2=None, op1=None, eng="dve"):
        reads = [in_tv]
        a1 = s1.ap if isinstance(s1, TV) else s1
        if isinstance(s1, TV):
            reads.append(s1)
        a2 = s2.ap if isinstance(s2, TV) else s2
        if isinstance(s2, TV):
            reads.append(s2)
        kw = {}
        if op1 is not None:
            kw["op1"] = op1
        o, i = out_tv.ap, in_tv.ap
        self._emit(eng, lambda E, o=o, i=i, a1=a1, a2=a2, op0=op0, kw=kw:
                   E.tensor_scalar(o, i, a1, a2, op0, **kw), reads, [out_tv])

    def stt(self, out_tv, in0_tv, scalar, in1_tv, op0, op1, eng="dve"):
        reads = [in0_tv, in1_tv]
        sc = scalar.ap if isinstance(scalar, TV) else scalar
        if isinstance(scalar, TV):
            reads.append(scalar)
        o, i0, i1 = out_tv.ap, in0_tv.ap, in1_tv.ap
        self._emit(eng, lambda E, o=o, i0=i0, sc=sc, i1=i1, op0=op0, op1=op1:
                   E.scalar_tensor_tensor(o, i0, sc, i1, op0, op1), reads, [out_tv])

    def red(self, op, out_tv, in_tv, negate=False, eng="dve"):
        o, i = out_tv.ap, in_tv.ap
        self._emit(eng, lambda E, o=o, i=i, op=op, n=negate:
                   E.tensor_reduce(o, i, AX.X, op, negate=n), [in_tv], [out_tv])

    def copy(self, out_tv, in_tv, eng="dve"):
        o, i = out_tv.ap, in_tv.ap
        if eng == "act":
            self._emit(eng, lambda E, o=o, i=i: E.copy(o, i), [in_tv], [out_tv])
        else:
            self._emit(eng, lambda E, o=o, i=i: E.tensor_copy(o, i), [in_tv], [out_tv])

    def memset(self, tv, c, eng="dve"):
        a = tv.ap
        self._emit(eng, lambda E, a=a, c=c: E.memset(a, c), [], [tv])

    def recip(self, out_tv, in_tv):
        o, i = out_tv.ap, in_tv.ap
        self._emit("dve", lambda E, o=o, i=i: E.reciprocal(o, i), [in_tv], [out_tv])

    def _dma(self, queue, fn, read_tvs, write_tvs):
        slot_i = self.dma_rr[queue]
        self.dma_rr[queue] = (slot_i + 1) % len(self.dma_slots[queue])
        sn = self.dma_slots[queue][slot_i]
        self._edge(queue, sn, 16 * self.slot_uses[sn])
        for tv in read_tvs:
            self._haz_read(queue, tv)
        for tv in write_tvs:
            self._haz_write(queue, tv)
        self.slot_uses[sn] += 1
        self.cnt[sn] += 16
        val = self.cnt[sn]
        s = self.sems[sn]
        self.q[queue].append(lambda E, fn=fn, s=s: fn(E).then_inc(s, 16))
        for tv in read_tvs:
            self._note_read(tv.name, self._breg(tv), sn, val)
        for tv in write_tvs:
            self._note_write(tv.name, self._breg(tv), sn, val)
        self.n_instr += 1

    def load(self, queue, dst_tv, src_ap):
        d = dst_tv.ap
        self._dma(queue, lambda E, d=d, s=src_ap: E.dma_start(d, s), [], [dst_tv])

    def store(self, queue, dst_ap, src_tv):
        s = src_tv.ap
        self._dma(queue, lambda E, d=dst_ap, s=s: E.dma_start(d, s), [src_tv], [])

    def sb2sb(self, queue, dst_tv, src_tv):
        d, s = dst_tv.ap, src_tv.ap
        self._dma(queue, lambda E, d=d, s=s: E.dma_start(d, s), [src_tv], [dst_tv])

    def iscatter(self, out_ap, out_name, idx_tv, in_tv, bound, op, in_dram=None):
        from concourse.bass import IndirectOffsetOnAxis
        i = idx_tv.ap
        if in_dram is not None:
            s, s_tv = in_dram[0], TV(_DramReg(in_dram[1]))
        else:
            s, s_tv = in_tv.ap, in_tv
        fake = TV(_DramReg(out_name))

        def _go(E, o=out_ap, i=i, s=s, b=bound, op=op):
            try:
                return E.indirect_dma_start(o, IndirectOffsetOnAxis(i, 0), s, None,
                                            bounds_check=b, oob_is_err=False, compute_op=op)
            except Exception:
                print("ISCATTER-FAIL out=", o, "idx=", i, "in=", s, "bound=", b, "op=", op)
                raise
        self._dma("pool", _go,
                  [idx_tv, s_tv, fake] if op != ALU.bypass else [idx_tv, s_tv],
                  [fake])

    def igather(self, out_tv, in_ap, in_name, idx_tv, bound):
        from concourse.bass import IndirectOffsetOnAxis
        o, i = out_tv.ap, idx_tv.ap
        fake = TV(_DramReg(in_name))
        self._dma("pool", lambda E, o=o, i=i, s=in_ap, b=bound:
                  E.indirect_dma_start(o, None, s, IndirectOffsetOnAxis(i, 0),
                                       bounds_check=b, oob_is_err=False, compute_op=ALU.bypass),
                  [idx_tv, fake], [out_tv])

    def dram_store(self, queue, dst_ap, dst_name, src_tv, region=None, accum=None):
        s = src_tv.ap
        fake = region if region is not None else TV(_DramReg(dst_name))
        kw = {} if accum is None else {"accum_op": accum}
        reads = [src_tv] + ([fake] if accum is not None else [])
        self._dma(queue, lambda E, d=dst_ap, s=s, kw=kw: E.dma_start(d, s, **kw),
                  reads, [fake])

    def dram_load(self, queue, dst_tv, src_ap, src_name, region=None):
        d = dst_tv.ap
        fake = region if region is not None else TV(_DramReg(src_name))
        self._dma(queue, lambda E, d=d, s=src_ap: E.dma_start(d, s), [fake], [dst_tv])

    def finish(self):
        for qn in DMA_QUEUES:
            for sn in self.dma_slots[qn]:
                self._edge(qn, sn, self.cnt[sn])

    def build(self):
        with self.nc.Block() as blk:
            @blk.tensor
            def _(E):
                for f in self.q["pe"]:
                    f(E)

            @blk.scalar
            def _(E):
                for f in self.q["act"]:
                    f(E)

            @blk.vector
            def _(E):
                for f in self.q["dve"]:
                    f(E)

            @blk.gpsimd
            def _(E):
                for f in self.q["pool"]:
                    f(E)

            @blk.sync
            def _(E):
                for f in self.q["sp"]:
                    f(E)


EMB = 1024
NH = 16
KVH = 4
HD = 64
QL = 64
KVL = 32
FFI = 4096
NEXP = 8
EPS = 1e-5
ROPE_BASE = 10000.0
HA = [0, 1, 2, 3, 8, 9, 10, 11]
HB = [4, 5, 6, 7, 12, 13, 14, 15]
CAP = 768
RS = 1024
I32 = mybir.dt.int32
DBG_STAGE = 3
DBG_R = 9


def host_prep(inputs, S=2048):
    f32 = np.float32
    d = {}
    Wqa = np.asarray(inputs["L0_attn_Wqa"], f32)
    Wqb = np.asarray(inputs["L0_attn_Wqb"], f32)
    Wkva = np.asarray(inputs["L0_attn_Wkva"], f32)
    Wkvb = np.asarray(inputs["L0_attn_Wkvb"], f32)
    Wout = np.asarray(inputs["L0_attn_Wout"], f32)
    bout = np.asarray(inputs["L0_attn_bout"], f32)
    W1s = np.asarray(inputs["L0_ffn_w1_shared"], f32)
    W2s = np.asarray(inputs["L0_ffn_w2_shared"], f32)
    W1r = np.asarray(inputs["L0_ffn_w1_routed"], f32)
    W2r = np.asarray(inputs["L0_ffn_w2_routed"], f32)
    Wr = np.asarray(inputs["L0_ffn_w_router"], f32)

    d["wqa_t"] = np.ascontiguousarray(Wqa.reshape(8, 128, QL).transpose(1, 0, 2).reshape(128, 8 * QL))
    d["wkva_t"] = np.ascontiguousarray(Wkva.reshape(8, 128, KVL).transpose(1, 0, 2).reshape(128, 8 * KVL))

    qe_idx = np.zeros(EMB, np.int64)
    qo_idx = np.zeros(EMB, np.int64)
    for j in range(8):
        for b, hl in ((0, HA), (1, HB)):
            h = hl[j]
            for r in range(64):
                qe_idx[j * 128 + b * 64 + r] = h * 64 + 2 * (r % 32)
                qo_idx[j * 128 + b * 64 + r] = h * 64 + 2 * (r % 32) + 1
    d["wqed"] = np.ascontiguousarray(Wqb[:, qe_idx])
    d["wqod"] = np.ascontiguousarray(Wqb[:, qo_idx])

    ke_idx = np.zeros(KVH * HD, np.int64)
    ko_idx = np.zeros(KVH * HD, np.int64)
    v_idx = np.zeros(KVH * HD, np.int64)
    for kh in range(KVH):
        for r in range(64):
            ke_idx[kh * 64 + r] = kh * 128 + 4 * (r % 32)
            ko_idx[kh * 64 + r] = kh * 128 + 4 * (r % 32) + 2
            v_idx[kh * 64 + r] = kh * 128 + 2 * r + 1
    d["wked"] = np.ascontiguousarray(Wkvb[:, ke_idx])
    d["wkod"] = np.ascontiguousarray(Wkvb[:, ko_idx])
    d["wv"] = np.ascontiguousarray(Wkvb[:, v_idx])

    rowperm = np.zeros(EMB, np.int64)
    for j in range(8):
        for b, hl in ((0, HA), (1, HB)):
            h = hl[j]
            for dd in range(64):
                rowperm[j * 128 + b * 64 + dd] = h * 64 + dd
    Wout_p = Wout[rowperm, :]
    d["wout_t"] = np.ascontiguousarray(
        Wout_p.reshape(8, 128, 8, 128).transpose(1, 2, 0, 3).reshape(128, 8192))
    d["bout_t"] = np.ascontiguousarray(bout.reshape(8, 128).T)

    inv = ROPE_BASE ** (-(np.arange(32, dtype=np.float64) / 32.0))
    ang = np.outer(inv, np.arange(S, dtype=np.float64))
    cos = np.cos(ang).astype(f32)
    sin = np.sin(ang).astype(f32)
    t1 = np.concatenate([cos, sin, cos, sin], 0)
    t2 = np.concatenate([-sin, cos, -sin, cos], 0)
    d["t1"] = np.ascontiguousarray(t1)
    d["t2"] = np.ascontiguousarray(t2)
    tri = np.zeros((128, 128), f32)
    iu = np.triu_indices(128, 1)
    tri[iu] = -1e30
    d["trineg"] = tri
    d["identf"] = np.eye(128, dtype=f32)

    def tile_w1(w):
        return np.ascontiguousarray(w.reshape(8, 128, 32, 128).transpose(1, 2, 0, 3).reshape(128, 32768))

    def tile_w2(w):
        return np.ascontiguousarray(w.reshape(32, 128, 2, 512).transpose(1, 2, 0, 3).reshape(128, 32768))

    def blockify(w):
        return np.ascontiguousarray(w.reshape(128, 16, 2048).transpose(1, 0, 2).reshape(16 * 128, 2048))

    d["w1s_t"] = tile_w1(W1s)
    d["w2s_t"] = tile_w2(W2s)
    d["w1r_t"] = np.ascontiguousarray(np.stack([blockify(tile_w1(W1r[e])) for e in range(NEXP)], 0).reshape(NEXP * 16 * 128, 2048))
    d["w2r_t"] = np.ascontiguousarray(np.stack([blockify(tile_w2(W2r[e])) for e in range(NEXP)], 0).reshape(NEXP * 16 * 128, 2048))
    d["wr_t"] = np.ascontiguousarray(Wr.reshape(8, 128, NEXP).transpose(1, 0, 2).reshape(128, 8 * NEXP))

    d["triu_t"] = np.triu(np.ones((128, 128), f32))
    d["srange"] = np.ascontiguousarray(
        np.broadcast_to(np.arange(CAP // 2, dtype=f32)[None, :], (128, CAP // 2)))
    d["ebase"] = np.ascontiguousarray(
        np.broadcast_to(np.arange(NEXP, dtype=f32)[None, :] * CAP, (128, NEXP)))

    x = np.asarray(inputs["L0_attn_x"], f32)
    xts = [np.ascontiguousarray(x[b].T) for b in range(x.shape[0])]
    return d, xts


def build_core(S=2048):
    nt = S // 128
    ns = S // 512
    nc = bass.Bass("TRN2", target_bir_lowering=False)

    xT = nc.dram_tensor("xT", [EMB, S], F32, kind="ExternalInput").ap()
    wqa_t = nc.dram_tensor("wqa_t", [128, 8 * QL], F32, kind="ExternalInput").ap()
    wqed = nc.dram_tensor("wqed", [QL, EMB], F32, kind="ExternalInput").ap()
    wqod = nc.dram_tensor("wqod", [QL, EMB], F32, kind="ExternalInput").ap()
    wkva_t = nc.dram_tensor("wkva_t", [128, 8 * KVL], F32, kind="ExternalInput").ap()
    wked = nc.dram_tensor("wked", [KVL, 256], F32, kind="ExternalInput").ap()
    wkod = nc.dram_tensor("wkod", [KVL, 256], F32, kind="ExternalInput").ap()
    wv = nc.dram_tensor("wv", [KVL, 256], F32, kind="ExternalInput").ap()
    wout_t = nc.dram_tensor("wout_t", [128, 8192], F32, kind="ExternalInput").ap()
    bout_t = nc.dram_tensor("bout_t", [128, 8], F32, kind="ExternalInput").ap()
    t1d = nc.dram_tensor("t1", [128, S], F32, kind="ExternalInput").ap()
    t2d = nc.dram_tensor("t2", [128, S], F32, kind="ExternalInput").ap()
    trineg_d = nc.dram_tensor("trineg", [128, 128], F32, kind="ExternalInput").ap()
    identf_d = nc.dram_tensor("identf", [128, 128], F32, kind="ExternalInput").ap()
    w1s_t = nc.dram_tensor("w1s_t", [128, 32768], F32, kind="ExternalInput").ap()
    w2s_t = nc.dram_tensor("w2s_t", [128, 32768], F32, kind="ExternalInput").ap()
    w1r_t = nc.dram_tensor("w1r_t", [NEXP * 16 * 128, 2048], F32, kind="ExternalInput").ap()
    w2r_t = nc.dram_tensor("w2r_t", [NEXP * 16 * 128, 2048], F32, kind="ExternalInput").ap()
    wr_t = nc.dram_tensor("wr_t", [128, 8 * NEXP], F32, kind="ExternalInput").ap()
    out_d = nc.dram_tensor("out", [S, EMB], F32, kind="ExternalOutput").ap()
    triu_d = nc.dram_tensor("triu_t", [128, 128], F32, kind="ExternalInput").ap()
    srange_d = nc.dram_tensor("srange", [128, CAP // 2], F32, kind="ExternalInput").ap()
    ebase_d = nc.dram_tensor("ebase", [128, NEXP], F32, kind="ExternalInput").ap()
    xg_d = nc.dram_tensor("xg", [NEXP * CAP + 128, RS], BF16).ap()
    acc_d = nc.dram_tensor("acc", [S, EMB], F32).ap()
    w1s_b = nc.dram_tensor("w1s_b", [2048, 2048], BF16).ap()
    w2s_b = nc.dram_tensor("w2s_b", [2048, 2048], BF16).ap()

    with contextlib.ExitStack() as ctx:
        E = ctx.enter_context
        fw = FW(nc, ctx)

        def sb(name, shape, dt):
            return TV(E(nc.sbuf_tensor(name, shape, dt)))

        def psum(name, shape, dt):
            t = TV(E(nc.psum_tensor(name, shape, dt)))
            fw.reg_psum(t.t, dt)
            return t

        xf = sb("xf", [128, 8 * S], F32)
        xb = sb("xb", [128, 8 * S], BF16)
        wqa_b = sb("wqa_b", [128, 8 * QL], BF16)
        wqed_b = sb("wqed_b", [QL, EMB], BF16)
        wqod_b = sb("wqod_b", [QL, EMB], BF16)
        wkva_b = sb("wkva_b", [128, 8 * KVL], BF16)
        wked_b = sb("wked_b", [KVL, 256], BF16)
        wkod_b = sb("wkod_b", [KVL, 256], BF16)
        wv_b = sb("wv_b", [KVL, 256], BF16)
        wr_b = sb("wr_b", [128, 8 * NEXP], BF16)
        idf = sb("idf", [128, 128], F32)
        idb = sb("idb", [128, 128], BF16)
        trineg = sb("trineg_s", [128, 128], F32)
        bout_s = sb("bout_s", [128, 8], F32)
        onescol = sb("onescol", [128, 1], BF16)
        onesrow_f = sb("onesrow_f", [1, 128], F32)
        onesrow_b = sb("onesrow_b", [1, 128], BF16)
        negm = sb("negm", [128, 1], F32)
        rs1 = sb("rs1", [128, 1], F32)
        rs2 = sb("rs2", [128, 1], F32)
        rsum = sb("rsum", [128, 1], F32)
        rc = sb("rc", [128, 1], F32)
        m1_8 = sb("m1_8", [128, 1], F32)
        negm1 = sb("negm1", [128, 1], F32)
        m2_8 = sb("m2_8", [128, 1], F32)
        e2b = sb("e2b", [128, 1], F32)
        rc2 = sb("rc2", [128, 1], F32)
        eq8 = sb("eq8", [128, NEXP], F32)
        msk8 = sb("msk8", [128, NEXP], F32)
        sel8 = sb("sel8", [128, NEXP], F32)
        en8 = sb("en8", [128, NEXP], F32)
        ssum = sb("ssum", [128, 1], F32)
        ssq = sb("ssq", [128, 1], F32)
        mmt = sb("mmt", [128, 1], F32)
        rstd2 = sb("rstd2", [128, 1], F32)
        mneg2 = sb("mneg2", [128, 1], F32)
        eps_c = sb("eps_c", [128, 1], F32)
        triu_b = sb("triu_b", [128, 128], BF16)
        srange_s = sb("srange_s", [128, CAP // 2], F32)
        sshift = sb("sshift", [128, 1], F32)
        ebase_s = sb("ebase_s", [128, NEXP], F32)
        sel_b = sb("sel_b", [128, (S // 128) * NEXP], BF16)
        eq_b = sb("eq_b", [128, (S // 128) * NEXP], BF16)
        slotf_all = sb("slotf_all", [128, (S // 128) * NEXP], F32)
        sif = sb("sif", [128, 2 * (S // 128)], F32)
        sidx = sb("sidx", [128, 2 * (S // 128)], I32)


        AB = sb("AB", [128, 39040], BF16)
        AFa = sb("AFa", [128, 5760], F32)
        qx = AB.v(f=slice(0, 8 * S))
        kbf = AB.v(f=slice(8 * S, 10 * S))
        vtm = AB.v(f=slice(10 * S, 10 * S + 4 * (S // 128) * 64))
        xab = AB.v(p=slice(0, 64), f=slice(12 * S, 13 * S))
        xkb = AB.v(p=slice(0, 32), f=slice(13 * S, 14 * S))
        pbf = AB.v(f=slice(14 * S, 15 * S))
        pTs = AB.v(f=slice(15 * S, 16 * S))
        t1b = AB.v(f=slice(16 * S, 17 * S))
        t2b = AB.v(f=slice(17 * S, 18 * S))
        diag_rc = AB.v(f=slice(18 * S, 18 * S + 128))
        sbf = AB.v(f=slice(18 * S + 128, 18 * S + 640))
        sqb = AB.v(f=slice(18 * S + 640, 18 * S + 1152))
        woutsl = [AB.v(f=slice(18 * S + 1152, 18 * S + 1152 + 1024)),
                  AB.v(f=slice(15 * S, 15 * S + 1024))]
        y1e = AB.v(f=slice(0, 24576))
        x1e = AB.v(f=slice(24576, 24576 + 8 * CAP))
        y2sb = AB.v(f=slice(24576, 24576 + 8 * CAP))
        wst = [AB.v(f=slice(30720 + i * 2048, 30720 + (i + 1) * 2048)) for i in range(2)]
        xgt = [AB.v(f=slice(34816 + i * 1024, 34816 + (i + 1) * 1024)) for i in range(2)]
        ptw = AB.v(f=slice(36864, 36864 + CAP))
        rwsb = AB.v(f=slice(36864 + CAP, 36864 + 2 * CAP))

        ropes1 = AFa.v(f=slice(0, 512))
        ropes2 = AFa.v(f=slice(512, 1024))
        sdiag = AFa.v(f=slice(1024, 1152))
        st_sum = AFa.v(p=slice(0, 1), f=slice(1152, 1152 + S))
        st_sq = AFa.v(p=slice(0, 1), f=slice(1152 + S, 1152 + 2 * S))
        st_tmp = AFa.v(p=slice(0, 1), f=slice(1152 + 2 * S, 1152 + 2 * S + 512))
        scrf = AFa.v(f=slice(0, 512))
        cwb = AFa.v(f=slice(512, 1024))
        lg_sb = AFa.v(p=slice(0, NEXP), f=slice(1152, 1152 + S))
        lgtm = AFa.v(f=slice(1152 + S, 1152 + S + 8 * nt))
        cw_tm = AFa.v(f=slice(1152 + S + 8 * nt, 1152 + S + 16 * nt))
        ffn_tmp = AFa.v(f=slice(1152 + S + 16 * nt, 1664 + S + 16 * nt))
        totals_sb = AFa.v(p=slice(0, 1), f=slice(3968, 3968 + nt * 8))
        offs_sb = AFa.v(p=slice(0, 1), f=slice(4096, 4096 + nt * 8))
        csum_all = AFa.v(f=slice(4224, 4224 + nt * 8))
        slotw = AFa.v(f=slice(4352, 4360))
        slotm = AFa.v(f=slice(4360, 4368))
        y2h = [AFa.v(f=slice(4368 + i * 512, 4368 + (i + 1) * 512)) for i in range(2)]
        s2tt = [AFa.v(f=slice(0, 1024)), AFa.v(f=slice(1024, 2048))]
        sqtt = AFa.v(f=slice(3072, 4096))
        outtm = [AFa.v(f=slice(2048, 3072)), AFa.v(f=slice(4096, 5120))]
        accl0 = AFa.v(f=slice(3072, 3584))
        accl1 = AFa.v(f=slice(3584, 4096))

        ps_big = psum("ps_big", [128, 2048], F32)
        ps_big2 = psum("ps_big2", [128, 2048], F32)
        pb_a = TV(ps_big2.t, f=slice(0, 512))
        pb_b = TV(ps_big2.t, f=slice(512, 1024))
        po = TV(ps_big2.t, f=slice(1024, 1536))
        pt = TV(ps_big2.t, f=slice(1536, 2048))

        for c in range(8):
            fw.load("sp", xf.v(f=slice(c * S, (c + 1) * S)), xT[c * 128:(c + 1) * 128, :])
        fw.load("sp", idf, identf_d)
        fw.load("sp", trineg, trineg_d)
        fw.load("sp", bout_s, bout_t)
        fw.load("pool", idb, identf_d)
        fw.load("pool", t1b, t1d)
        fw.load("pool", t2b, t2d)
        fw.load("pool", wqa_b, wqa_t)
        fw.load("pool", wqed_b, wqed)
        fw.load("pool", wqod_b, wqod)
        fw.load("pool", wkva_b, wkva_t)
        fw.load("pool", wked_b, wked)
        fw.load("pool", wkod_b, wkod)
        fw.load("pool", wv_b, wv)
        fw.load("pool", wr_b, wr_t)
        fw.memset(onescol, 1.0)
        fw.memset(eps_c, EPS)
        fw.load("pool", triu_b, triu_d)
        fw.load("sp", srange_s, srange_d)
        fw.load("sp", ebase_s, ebase_d)
        fw.memset(xgt[0].v(f=slice(0, RS)), 0.0)
        for r0 in range(0, NEXP * CAP + 128, 128):
            fw.dram_store("act", xg_d[r0:r0 + 128, :], "xg", xgt[0].v(f=slice(0, RS)))
        fw.memset(onesrow_f, 1.0)
        fw.memset(onesrow_b, 1.0)
        fw.ts(wqa_b, wqa_b, 0.125, ALU.mult)
        for c in range(8):
            fw.copy(xb.v(f=slice(c * S, (c + 1) * S)), xf.v(f=slice(c * S, (c + 1) * S)))

        bnc = AFa.t.bitcast(BF16)
        WSH = [2048, 2048]
        cb_i = [0]

        def cast_block(src_ap, dst_ap, name, c0):
            blk = c0 // 2048
            i = cb_i[0] % 4
            cb_i[0] += 1
            trk = AFa.v(f=slice(1152 + i * 1024, 1152 + (i + 1) * 1024))
            bap = bnc[:, 2304 + i * 2048:2304 + (i + 1) * 2048]
            fw._dma("pool",
                    lambda E, d=bap, s=src_ap[0:128, c0:c0 + 2048]: E.dma_start(d, s),
                    [], [trk])
            reg = dreg(name, WSH, slice(blk * 128, (blk + 1) * 128), slice(0, 2048))
            fw._dma("sp",
                    lambda E, d=dst_ap[blk * 128:(blk + 1) * 128, :], s=bap: E.dma_start(d, s),
                    [trk], [reg])

        for c0 in range(0, 32768, 2048):
            cast_block(w1s_t, w1s_b, "w1sb", c0)
        for c0 in range(0, 32768, 2048):
            cast_block(w2s_t, w2s_b, "w2sb", c0)

        for t in range(ns):
            sl = slice(t * 512, (t + 1) * 512)
            pscur = TV(ps_big.t, p=slice(0, 64), f=sl)
            for k in range(8):
                fw.mm(pscur, TV(wqa_b.t, f=slice(k * 64, (k + 1) * 64)),
                      xb.v(f=slice(k * S + t * 512, k * S + (t + 1) * 512)),
                      start=(k == 0), stop=(k == 7))
            fw.copy(xab.v(f=sl), pscur)
        for t in range(ns):
            sl = slice(t * 512, (t + 1) * 512)
            pscur = TV(ps_big.t, p=slice(0, 32), f=sl)
            for k in range(8):
                fw.mm(pscur, TV(wkva_b.t, f=slice(k * 32, (k + 1) * 32)),
                      xb.v(f=slice(k * S + t * 512, k * S + (t + 1) * 512)),
                      start=(k == 0), stop=(k == 7))
            fw.copy(xkb.v(f=sl), pscur)

        for j in range(8):
            for t in range(ns):
                sl = slice(t * 512, (t + 1) * 512)
                psE = TV(ps_big.t, f=slice((t % 2) * 512, (t % 2) * 512 + 512))
                psO = TV(ps_big.t, f=slice(1024 + (t % 2) * 512, 1024 + (t % 2) * 512 + 512))
                fw.mm(psE, TV(wqed_b.t, f=slice(j * 128, (j + 1) * 128)), xab.v(f=sl), True, True)
                fw.mm(psO, TV(wqod_b.t, f=slice(j * 128, (j + 1) * 128)), xab.v(f=sl), True, True)
                fw.tt(ALU.mult, ropes1, psE, t1b.v(f=sl))
                fw.tt(ALU.mult, ropes2, psO, t2b.v(f=sl))
                fw.tt(ALU.add, qx.v(f=slice(j * S + t * 512, j * S + (t + 1) * 512)), ropes1, ropes2)
        for kc in range(2):
            for t in range(ns):
                sl = slice(t * 512, (t + 1) * 512)
                psE = TV(ps_big.t, f=slice((t % 2) * 512, (t % 2) * 512 + 512))
                psO = TV(ps_big.t, f=slice(1024 + (t % 2) * 512, 1024 + (t % 2) * 512 + 512))
                fw.mm(psE, TV(wked_b.t, f=slice(kc * 128, (kc + 1) * 128)), xkb.v(f=sl), True, True)
                fw.mm(psO, TV(wkod_b.t, f=slice(kc * 128, (kc + 1) * 128)), xkb.v(f=sl), True, True)
                fw.tt(ALU.mult, ropes1, psE, t1b.v(f=sl))
                fw.tt(ALU.mult, ropes2, psO, t2b.v(f=sl))
                fw.tt(ALU.add, kbf.v(f=slice(kc * S + t * 512, kc * S + (t + 1) * 512)), ropes1, ropes2)
        for kc in range(2):
            for t in range(ns):
                sl = slice(t * 512, (t + 1) * 512)
                pscur = TV(ps_big.t, f=slice((t % 4) * 512, (t % 4) * 512 + 512))
                fw.mm(pscur, TV(wv_b.t, f=slice(kc * 128, (kc + 1) * 128)), xkb.v(f=sl), True, True)
                fw.copy(pbf.v(f=sl), pscur)
            for b in range(2):
                kh = 2 * kc + b
                for tt_ in range(nt):
                    fw.mm(TV(pt.t, f=slice((tt_ % 8) * 64, (tt_ % 8) * 64 + 64)),
                          pbf.v(p=slice(b * 64, b * 64 + 64), f=slice(tt_ * 128, (tt_ + 1) * 128)),
                          TV(idb.t, p=slice(b * 64, b * 64 + 64), f=slice(b * 64, b * 64 + 64)),
                          True, True)
                    if tt_ % 8 == 7 or tt_ == nt - 1:
                        lo = (tt_ // 8) * 8
                        fw.copy(vtm.v(f=slice(kh * (nt * 64) + lo * 64, kh * (nt * 64) + (tt_ + 1) * 64)),
                                TV(pt.t, f=slice(0, (tt_ + 1 - lo) * 64)))

        negm2 = sb("negm2", [128, 1], F32)
        rs1b = sb("rs1b", [128, 1], F32)
        rs2b = sb("rs2b", [128, 1], F32)
        rsumb = sb("rsumb", [128, 1], F32)
        rcb = sb("rcb", [128, 1], F32)
        pbf2 = t1b
        pTs2 = t2b
        sdiag2 = AFa.v(f=slice(0, 128))
        drc2 = AB.v(f=slice(18 * S + 128, 18 * S + 256))
        groups = [ps_big, ps_big2]
        bufsets = [
            dict(pbf=pbf, pTs=pTs, sdiag=sdiag, negm=negm, rs1=rs1, rs2=rs2,
                 rsum=rsum, rc=rc, drc=diag_rc),
            dict(pbf=pbf2, pTs=pTs2, sdiag=sdiag2, negm=negm2, rs1=rs1b,
                 rs2=rs2b, rsum=rsumb, rc=rcb, drc=drc2),
        ]

        for j in range(8):
            for b_, hl in ((0, HA), (1, HB)):
                h = hl[j]
                rb = b_ * 64
                kh = h // 4
                kc = kh // 2

                def sc_stage(qt):
                    G = groups[qt % 2]
                    kspan = (qt + 1) * 128
                    qsl = qx.v(p=slice(rb, rb + 64), f=slice(j * S + qt * 128, j * S + (qt + 1) * 128))
                    for si in range((kspan + 511) // 512):
                        sl0 = si * 512
                        N = min(512, kspan - sl0)
                        fw.mm(G.v(f=slice(sl0, sl0 + N)), qsl,
                              kbf.v(p=slice(rb, rb + 64), f=slice(kc * S + sl0, kc * S + sl0 + N)),
                              True, True)

                def red_stage(qt):
                    G, B = groups[qt % 2], bufsets[qt % 2]
                    kspan = (qt + 1) * 128
                    nfull = qt * 128
                    fw.red(ALU.max, B["negm"], G.v(f=slice(0, kspan)), negate=True)
                    fw.tt(ALU.add, B["sdiag"], G.v(f=slice(nfull, kspan)), trineg)

                def exp_stage(qt):
                    G, B = groups[qt % 2], bufsets[qt % 2]
                    kspan = (qt + 1) * 128
                    nfull = qt * 128
                    if nfull > 0:
                        fw.act(B["pbf"].v(f=slice(0, nfull)), G.v(f=slice(0, nfull)),
                               AF.Exp, bias=B["negm"], accum=B["rs1"])
                    fw.act(B["pbf"].v(f=slice(nfull, kspan)), B["sdiag"], AF.Exp,
                           bias=B["negm"], accum=B["rs2"])

                def post_stage(qt):
                    B = bufsets[qt % 2]
                    if qt > 0:
                        fw.tt(ALU.add, B["rsum"], B["rs1"], B["rs2"])
                    else:
                        fw.copy(B["rsum"], B["rs2"])
                    fw.recip(B["rc"], B["rsum"])
                    fw.ts(B["drc"], idb, B["rc"], ALU.mult)

                def trpv_stage(qt):
                    G, B = groups[qt % 2], bufsets[qt % 2]
                    if qt <= 10:
                        tbs = [groups[(qt + 1) % 2].v(f=slice(1536, 2048)),
                               G.v(f=slice(1536, 2048))]
                        pocur = groups[(qt + 1) % 2].v(p=slice(0, 64), f=slice(1536, 1664))
                    else:
                        tbs = [G.v(f=slice(0, 512)), G.v(f=slice(512, 1024))]
                        pocur = G.v(p=slice(0, 64), f=slice(1024, 1152))
                    for g in range(0, qt + 1, 4):
                        ntile = min(4, qt + 1 - g)
                        tb = tbs[(g // 4) % 2]
                        for kt in range(g, g + ntile):
                            fw.mm(tb.v(f=slice((kt - g) * 128, (kt - g + 1) * 128)),
                                  B["pbf"].v(f=slice(kt * 128, (kt + 1) * 128)), B["drc"],
                                  kt == g, kt == g + ntile - 1)
                        fw.copy(B["pTs"].v(f=slice(g * 128, (g + ntile) * 128)),
                                tb.v(f=slice(0, ntile * 128)), eng="act")
                    for kt in range(qt + 1):
                        fw.mm(pocur, vtm.v(f=slice(kh * (nt * 64) + kt * 64, kh * (nt * 64) + (kt + 1) * 64)),
                              B["pTs"].v(f=slice(kt * 128, (kt + 1) * 128)),
                              start=(kt == 0), stop=(kt == qt))
                    fw.copy(qx.v(p=slice(rb, rb + 64), f=slice(j * S + qt * 128, j * S + (qt + 1) * 128)),
                            pocur, eng="dve")

                for qt in range(nt + 1):
                    if qt < nt:
                        sc_stage(qt)
                    if qt >= 1:
                        post_stage(qt - 1)
                    if qt < nt:
                        red_stage(qt)
                    if qt >= 1:
                        trpv_stage(qt - 1)
                    if qt < nt:
                        exp_stage(qt)

        for c in range(8):
            wsl = woutsl[c % 2]
            fw.load("pool", wsl, wout_t[:, c * 1024:(c + 1) * 1024])
            for t in range(ns):
                sl = slice(t * 512, (t + 1) * 512)
                pscur = TV(ps_big.t, f=sl)
                for k in range(8):
                    fw.mm(pscur, wsl.v(f=slice(k * 128, (k + 1) * 128)),
                          qx.v(f=slice(k * S + t * 512, k * S + (t + 1) * 512)),
                          start=(k == 0), stop=(k == 7))
                xfs = xf.v(f=slice(c * S + t * 512, c * S + (t + 1) * 512))
                fw.stt(xfs, pscur, TV(bout_s.t, f=slice(c, c + 1)), xfs, ALU.add, ALU.add)
        for t in range(ns):
            ss_row = st_sum.v(f=slice(t * 512, (t + 1) * 512))
            sq_row = st_sq.v(f=slice(t * 512, (t + 1) * 512))
            for c in range(8):
                pss = TV(ps_big.t, p=slice(0, 1), f=slice((c % 2) * 512, (c % 2) * 512 + 512))
                psq = TV(ps_big.t, p=slice(0, 1), f=slice(1024 + (c % 2) * 512, 1024 + (c % 2) * 512 + 512))
                xfs = xf.v(f=slice(c * S + t * 512, c * S + (t + 1) * 512))
                fw.copy(sbf, xfs, eng="act")
                fw.act(sqb, xfs, AF.Square)
                fw.mm(pss, onescol, sbf, True, True)
                fw.mm(psq, onescol, sqb, True, True)
                if c == 0:
                    fw.copy(ss_row, pss)
                    fw.copy(sq_row, psq)
                else:
                    fw.tt(ALU.add, ss_row, ss_row, pss)
                    fw.tt(ALU.add, sq_row, sq_row, psq)
        fw.ts(st_sum, st_sum, 1.0 / EMB, ALU.mult)
        fw.ts(st_sq, st_sq, 1.0 / EMB, ALU.mult)
        for hf in range(0, S, 512):
            w_ = min(512, S - hf)
            su = st_sum.v(f=slice(hf, hf + w_))
            sq = st_sq.v(f=slice(hf, hf + w_))
            tp = st_tmp.v(f=slice(0, w_))
            fw.tt(ALU.mult, tp, su, su)
            fw.tt(ALU.subtract, sq, sq, tp)
            fw.act(tp, sq, AF.Sqrt, bias=eps_c.v(p=slice(0, 1)))
            fw.recip(sq, tp)
            fw.stt(su, su, -1.0, sq, ALU.mult, ALU.mult)
        for t in range(ns):
            fw.mm(pb_a, onesrow_f,
                  TV(st_sq.t, p=slice(0, 1), f=slice(st_sq.f0 + t * 512, st_sq.f0 + (t + 1) * 512)), True, True)
            fw.mm(pb_b, onesrow_f,
                  TV(st_sum.t, p=slice(0, 1), f=slice(st_sum.f0 + t * 512, st_sum.f0 + (t + 1) * 512)), True, True)
            for c in range(8):
                xfs = xf.v(f=slice(c * S + t * 512, c * S + (t + 1) * 512))
                fw.tt(ALU.mult, scrf, xfs, pb_a)
                fw.tt(ALU.add, xb.v(f=slice(c * S + t * 512, c * S + (t + 1) * 512)), scrf, pb_b)

        for t in range(ns):
            psl = TV(ps_big.t, p=slice(0, NEXP), f=slice(t * 512, (t + 1) * 512))
            for k in range(8):
                fw.mm(psl, TV(wr_b.t, f=slice(k * NEXP, (k + 1) * NEXP)),
                      xb.v(f=slice(k * S + t * 512, k * S + (t + 1) * 512)),
                      start=(k == 0), stop=(k == 7))
            fw.copy(TV(lg_sb.t, p=slice(0, NEXP), f=slice(lg_sb.f0 + t * 512, lg_sb.f0 + (t + 1) * 512)), psl)
        for tt_ in range(nt):
            fw.mm(pb_a.v(f=slice((tt_ % 16) * 8, (tt_ % 16) * 8 + 8)),
                  lg_sb.v(f=slice(tt_ * 128, (tt_ + 1) * 128)),
                  TV(idf.t, p=slice(0, NEXP), f=slice(0, NEXP)), True, True)
            if tt_ % 16 == 15 or tt_ == nt - 1:
                lo = (tt_ // 16) * 16
                fw.copy(lgtm.v(f=slice(lo * 8, (tt_ + 1) * 8)),
                        pb_a.v(f=slice(0, (tt_ + 1 - lo) * 8)))
        for tt_ in range(nt):
            lgt = TV(lgtm.t, f=slice(lgtm.f0 + tt_ * 8, lgtm.f0 + (tt_ + 1) * 8))
            fw.red(ALU.max, m1_8, lgt)
            fw.ts(negm1, m1_8, -1.0, ALU.mult)
            fw.ts(eq8, lgt, m1_8, ALU.is_equal)
            fw.stt(msk8, eq8, -1e30, lgt, ALU.mult, ALU.add)
            fw.red(ALU.max, m2_8, msk8)
            fw.ts(sel8, lgt, m2_8, ALU.is_ge)
            fw.act(en8, lgt, AF.Exp, bias=negm1)
            fw.act(e2b, m2_8, AF.Exp, bias=negm1)
            fw.ts(e2b, e2b, 1.0, ALU.add)
            fw.recip(rc2, e2b)
            fw.tt(ALU.mult, en8, en8, sel8)
            fw.ts(TV(cw_tm.t, f=slice(cw_tm.f0 + tt_ * 8, cw_tm.f0 + (tt_ + 1) * 8)), en8, rc2, ALU.mult)
            fw.copy(sel_b.v(f=slice(tt_ * 8, (tt_ + 1) * 8)), sel8)
            fw.copy(eq_b.v(f=slice(tt_ * 8, (tt_ + 1) * 8)), eq8)

        for tt_ in range(nt):
            selt = sel_b.v(f=slice(tt_ * 8, (tt_ + 1) * 8))
            fw.mm(pb_a.v(f=slice(0, 8)), triu_b, selt, True, True)
            fw.mm(pb_b.v(p=slice(0, 1), f=slice(0, 8)), onescol, selt, True, True)
            fw.copy(csum_all.v(f=slice(tt_ * 8, (tt_ + 1) * 8)), pb_a.v(f=slice(0, 8)))
            fw.copy(totals_sb.v(f=slice(tt_ * 8, (tt_ + 1) * 8)), pb_b.v(p=slice(0, 1), f=slice(0, 8)))
        fw.memset(offs_sb.v(f=slice(0, 8)), 0.0)
        for tt_ in range(1, nt):
            fw.tt(ALU.add, offs_sb.v(f=slice(tt_ * 8, (tt_ + 1) * 8)),
                  offs_sb.v(f=slice((tt_ - 1) * 8, tt_ * 8)),
                  totals_sb.v(f=slice((tt_ - 1) * 8, tt_ * 8)))
        DUM = float(NEXP * CAP)
        LDUM = 4096.0
        for tt_ in range(nt):
            fw.mm(pb_a.v(f=slice(0, 8)), onesrow_f, offs_sb.v(f=slice(tt_ * 8, (tt_ + 1) * 8)), True, True)
            fw.tt(ALU.add, slotw, csum_all.v(f=slice(tt_ * 8, (tt_ + 1) * 8)), pb_a.v(f=slice(0, 8)))
            fw.ts(slotw, slotw, -1.0, ALU.add)
            fw.ts(slotm, slotw, float(CAP - 1) + 0.5, ALU.is_le)
            fw.tt(ALU.mult, slotm, slotm, sel_b.v(f=slice(tt_ * 8, (tt_ + 1) * 8)))
            lf = slotf_all.v(f=slice(tt_ * 8, (tt_ + 1) * 8))
            fw.ts(lf, slotw, -LDUM, ALU.add)
            fw.tt(ALU.mult, lf, lf, slotm)
            fw.ts(lf, lf, LDUM, ALU.add)
            fw.tt(ALU.add, slotw, slotw, ebase_s)
            fw.ts(slotw, slotw, -DUM, ALU.add)
            fw.tt(ALU.mult, slotw, slotw, slotm)
            fw.ts(slotw, slotw, DUM, ALU.add)
            fw.tt(ALU.mult, slotm, slotw, eq_b.v(f=slice(tt_ * 8, (tt_ + 1) * 8)))
            fw.red(ALU.add, TV(sif.t, f=slice(tt_, tt_ + 1)), slotm)
            fw.tt(ALU.subtract, slotm, sel_b.v(f=slice(tt_ * 8, (tt_ + 1) * 8)),
                  eq_b.v(f=slice(tt_ * 8, (tt_ + 1) * 8)))
            fw.tt(ALU.mult, slotm, slotm, slotw)
            fw.red(ALU.add, TV(sif.t, f=slice(nt + tt_, nt + tt_ + 1)), slotm)
        fw.copy(TV(sidx.t, f=slice(0, 2 * nt)), TV(sif.t, f=slice(0, 2 * nt)))

        def p5b(tt_):
            x1row = xgt[tt_ % 2]
            for c in range(8):
                bank = pb_a if c < 4 else pb_b
                fw.mm(bank.v(f=slice((c % 4) * 128, (c % 4) * 128 + 128)),
                      xb.v(f=slice(c * S + tt_ * 128, c * S + (tt_ + 1) * 128)), idb,
                      c % 4 == 0, c % 4 == 3)
                if c == 3:
                    fw.copy(x1row.v(f=slice(0, 512)), pb_a, eng="act")
                if c == 7:
                    fw.copy(x1row.v(f=slice(512, 1024)), pb_b, eng="act")
            if DBG_STAGE >= 1:
                fw.iscatter(xg_d[:, :], "xg", TV(sidx.t, f=slice(tt_, tt_ + 1)),
                            x1row.v(f=slice(0, RS)), NEXP * CAP, ALU.bypass)
                fw.iscatter(xg_d[:, :], "xg", TV(sidx.t, f=slice(nt + tt_, nt + tt_ + 1)),
                            x1row.v(f=slice(0, RS)), NEXP * CAP, ALU.bypass)

        wrot = [0]

        def wslot():
            s_ = wst[wrot[0] % 2]
            wrot[0] += 1
            return s_

        psum6 = [TV(ps_big.t, f=slice(0, 512)), TV(ps_big.t, f=slice(512, 1024)),
                 TV(ps_big.t, f=slice(1024, 1536)), TV(ps_big.t, f=slice(1536, 2048)),
                 pb_a, pb_b]
        ACCSH = [S, EMB]

        def shared_f1(t):
            for bi in range(16):
                wsl = wslot()
                fw.dram_load("pool", wsl, w1s_b[bi * 128:(bi + 1) * 128, :], "w1sb",
                             region=dreg("w1sb", WSH, slice(bi * 128, (bi + 1) * 128),
                                         slice(0, 2048)))
                for s_ in range(2):
                    m1 = bi * 2 + s_
                    pscur = TV(ps_big.t, f=slice((m1 % 4) * 512, (m1 % 4) * 512 + 512))
                    for k in range(8):
                        fw.mm(pscur, wsl.v(f=slice(s_ * 1024 + k * 128, s_ * 1024 + (k + 1) * 128)),
                              xb.v(f=slice(k * S + t * 512, k * S + (t + 1) * 512)),
                              start=(k == 0), stop=(k == 7))
                    fw.act(y1e.v(f=slice(m1 * 512, (m1 + 1) * 512)), pscur, AF.Silu)

        def shared_f2(t):
            for fh in range(2):
                for bi in range(8):
                    wsl = wslot()
                    blk_ = fh * 8 + bi
                    fw.dram_load("pool", wsl, w2s_b[blk_ * 128:(blk_ + 1) * 128, :], "w2sb",
                                 region=dreg("w2sb", WSH, slice(blk_ * 128, (blk_ + 1) * 128),
                                             slice(0, 2048)))
                    for kk in range(4):
                        k2 = bi * 4 + kk
                        for tile in range(4):
                            fw.mm(psum6[tile], y1e.v(f=slice(k2 * 512 + tile * 128, k2 * 512 + (tile + 1) * 128)),
                                  wsl.v(f=slice(kk * 512, (kk + 1) * 512)),
                                  start=(k2 == 0), stop=(kk == 3), skip_check=(0 < k2 < 31))
                for tile in range(4):
                    fw.copy(y2h[tile % 2], psum6[tile])
                    r0 = t * 512 + tile * 128
                    fw.dram_store("act", acc_d[r0:r0 + 128, fh * 512:(fh + 1) * 512],
                                  "acc", y2h[tile % 2],
                                  region=dreg("acc", ACCSH, slice(r0, r0 + 128),
                                              slice(fh * 512, (fh + 1) * 512)))

        npieces = min(4, 2 * ns)
        tt_groups = [list(range(g * nt // npieces, (g + 1) * nt // npieces))
                     for g in range(npieces)]
        pi = [0]

        def p5b_next():
            if pi[0] < npieces:
                for x in tt_groups[pi[0]]:
                    p5b(x)
                pi[0] += 1

        for t in range(ns):
            shared_f1(t)
            p5b_next()
            shared_f2(t)
            p5b_next()

        NTE = CAP // 128
        pre_w1 = [None, None]

        def wload(wsl_, src_slice):
            fw.load("pool", wsl_, src_slice)

        def prefetch_w1(e_):
            for b2 in range(2):
                wsl_ = wslot()
                r0_ = (e_ * 16 + b2) * 128
                wload(wsl_, w1r_t[r0_:r0_ + 128, :])
                pre_w1[b2] = wsl_

        if DBG_STAGE >= 0 and DBG_R >= 9:
            prefetch_w1(0)
        for e in range(NEXP if DBG_STAGE >= 0 else 0):
            for tile in range(NTE if DBG_R >= 1 else 0):
                xslot = xgt[tile % 2]
                fw.dram_load("sp", xslot.v(f=slice(0, RS)),
                             xg_d[e * CAP + tile * 128:e * CAP + (tile + 1) * 128, :], "xg")
                for c in range(8 if DBG_R >= 6 else 0):
                    bank = pb_a if c < 4 else pb_b
                    fw.mm(bank.v(f=slice((c % 4) * 128, (c % 4) * 128 + 128)),
                          xslot.v(f=slice(c * 128, (c + 1) * 128)), idb,
                          c % 4 == 0, c % 4 == 3)
                    if c == 3 and DBG_R >= 7:
                        for cc in range(4):
                            fw.copy(x1e.v(f=slice(cc * CAP + tile * 128, cc * CAP + (tile + 1) * 128)),
                                    pb_a.v(f=slice(cc * 128, (cc + 1) * 128)))
                    if c == 7 and DBG_R >= 7:
                        for cc in range(4):
                            fw.copy(x1e.v(f=slice((cc + 4) * CAP + tile * 128, (cc + 4) * CAP + (tile + 1) * 128)),
                                    pb_b.v(f=slice(cc * 128, (cc + 1) * 128)))
            for bi in range(16 if DBG_R >= 9 else 0):
                if bi < 2:
                    wsl = pre_w1[bi]
                else:
                    wsl = wslot()
                    r0_ = (e * 16 + bi) * 128
                    wload(wsl, w1r_t[r0_:r0_ + 128, :])
                for s_ in range(2):
                    m1 = bi * 2 + s_
                    ps0 = TV(ps_big.t, f=slice((m1 % 2) * 1024, (m1 % 2) * 1024 + 512))
                    ps1 = TV(ps_big.t, f=slice((m1 % 2) * 1024 + 512, (m1 % 2) * 1024 + 512 + 256))
                    for k in range(8):
                        lh = wsl.v(f=slice(s_ * 1024 + k * 128, s_ * 1024 + (k + 1) * 128))
                        fw.mm(ps0, lh, x1e.v(f=slice(k * CAP, k * CAP + 512)),
                              start=(k == 0), stop=(k == 7))
                        fw.mm(ps1, lh, x1e.v(f=slice(k * CAP + 512, k * CAP + 768)),
                              start=(k == 0), stop=(k == 7))
                    fw.act(y1e.v(f=slice(m1 * CAP, m1 * CAP + 512)), ps0, AF.Silu)
                    fw.act(y1e.v(f=slice(m1 * CAP + 512, m1 * CAP + 768)), ps1, AF.Silu)
            for fh in range(2 if DBG_R >= 9 else 0):
                for bi in range(8):
                    wsl = wslot()
                    r0_ = (e * 16 + fh * 8 + bi) * 128
                    wload(wsl, w2r_t[r0_:r0_ + 128, :])
                    for kk in range(4):
                        k2 = bi * 4 + kk
                        for tile in range(NTE):
                            fw.mm(psum6[tile], y1e.v(f=slice(k2 * CAP + tile * 128, k2 * CAP + (tile + 1) * 128)),
                                  wsl.v(f=slice(kk * 512, (kk + 1) * 512)),
                                  start=(k2 == 0), stop=(kk == 3), skip_check=(0 < k2 < 31))
                for tile in range(NTE):
                    fw.copy(y2sb.v(f=slice(tile * 1024 + fh * 512, tile * 1024 + (fh + 1) * 512)),
                            psum6[tile], eng="act")
            if e + 1 < NEXP and DBG_R >= 9:
                prefetch_w1(e + 1)
            ptws = [ptw, AB.v(f=slice(30 * CAP, 31 * CAP))]
            rwsbs = [rwsb, AB.v(f=slice(31 * CAP, 32 * CAP)),
                     AB.v(f=slice(29 * CAP, 30 * CAP))]
            trbanks = [(pb_a, pb_b), (TV(ps_big.t, f=slice(0, 512)), TV(ps_big.t, f=slice(512, 1024)))]
            prs2 = [(po, pt),
                    (TV(ps_big.t, f=slice(1024, 1536)), TV(ps_big.t, f=slice(1536, 2048)))]
            y2cs = [TV(y2h[0].t, f=slice(y2h[0].f0, y2h[0].f0 + 1024)),
                    AFa.v(f=slice(0, 1024))]

            def ret_build(tt_):
                p_ = ptws[tt_ % 2]
                slotc = TV(slotf_all.t, f=slice(tt_ * 8 + e, tt_ * 8 + e + 1))
                fw.ts(p_.v(f=slice(0, CAP // 2)), srange_s, slotc, ALU.is_equal)
                fw.ts(sshift, slotc, -float(CAP // 2), ALU.add)
                fw.ts(p_.v(f=slice(CAP // 2, CAP)), srange_s, sshift, ALU.is_equal)
                fw.ts(p_.v(f=slice(0, CAP)), p_.v(f=slice(0, CAP)),
                      TV(cw_tm.t, f=slice(cw_tm.f0 + tt_ * 8 + e, cw_tm.f0 + tt_ * 8 + e + 1)), ALU.mult)

            def ret_tr(tt_):
                p_ = ptws[tt_ % 2]
                r_ = rwsbs[tt_ % 3]
                bA, bB = trbanks[tt_ % 2]
                for st in range(NTE):
                    bank = bA if st < 4 else bB
                    fw.mm(bank.v(f=slice((st % 4) * 128, (st % 4) * 128 + 128)),
                          p_.v(f=slice(st * 128, (st + 1) * 128)), idb,
                          st % 4 == 0, st % 4 == 3 or st == NTE - 1)
                    if st == 3:
                        fw.copy(r_.v(f=slice(0, 512)), bA)
                    if st == 5:
                        fw.copy(r_.v(f=slice(512, 768)), bB.v(f=slice(0, 256)), eng="act")

            def ret_mm(tt_):
                r_ = rwsbs[tt_ % 3]
                y2c = y2cs[tt_ % 2]
                for fh in range(2):
                    pr = prs2[tt_ % 2][fh]
                    for st in range(NTE):
                        fw.mm(pr, r_.v(f=slice(st * 128, (st + 1) * 128)),
                              y2sb.v(f=slice(st * 1024 + fh * 512, st * 1024 + (fh + 1) * 512)),
                              start=(st == 0), stop=(st == NTE - 1))
                    fw.copy(TV(y2c.t, f=slice(y2c.f0 + fh * 512, y2c.f0 + (fh + 1) * 512)),
                            pr, eng=("dve" if fh == 0 else "act"))
                r0 = tt_ * 128
                fw.dram_store("pool", acc_d[r0:r0 + 128, 0:EMB], "acc", y2c,
                              region=dreg("acc", ACCSH, slice(r0, r0 + 128), slice(0, EMB)),
                              accum=ALU.add)

            for tt_ in range(nt + 2):
                if tt_ < nt:
                    ret_build(tt_)
                if 1 <= tt_ <= nt:
                    ret_tr(tt_ - 1)
                if tt_ >= 2:
                    ret_mm(tt_ - 2)

        ssum2 = sb("ssum2", [128, 1], F32)
        ssq2 = sb("ssq2", [128, 1], F32)
        mmt2 = negm2
        rstd2b = rs1b
        mneg2b = rs2b
        accp = [[xf.v(f=slice(0, 512)), xf.v(f=slice(512, 1024))],
                [xf.v(f=slice(1024, 1536)), xf.v(f=slice(1536, 2048))]]

        def p7_load(tt_):
            a0, a1 = accp[tt_ % 2]
            fw.dram_load("sp", a0, acc_d[tt_ * 128:(tt_ + 1) * 128, 0:512], "acc",
                         region=dreg("acc", ACCSH, slice(tt_ * 128, (tt_ + 1) * 128), slice(0, 512)))
            fw.dram_load("sp", a1, acc_d[tt_ * 128:(tt_ + 1) * 128, 512:1024], "acc",
                         region=dreg("acc", ACCSH, slice(tt_ * 128, (tt_ + 1) * 128), slice(512, 1024)))

        p7_load(0)
        for tt_ in range(nt):
            if tt_ + 1 < nt:
                p7_load(tt_ + 1)
            buf = s2tt[tt_ % 2]
            ob = outtm[tt_ % 2]
            a0, a1 = accp[tt_ % 2]
            ss_ = ssum if tt_ % 2 == 0 else ssum2
            sq_ = ssq if tt_ % 2 == 0 else ssq2
            mm_ = mmt if tt_ % 2 == 0 else mmt2
            rs_ = rstd2 if tt_ % 2 == 0 else rstd2b
            mn_ = mneg2 if tt_ % 2 == 0 else mneg2b
            for c in range(8):
                bank = pb_a if c < 4 else pb_b
                fw.mm(bank.v(f=slice((c % 4) * 128, (c % 4) * 128 + 128)),
                      xb.v(f=slice(c * S + tt_ * 128, c * S + (tt_ + 1) * 128)), idb,
                      c % 4 == 0, c % 4 == 3)
                if c == 3:
                    fw.tt(ALU.add, buf.v(f=slice(0, 512)), pb_a, a0)
                if c == 7:
                    fw.tt(ALU.add, buf.v(f=slice(512, 1024)), pb_b, a1)
            fw.act(sqtt, buf, AF.Square, accum=sq_)
            fw.red(ALU.add, ss_, buf)
            fw.ts(ss_, ss_, 1.0 / EMB, ALU.mult)
            fw.ts(sq_, sq_, 1.0 / EMB, ALU.mult)
            fw.tt(ALU.mult, mm_, ss_, ss_)
            fw.tt(ALU.subtract, sq_, sq_, mm_)
            fw.act(mm_, sq_, AF.Sqrt, bias=eps_c)
            fw.recip(rs_, mm_)
            fw.stt(mn_, ss_, -1.0, rs_, ALU.mult, ALU.mult)
            fw.act(ob, buf, AF.Identity, bias=mn_, scale=rs_)
            fw.store("act", out_d[tt_ * 128:(tt_ + 1) * 128, :], ob)

        fw.finish()
        fw.build()
    return nc


def kernel(**inputs):
    shared, xts = host_prep(inputs)
    nc = build_core(2048)
    in_maps = [dict(shared, xT=xts[b]) for b in range(8)]
    res = run_bass_kernel_spmd(nc, in_maps, core_ids=list(range(8)))
    return np.stack([np.asarray(r["out"], np.float32) for r in res.results], 0)


# version 30
# speedup vs baseline: 1.0241x; 1.0241x over previous
import contextlib
import numpy as np
import concourse.bass as bass
import concourse.mybir as mybir
from concourse.bass_utils import run_bass_kernel_spmd

F32 = mybir.dt.float32
BF16 = mybir.dt.bfloat16
AF = mybir.ActivationFunctionType
ALU = mybir.AluOpType
AX = mybir.AxisListType

ENGINES = ("pe", "act", "dve", "pool", "sp")
DMA_QUEUES = ("sp", "pool", "act")


class TV:
    __slots__ = ("t", "p0", "p1", "f0", "f1")

    def __init__(self, t, p=None, f=None):
        self.t = t
        shape = t.shape
        self.p0, self.p1 = (p.start or 0, p.stop if p.stop is not None else shape[0]) if p else (0, shape[0])
        self.f0, self.f1 = (f.start or 0, f.stop if f.stop is not None else shape[1]) if f else (0, shape[1])

    @property
    def ap(self):
        return self.t[self.p0:self.p1, self.f0:self.f1]

    @property
    def name(self):
        return self.t.name

    def v(self, p=None, f=None):
        pp = slice(self.p0 + (p.start or 0), self.p0 + p.stop) if p else slice(self.p0, self.p1)
        ff = slice(self.f0 + (f.start or 0), self.f0 + f.stop) if f else slice(self.f0, self.f1)
        return TV(self.t, pp, ff)


def _overlap(a, b):
    return not (a[1] <= b[0] or b[1] <= a[0] or a[3] <= b[2] or b[3] <= a[2])


def _covers(a, b):
    return a[0] <= b[0] and a[1] >= b[1] and a[2] <= b[2] and a[3] >= b[3]


def _reg(tv):
    return (tv.p0, tv.p1, tv.f0, tv.f1)


class _DramReg:
    def __init__(self, name, shape=(1, 1)):
        self.name = name
        self.shape = list(shape)

    def __getitem__(self, _):
        return None


def dreg(name, shape, rows=None, cols=None):
    return TV(_DramReg(name, shape), rows, cols)


class FW:
    def __init__(self, nc, stack, n_dma_slots=4):
        self.nc = nc
        self.stack = stack
        self.q = {e: [] for e in ENGINES}
        self.sems = {}
        self.cnt = {}
        self.waited = {e: {} for e in ENGINES}
        for e in ENGINES:
            self._mksem("s_" + e)
        self.dma_slots = {}
        self.dma_rr = {}
        self.slot_uses = {}
        for qn in DMA_QUEUES:
            names = []
            for i in range(n_dma_slots):
                sn = f"dq_{qn}_{i}"
                self._mksem(sn)
                names.append(sn)
                self.slot_uses[sn] = 0
            self.dma_slots[qn] = names
            self.dma_rr[qn] = 0
        self.writes = {}
        self.reads = {}
        self.pe_pend = []
        self.psum_bank_state = {}
        self.psum_width = {}
        self.n_instr = 0

    def reg_psum(self, t, dtype):
        self.psum_width[t.name] = 512 if dtype == F32 else 1024

    def _mksem(self, name):
        self.sems[name] = self.stack.enter_context(self.nc.semaphore(name))
        self.cnt[name] = 0

    def _edge(self, eng, sem_name, val):
        if val <= 0:
            return
        if self.waited[eng].get(sem_name, 0) >= val:
            return
        self.waited[eng][sem_name] = val
        s = self.sems[sem_name]
        self.q[eng].append(lambda E, s=s, v=val: E.wait_ge(s, v))

    def barrier(self):
        for e in ENGINES:
            for sn, v in self.cnt.items():
                self._edge(e, sn, v)

    def _pend_guard(self, eng, tv, writing):
        if eng == "pe" or not self.pe_pend:
            return
        reg = _reg(tv)
        for kind, nm, r in self.pe_pend:
            if nm == tv.name and (writing or kind == "w") and _overlap(reg, r):
                raise RuntimeError(
                    f"{eng} touches {tv.name}{reg} while an open PE accumulation "
                    f"group still {'reads' if kind == 'r' else 'writes'} {r} — close "
                    f"the group (stop=True) first")

    def _breg(self, tv):
        w = self.psum_width.get(tv.name)
        if w is None:
            return _reg(tv)
        return (tv.p0, tv.p1, (tv.f0 // w) * w, ((tv.f1 + w - 1) // w) * w)

    def _haz_read(self, eng, tv):
        self._pend_guard(eng, tv, False)
        reg = self._breg(tv)
        for r in self.writes.get(tv.name, ()):
            if _overlap(reg, r):
                self._edge(eng, r[4], r[5])

    def _haz_write(self, eng, tv):
        self._pend_guard(eng, tv, True)
        reg = self._breg(tv)
        for r in self.writes.get(tv.name, ()):
            if _overlap(reg, r):
                self._edge(eng, r[4], r[5])
        for r in self.reads.get(tv.name, ()):
            if _overlap(reg, r):
                self._edge(eng, r[4], r[5])

    def _note_read(self, name, reg, sem, val):
        lst = self.reads.setdefault(name, [])
        for r in lst:
            if r[4] == sem and r[0] == reg[0] and r[1] == reg[1] and r[2] == reg[2] and r[3] == reg[3]:
                r[5] = max(r[5], val)
                return
        lst.append([reg[0], reg[1], reg[2], reg[3], sem, val])

    def _note_write(self, name, reg, sem, val):
        wl = self.writes.setdefault(name, [])
        wl[:] = [r for r in wl if not _covers(reg, r)]
        wl.append([reg[0], reg[1], reg[2], reg[3], sem, val])
        rl = self.reads.get(name)
        if rl:
            rl[:] = [r for r in rl if not _covers(reg, r)]

    def _emit(self, eng, fn, reads, writes):
        for tv in reads:
            self._haz_read(eng, tv)
        for tv in writes:
            self._haz_write(eng, tv)
        sem = "s_" + eng
        self.cnt[sem] += 1
        val = self.cnt[sem]
        s = self.sems[sem]
        self.q[eng].append(lambda E, fn=fn, s=s: fn(E).then_inc(s, 1))
        for tv in reads:
            self._note_read(tv.name, self._breg(tv), sem, val)
        for tv in writes:
            self._note_write(tv.name, self._breg(tv), sem, val)
        self.n_instr += 1
        return val

    def _pe_flush(self, val):
        for kind, nm, reg in self.pe_pend:
            if kind == "r":
                self._note_read(nm, reg, "s_pe", val)
            else:
                self._note_write(nm, reg, "s_pe", val)
        self.pe_pend.clear()

    def _psum_base_guard(self, out_tv, base, is_group_start):
        w = self.psum_width.get(out_tv.name)
        assert w is not None, f"psum tensor {out_tv.name} not registered"
        for bank in range(out_tv.f0 // w, (out_tv.f1 + w - 1) // w):
            key = (out_tv.name, bank)
            st = self.psum_bank_state.get(key)
            if st is not None and st[0] != base:
                if st[1] is None:
                    raise RuntimeError("mixed-base write into bank of an open PE group")
                self._edge("pe", "s_pe", st[1])
            self.psum_bank_state[key] = (base, None)

    def _psum_close(self, out_tvs, val):
        for out_tv in out_tvs:
            w = self.psum_width[out_tv.name]
            for bank in range(out_tv.f0 // w, (out_tv.f1 + w - 1) // w):
                key = (out_tv.name, bank)
                base = self.psum_bank_state[key][0]
                self.psum_bank_state[key] = (base, val)

    def mm(self, out_tv, lhsT_tv, rhs_tv, start, stop, skip_check=False):
        assert lhsT_tv.p0 == rhs_tv.p0, "lhsT/rhs partition bases must match"
        assert lhsT_tv.p1 - lhsT_tv.p0 == rhs_tv.p1 - rhs_tv.p0
        self._psum_base_guard(out_tv, lhsT_tv.p0, start)
        self._haz_read("pe", lhsT_tv)
        self._haz_read("pe", rhs_tv)
        self._haz_write("pe", out_tv)
        self.pe_pend.append(("r", lhsT_tv.name, self._breg(lhsT_tv)))
        self.pe_pend.append(("r", rhs_tv.name, self._breg(rhs_tv)))
        self.pe_pend.append(("w", out_tv.name, self._breg(out_tv)))
        o, l, r = out_tv.ap, lhsT_tv.ap, rhs_tv.ap
        if stop:
            sem = "s_pe"
            self.cnt[sem] += 1
            val = self.cnt[sem]
            s = self.sems[sem]
            self.q["pe"].append(lambda E, o=o, l=l, r=r, st=start, s=s, sk=skip_check:
                                E.matmul(o, l, r, start=st, stop=True, skip_group_check=sk).then_inc(s, 1))
            self._pe_flush(val)
            self._psum_close([out_tv], val)
        else:
            self.q["pe"].append(lambda E, o=o, l=l, r=r, st=start, sk=skip_check:
                                E.matmul(o, l, r, start=st, stop=False, skip_group_check=sk))
        self.n_instr += 1

    def transpose(self, out_tv, in_tv, ident_tv):
        self._psum_base_guard(out_tv, in_tv.p0, True)
        self._haz_read("pe", in_tv)
        self._haz_read("pe", ident_tv)
        self._haz_write("pe", out_tv)
        self.pe_pend.append(("r", in_tv.name, self._breg(in_tv)))
        self.pe_pend.append(("r", ident_tv.name, self._breg(ident_tv)))
        self.pe_pend.append(("w", out_tv.name, self._breg(out_tv)))
        o, i, idn = out_tv.ap, in_tv.ap, ident_tv.ap
        sem = "s_pe"
        self.cnt[sem] += 1
        val = self.cnt[sem]
        s = self.sems[sem]
        self.q["pe"].append(lambda E, o=o, i=i, idn=idn, s=s:
                            E.transpose(o, i, idn).then_inc(s, 1))
        self._pe_flush(val)
        self._psum_close([out_tv], val)
        self.n_instr += 1

    def act(self, out_tv, in_tv, func, bias=0.0, scale=1.0, accum=None):
        reads = [in_tv]
        writes = [out_tv]
        b = bias.ap if isinstance(bias, TV) else bias
        sc = scale.ap if isinstance(scale, TV) else scale
        if isinstance(bias, TV):
            reads.append(bias)
        if isinstance(scale, TV):
            reads.append(scale)
        acc_ap = None
        if accum is not None:
            writes.append(accum)
            acc_ap = accum.ap
        o, i = out_tv.ap, in_tv.ap
        self._emit("act", lambda E, o=o, i=i, f=func, b=b, sc=sc, a=acc_ap:
                   E.activation(o, i, f, bias=b, scale=sc, accum_out=a), reads, writes)

    def tt(self, op, out_tv, a_tv, b_tv, eng="dve"):
        o, a, b = out_tv.ap, a_tv.ap, b_tv.ap
        self._emit(eng, lambda E, o=o, a=a, b=b, op=op: E.tensor_tensor(o, a, b, op),
                   [a_tv, b_tv], [out_tv])

    def ts(self, out_tv, in_tv, s1, op0, s2=None, op1=None, eng="dve"):
        reads = [in_tv]
        a1 = s1.ap if isinstance(s1, TV) else s1
        if isinstance(s1, TV):
            reads.append(s1)
        a2 = s2.ap if isinstance(s2, TV) else s2
        if isinstance(s2, TV):
            reads.append(s2)
        kw = {}
        if op1 is not None:
            kw["op1"] = op1
        o, i = out_tv.ap, in_tv.ap
        self._emit(eng, lambda E, o=o, i=i, a1=a1, a2=a2, op0=op0, kw=kw:
                   E.tensor_scalar(o, i, a1, a2, op0, **kw), reads, [out_tv])

    def stt(self, out_tv, in0_tv, scalar, in1_tv, op0, op1, eng="dve"):
        reads = [in0_tv, in1_tv]
        sc = scalar.ap if isinstance(scalar, TV) else scalar
        if isinstance(scalar, TV):
            reads.append(scalar)
        o, i0, i1 = out_tv.ap, in0_tv.ap, in1_tv.ap
        self._emit(eng, lambda E, o=o, i0=i0, sc=sc, i1=i1, op0=op0, op1=op1:
                   E.scalar_tensor_tensor(o, i0, sc, i1, op0, op1), reads, [out_tv])

    def red(self, op, out_tv, in_tv, negate=False, eng="dve"):
        o, i = out_tv.ap, in_tv.ap
        self._emit(eng, lambda E, o=o, i=i, op=op, n=negate:
                   E.tensor_reduce(o, i, AX.X, op, negate=n), [in_tv], [out_tv])

    def copy(self, out_tv, in_tv, eng="dve"):
        o, i = out_tv.ap, in_tv.ap
        if eng == "act":
            self._emit(eng, lambda E, o=o, i=i: E.copy(o, i), [in_tv], [out_tv])
        else:
            self._emit(eng, lambda E, o=o, i=i: E.tensor_copy(o, i), [in_tv], [out_tv])

    def memset(self, tv, c, eng="dve"):
        a = tv.ap
        self._emit(eng, lambda E, a=a, c=c: E.memset(a, c), [], [tv])

    def recip(self, out_tv, in_tv):
        o, i = out_tv.ap, in_tv.ap
        self._emit("dve", lambda E, o=o, i=i: E.reciprocal(o, i), [in_tv], [out_tv])

    def _dma(self, queue, fn, read_tvs, write_tvs):
        slot_i = self.dma_rr[queue]
        self.dma_rr[queue] = (slot_i + 1) % len(self.dma_slots[queue])
        sn = self.dma_slots[queue][slot_i]
        self._edge(queue, sn, 16 * self.slot_uses[sn])
        for tv in read_tvs:
            self._haz_read(queue, tv)
        for tv in write_tvs:
            self._haz_write(queue, tv)
        self.slot_uses[sn] += 1
        self.cnt[sn] += 16
        val = self.cnt[sn]
        s = self.sems[sn]
        self.q[queue].append(lambda E, fn=fn, s=s: fn(E).then_inc(s, 16))
        for tv in read_tvs:
            self._note_read(tv.name, self._breg(tv), sn, val)
        for tv in write_tvs:
            self._note_write(tv.name, self._breg(tv), sn, val)
        self.n_instr += 1

    def load(self, queue, dst_tv, src_ap):
        d = dst_tv.ap
        self._dma(queue, lambda E, d=d, s=src_ap: E.dma_start(d, s), [], [dst_tv])

    def store(self, queue, dst_ap, src_tv):
        s = src_tv.ap
        self._dma(queue, lambda E, d=dst_ap, s=s: E.dma_start(d, s), [src_tv], [])

    def sb2sb(self, queue, dst_tv, src_tv):
        d, s = dst_tv.ap, src_tv.ap
        self._dma(queue, lambda E, d=d, s=s: E.dma_start(d, s), [src_tv], [dst_tv])

    def iscatter(self, out_ap, out_name, idx_tv, in_tv, bound, op, in_dram=None):
        from concourse.bass import IndirectOffsetOnAxis
        i = idx_tv.ap
        if in_dram is not None:
            s, s_tv = in_dram[0], TV(_DramReg(in_dram[1]))
        else:
            s, s_tv = in_tv.ap, in_tv
        fake = TV(_DramReg(out_name))

        def _go(E, o=out_ap, i=i, s=s, b=bound, op=op):
            try:
                return E.indirect_dma_start(o, IndirectOffsetOnAxis(i, 0), s, None,
                                            bounds_check=b, oob_is_err=False, compute_op=op)
            except Exception:
                print("ISCATTER-FAIL out=", o, "idx=", i, "in=", s, "bound=", b, "op=", op)
                raise
        self._dma("pool", _go,
                  [idx_tv, s_tv, fake] if op != ALU.bypass else [idx_tv, s_tv],
                  [fake])

    def igather(self, out_tv, in_ap, in_name, idx_tv, bound):
        from concourse.bass import IndirectOffsetOnAxis
        o, i = out_tv.ap, idx_tv.ap
        fake = TV(_DramReg(in_name))
        self._dma("pool", lambda E, o=o, i=i, s=in_ap, b=bound:
                  E.indirect_dma_start(o, None, s, IndirectOffsetOnAxis(i, 0),
                                       bounds_check=b, oob_is_err=False, compute_op=ALU.bypass),
                  [idx_tv, fake], [out_tv])

    def dram_store(self, queue, dst_ap, dst_name, src_tv, region=None, accum=None):
        s = src_tv.ap
        fake = region if region is not None else TV(_DramReg(dst_name))
        kw = {} if accum is None else {"accum_op": accum}
        reads = [src_tv] + ([fake] if accum is not None else [])
        self._dma(queue, lambda E, d=dst_ap, s=s, kw=kw: E.dma_start(d, s, **kw),
                  reads, [fake])

    def dram_load(self, queue, dst_tv, src_ap, src_name, region=None):
        d = dst_tv.ap
        fake = region if region is not None else TV(_DramReg(src_name))
        self._dma(queue, lambda E, d=d, s=src_ap: E.dma_start(d, s), [fake], [dst_tv])

    def finish(self):
        for qn in DMA_QUEUES:
            for sn in self.dma_slots[qn]:
                self._edge(qn, sn, self.cnt[sn])

    def build(self):
        with self.nc.Block() as blk:
            @blk.tensor
            def _(E):
                for f in self.q["pe"]:
                    f(E)

            @blk.scalar
            def _(E):
                for f in self.q["act"]:
                    f(E)

            @blk.vector
            def _(E):
                for f in self.q["dve"]:
                    f(E)

            @blk.gpsimd
            def _(E):
                for f in self.q["pool"]:
                    f(E)

            @blk.sync
            def _(E):
                for f in self.q["sp"]:
                    f(E)


EMB = 1024
NH = 16
KVH = 4
HD = 64
QL = 64
KVL = 32
FFI = 4096
NEXP = 8
EPS = 1e-5
ROPE_BASE = 10000.0
HA = [0, 1, 2, 3, 8, 9, 10, 11]
HB = [4, 5, 6, 7, 12, 13, 14, 15]
CAP = 768
RS = 1024
I32 = mybir.dt.int32
DBG_STAGE = 3
DBG_R = 9


def host_prep(inputs, S=2048):
    f32 = np.float32
    d = {}
    Wqa = np.asarray(inputs["L0_attn_Wqa"], f32)
    Wqb = np.asarray(inputs["L0_attn_Wqb"], f32)
    Wkva = np.asarray(inputs["L0_attn_Wkva"], f32)
    Wkvb = np.asarray(inputs["L0_attn_Wkvb"], f32)
    Wout = np.asarray(inputs["L0_attn_Wout"], f32)
    bout = np.asarray(inputs["L0_attn_bout"], f32)
    W1s = np.asarray(inputs["L0_ffn_w1_shared"], f32)
    W2s = np.asarray(inputs["L0_ffn_w2_shared"], f32)
    W1r = np.asarray(inputs["L0_ffn_w1_routed"], f32)
    W2r = np.asarray(inputs["L0_ffn_w2_routed"], f32)
    Wr = np.asarray(inputs["L0_ffn_w_router"], f32)

    d["wqa_t"] = np.ascontiguousarray(Wqa.reshape(8, 128, QL).transpose(1, 0, 2).reshape(128, 8 * QL))
    d["wkva_t"] = np.ascontiguousarray(Wkva.reshape(8, 128, KVL).transpose(1, 0, 2).reshape(128, 8 * KVL))

    qe_idx = np.zeros(EMB, np.int64)
    qo_idx = np.zeros(EMB, np.int64)
    for j in range(8):
        for b, hl in ((0, HA), (1, HB)):
            h = hl[j]
            for r in range(64):
                qe_idx[j * 128 + b * 64 + r] = h * 64 + 2 * (r % 32)
                qo_idx[j * 128 + b * 64 + r] = h * 64 + 2 * (r % 32) + 1
    d["wqed"] = np.ascontiguousarray(Wqb[:, qe_idx])
    d["wqod"] = np.ascontiguousarray(Wqb[:, qo_idx])

    ke_idx = np.zeros(KVH * HD, np.int64)
    ko_idx = np.zeros(KVH * HD, np.int64)
    v_idx = np.zeros(KVH * HD, np.int64)
    for kh in range(KVH):
        for r in range(64):
            ke_idx[kh * 64 + r] = kh * 128 + 4 * (r % 32)
            ko_idx[kh * 64 + r] = kh * 128 + 4 * (r % 32) + 2
            v_idx[kh * 64 + r] = kh * 128 + 2 * r + 1
    d["wked"] = np.ascontiguousarray(Wkvb[:, ke_idx])
    d["wkod"] = np.ascontiguousarray(Wkvb[:, ko_idx])
    d["wv"] = np.ascontiguousarray(Wkvb[:, v_idx])

    rowperm = np.zeros(EMB, np.int64)
    for j in range(8):
        for b, hl in ((0, HA), (1, HB)):
            h = hl[j]
            for dd in range(64):
                rowperm[j * 128 + b * 64 + dd] = h * 64 + dd
    Wout_p = Wout[rowperm, :]
    d["wout_t"] = np.ascontiguousarray(
        Wout_p.reshape(8, 128, 8, 128).transpose(1, 2, 0, 3).reshape(128, 8192))
    d["bout_t"] = np.ascontiguousarray(bout.reshape(8, 128).T)

    inv = ROPE_BASE ** (-(np.arange(32, dtype=np.float64) / 32.0))
    ang = np.outer(inv, np.arange(S, dtype=np.float64))
    cos = np.cos(ang).astype(f32)
    sin = np.sin(ang).astype(f32)
    t1 = np.concatenate([cos, sin, cos, sin], 0)
    t2 = np.concatenate([-sin, cos, -sin, cos], 0)
    d["t1"] = np.ascontiguousarray(t1)
    d["t2"] = np.ascontiguousarray(t2)
    tri = np.zeros((128, 128), f32)
    iu = np.triu_indices(128, 1)
    tri[iu] = -1e30
    d["trineg"] = tri
    d["identf"] = np.eye(128, dtype=f32)

    def tile_w1(w):
        return np.ascontiguousarray(w.reshape(8, 128, 32, 128).transpose(1, 2, 0, 3).reshape(128, 32768))

    def tile_w2(w):
        return np.ascontiguousarray(w.reshape(32, 128, 2, 512).transpose(1, 2, 0, 3).reshape(128, 32768))

    def blockify(w):
        return np.ascontiguousarray(w.reshape(128, 16, 2048).transpose(1, 0, 2).reshape(16 * 128, 2048))

    d["w1s_t"] = tile_w1(W1s)
    d["w2s_t"] = tile_w2(W2s)
    d["w1r_t"] = np.ascontiguousarray(np.stack([blockify(tile_w1(W1r[e])) for e in range(NEXP)], 0).reshape(NEXP * 16 * 128, 2048))
    d["w2r_t"] = np.ascontiguousarray(np.stack([blockify(tile_w2(W2r[e])) for e in range(NEXP)], 0).reshape(NEXP * 16 * 128, 2048))
    d["wr_t"] = np.ascontiguousarray(Wr.reshape(8, 128, NEXP).transpose(1, 0, 2).reshape(128, 8 * NEXP))

    d["triu_t"] = np.triu(np.ones((128, 128), f32))
    d["srange"] = np.ascontiguousarray(
        np.broadcast_to(np.arange(CAP // 2, dtype=f32)[None, :], (128, CAP // 2)))
    d["ebase"] = np.ascontiguousarray(
        np.broadcast_to(np.arange(NEXP, dtype=f32)[None, :] * CAP, (128, NEXP)))

    x = np.asarray(inputs["L0_attn_x"], f32)
    xts = [np.ascontiguousarray(x[b].T) for b in range(x.shape[0])]
    return d, xts


def build_core(S=2048):
    nt = S // 128
    ns = S // 512
    nc = bass.Bass("TRN2", target_bir_lowering=False)

    xT = nc.dram_tensor("xT", [EMB, S], F32, kind="ExternalInput").ap()
    wqa_t = nc.dram_tensor("wqa_t", [128, 8 * QL], F32, kind="ExternalInput").ap()
    wqed = nc.dram_tensor("wqed", [QL, EMB], F32, kind="ExternalInput").ap()
    wqod = nc.dram_tensor("wqod", [QL, EMB], F32, kind="ExternalInput").ap()
    wkva_t = nc.dram_tensor("wkva_t", [128, 8 * KVL], F32, kind="ExternalInput").ap()
    wked = nc.dram_tensor("wked", [KVL, 256], F32, kind="ExternalInput").ap()
    wkod = nc.dram_tensor("wkod", [KVL, 256], F32, kind="ExternalInput").ap()
    wv = nc.dram_tensor("wv", [KVL, 256], F32, kind="ExternalInput").ap()
    wout_t = nc.dram_tensor("wout_t", [128, 8192], F32, kind="ExternalInput").ap()
    bout_t = nc.dram_tensor("bout_t", [128, 8], F32, kind="ExternalInput").ap()
    t1d = nc.dram_tensor("t1", [128, S], F32, kind="ExternalInput").ap()
    t2d = nc.dram_tensor("t2", [128, S], F32, kind="ExternalInput").ap()
    trineg_d = nc.dram_tensor("trineg", [128, 128], F32, kind="ExternalInput").ap()
    identf_d = nc.dram_tensor("identf", [128, 128], F32, kind="ExternalInput").ap()
    w1s_t = nc.dram_tensor("w1s_t", [128, 32768], F32, kind="ExternalInput").ap()
    w2s_t = nc.dram_tensor("w2s_t", [128, 32768], F32, kind="ExternalInput").ap()
    w1r_t = nc.dram_tensor("w1r_t", [NEXP * 16 * 128, 2048], F32, kind="ExternalInput").ap()
    w2r_t = nc.dram_tensor("w2r_t", [NEXP * 16 * 128, 2048], F32, kind="ExternalInput").ap()
    wr_t = nc.dram_tensor("wr_t", [128, 8 * NEXP], F32, kind="ExternalInput").ap()
    out_d = nc.dram_tensor("out", [S, EMB], F32, kind="ExternalOutput").ap()
    triu_d = nc.dram_tensor("triu_t", [128, 128], F32, kind="ExternalInput").ap()
    srange_d = nc.dram_tensor("srange", [128, CAP // 2], F32, kind="ExternalInput").ap()
    ebase_d = nc.dram_tensor("ebase", [128, NEXP], F32, kind="ExternalInput").ap()
    xg_d = nc.dram_tensor("xg", [NEXP * CAP + 128, RS], BF16).ap()
    acc_d = nc.dram_tensor("acc", [S, EMB], F32).ap()
    w1s_b = nc.dram_tensor("w1s_b", [2048, 2048], BF16).ap()
    w2s_b = nc.dram_tensor("w2s_b", [2048, 2048], BF16).ap()

    with contextlib.ExitStack() as ctx:
        E = ctx.enter_context
        fw = FW(nc, ctx)

        def sb(name, shape, dt):
            return TV(E(nc.sbuf_tensor(name, shape, dt)))

        def psum(name, shape, dt):
            t = TV(E(nc.psum_tensor(name, shape, dt)))
            fw.reg_psum(t.t, dt)
            return t

        xf = sb("xf", [128, 8 * S], F32)
        xb = sb("xb", [128, 8 * S], BF16)
        wqa_b = sb("wqa_b", [128, 8 * QL], BF16)
        wqed_b = sb("wqed_b", [QL, EMB], BF16)
        wqod_b = sb("wqod_b", [QL, EMB], BF16)
        wkva_b = sb("wkva_b", [128, 8 * KVL], BF16)
        wked_b = sb("wked_b", [KVL, 256], BF16)
        wkod_b = sb("wkod_b", [KVL, 256], BF16)
        wv_b = sb("wv_b", [KVL, 256], BF16)
        wr_b = sb("wr_b", [128, 8 * NEXP], BF16)
        idf = sb("idf", [128, 128], F32)
        idb = sb("idb", [128, 128], BF16)
        trineg = sb("trineg_s", [128, 128], F32)
        bout_s = sb("bout_s", [128, 8], F32)
        onescol = sb("onescol", [128, 1], BF16)
        onesrow_f = sb("onesrow_f", [1, 128], F32)
        onesrow_b = sb("onesrow_b", [1, 128], BF16)
        negm = sb("negm", [128, 1], F32)
        rs1 = sb("rs1", [128, 1], F32)
        rs2 = sb("rs2", [128, 1], F32)
        rsum = sb("rsum", [128, 1], F32)
        rc = sb("rc", [128, 1], F32)
        m1_8 = sb("m1_8", [128, 1], F32)
        negm1 = sb("negm1", [128, 1], F32)
        m2_8 = sb("m2_8", [128, 1], F32)
        e2b = sb("e2b", [128, 1], F32)
        rc2 = sb("rc2", [128, 1], F32)
        eq8 = sb("eq8", [128, NEXP], F32)
        msk8 = sb("msk8", [128, NEXP], F32)
        sel8 = sb("sel8", [128, NEXP], F32)
        en8 = sb("en8", [128, NEXP], F32)
        ssum = sb("ssum", [128, 1], F32)
        ssq = sb("ssq", [128, 1], F32)
        mmt = sb("mmt", [128, 1], F32)
        rstd2 = sb("rstd2", [128, 1], F32)
        mneg2 = sb("mneg2", [128, 1], F32)
        eps_c = sb("eps_c", [128, 1], F32)
        triu_b = sb("triu_b", [128, 128], BF16)
        srange_s = sb("srange_s", [128, CAP // 2], F32)
        sshift = sb("sshift", [128, 1], F32)
        ebase_s = sb("ebase_s", [128, NEXP], F32)
        sel_b = sb("sel_b", [128, (S // 128) * NEXP], BF16)
        eq_b = sb("eq_b", [128, (S // 128) * NEXP], BF16)
        slotf_all = sb("slotf_all", [128, (S // 128) * NEXP], F32)
        sif = sb("sif", [128, 2 * (S // 128)], F32)
        sidx = sb("sidx", [128, 2 * (S // 128)], I32)


        AB = sb("AB", [128, 39040], BF16)
        AFa = sb("AFa", [128, 5760], F32)
        qx = AB.v(f=slice(0, 8 * S))
        kbf = AB.v(f=slice(8 * S, 10 * S))
        vtm = AB.v(f=slice(10 * S, 10 * S + 4 * (S // 128) * 64))
        xab = AB.v(p=slice(0, 64), f=slice(12 * S, 13 * S))
        xkb = AB.v(p=slice(0, 32), f=slice(13 * S, 14 * S))
        pbf = AB.v(f=slice(14 * S, 15 * S))
        pTs = AB.v(f=slice(15 * S, 16 * S))
        t1b = AB.v(f=slice(16 * S, 17 * S))
        t2b = AB.v(f=slice(17 * S, 18 * S))
        diag_rc = AB.v(f=slice(18 * S, 18 * S + 128))
        sbf = AB.v(f=slice(18 * S + 128, 18 * S + 640))
        sqb = AB.v(f=slice(18 * S + 640, 18 * S + 1152))
        woutsl = [AB.v(f=slice(18 * S + 1152, 18 * S + 1152 + 1024)),
                  AB.v(f=slice(15 * S, 15 * S + 1024))]
        y1e = AB.v(f=slice(0, 24576))
        x1e = AB.v(f=slice(24576, 24576 + 8 * CAP))
        y2sb = AB.v(f=slice(24576, 24576 + 8 * CAP))
        wst = [AB.v(f=slice(30720 + i * 2048, 30720 + (i + 1) * 2048)) for i in range(2)]
        xgt = [AB.v(f=slice(34816 + i * 1024, 34816 + (i + 1) * 1024)) for i in range(2)]
        ptw = AB.v(f=slice(36864, 36864 + CAP))
        rwsb = AB.v(f=slice(36864 + CAP, 36864 + 2 * CAP))

        ropes1 = AFa.v(f=slice(0, 512))
        ropes2 = AFa.v(f=slice(512, 1024))
        sdiag = AFa.v(f=slice(1024, 1152))
        st_sum = AFa.v(p=slice(0, 1), f=slice(1152, 1152 + S))
        st_sq = AFa.v(p=slice(0, 1), f=slice(1152 + S, 1152 + 2 * S))
        st_tmp = AFa.v(p=slice(0, 1), f=slice(1152 + 2 * S, 1152 + 2 * S + 512))
        scrf = AFa.v(f=slice(0, 512))
        cwb = AFa.v(f=slice(512, 1024))
        lg_sb = AFa.v(p=slice(0, NEXP), f=slice(1152, 1152 + S))
        lgtm = AFa.v(f=slice(1152 + S, 1152 + S + 8 * nt))
        cw_tm = AFa.v(f=slice(1152 + S + 8 * nt, 1152 + S + 16 * nt))
        ffn_tmp = AFa.v(f=slice(1152 + S + 16 * nt, 1664 + S + 16 * nt))
        totals_sb = AFa.v(p=slice(0, 1), f=slice(3968, 3968 + nt * 8))
        offs_sb = AFa.v(p=slice(0, 1), f=slice(4096, 4096 + nt * 8))
        csum_all = AFa.v(f=slice(4224, 4224 + nt * 8))
        slotw = AFa.v(f=slice(4352, 4360))
        slotm = AFa.v(f=slice(4360, 4368))
        y2h = [AFa.v(f=slice(4368 + i * 512, 4368 + (i + 1) * 512)) for i in range(2)]
        s2tt = [AFa.v(f=slice(0, 1024)), AFa.v(f=slice(1024, 2048))]
        sqtt = AFa.v(f=slice(3072, 4096))
        outtm = [AFa.v(f=slice(2048, 3072)), AFa.v(f=slice(4096, 5120))]
        accl0 = AFa.v(f=slice(3072, 3584))
        accl1 = AFa.v(f=slice(3584, 4096))

        ps_big = psum("ps_big", [128, 2048], F32)
        ps_big2 = psum("ps_big2", [128, 2048], F32)
        pb_a = TV(ps_big2.t, f=slice(0, 512))
        pb_b = TV(ps_big2.t, f=slice(512, 1024))
        po = TV(ps_big2.t, f=slice(1024, 1536))
        pt = TV(ps_big2.t, f=slice(1536, 2048))

        for c in range(8):
            fw.load("sp", xf.v(f=slice(c * S, (c + 1) * S)), xT[c * 128:(c + 1) * 128, :])
        fw.load("sp", idf, identf_d)
        fw.load("sp", trineg, trineg_d)
        fw.load("sp", bout_s, bout_t)
        fw.load("pool", idb, identf_d)
        fw.load("pool", t1b, t1d)
        fw.load("pool", t2b, t2d)
        fw.load("pool", wqa_b, wqa_t)
        fw.load("pool", wqed_b, wqed)
        fw.load("pool", wqod_b, wqod)
        fw.load("pool", wkva_b, wkva_t)
        fw.load("pool", wked_b, wked)
        fw.load("pool", wkod_b, wkod)
        fw.load("pool", wv_b, wv)
        fw.load("pool", wr_b, wr_t)
        fw.memset(onescol, 1.0)
        fw.memset(eps_c, EPS)
        fw.load("pool", triu_b, triu_d)
        fw.load("sp", srange_s, srange_d)
        fw.load("sp", ebase_s, ebase_d)
        fw.memset(xgt[0].v(f=slice(0, RS)), 0.0)
        for r0 in range(0, NEXP * CAP + 128, 128):
            fw.dram_store("act", xg_d[r0:r0 + 128, :], "xg", xgt[0].v(f=slice(0, RS)))
        fw.memset(onesrow_f, 1.0)
        fw.memset(onesrow_b, 1.0)
        fw.ts(wqa_b, wqa_b, 0.125, ALU.mult)
        for c in range(8):
            fw.copy(xb.v(f=slice(c * S, (c + 1) * S)), xf.v(f=slice(c * S, (c + 1) * S)))

        bnc = AFa.t.bitcast(BF16)
        WSH = [2048, 2048]
        cb_i = [0]

        def cast_block(src_ap, dst_ap, name, c0):
            blk = c0 // 2048
            i = cb_i[0] % 4
            cb_i[0] += 1
            trk = AFa.v(f=slice(1152 + i * 1024, 1152 + (i + 1) * 1024))
            bap = bnc[:, 2304 + i * 2048:2304 + (i + 1) * 2048]
            fw._dma("pool",
                    lambda E, d=bap, s=src_ap[0:128, c0:c0 + 2048]: E.dma_start(d, s),
                    [], [trk])
            reg = dreg(name, WSH, slice(blk * 128, (blk + 1) * 128), slice(0, 2048))
            fw._dma("sp",
                    lambda E, d=dst_ap[blk * 128:(blk + 1) * 128, :], s=bap: E.dma_start(d, s),
                    [trk], [reg])

        for c0 in range(0, 32768, 2048):
            cast_block(w1s_t, w1s_b, "w1sb", c0)
        for c0 in range(0, 32768, 2048):
            cast_block(w2s_t, w2s_b, "w2sb", c0)

        for t in range(ns):
            sl = slice(t * 512, (t + 1) * 512)
            pscur = TV(ps_big.t, p=slice(0, 64), f=sl)
            for k in range(8):
                fw.mm(pscur, TV(wqa_b.t, f=slice(k * 64, (k + 1) * 64)),
                      xb.v(f=slice(k * S + t * 512, k * S + (t + 1) * 512)),
                      start=(k == 0), stop=(k == 7))
            fw.copy(xab.v(f=sl), pscur)
        for t in range(ns):
            sl = slice(t * 512, (t + 1) * 512)
            pscur = TV(ps_big.t, p=slice(0, 32), f=sl)
            for k in range(8):
                fw.mm(pscur, TV(wkva_b.t, f=slice(k * 32, (k + 1) * 32)),
                      xb.v(f=slice(k * S + t * 512, k * S + (t + 1) * 512)),
                      start=(k == 0), stop=(k == 7))
            fw.copy(xkb.v(f=sl), pscur)

        for j in range(8):
            for t in range(ns):
                sl = slice(t * 512, (t + 1) * 512)
                psE = TV(ps_big.t, f=slice((t % 2) * 512, (t % 2) * 512 + 512))
                psO = TV(ps_big.t, f=slice(1024 + (t % 2) * 512, 1024 + (t % 2) * 512 + 512))
                fw.mm(psE, TV(wqed_b.t, f=slice(j * 128, (j + 1) * 128)), xab.v(f=sl), True, True)
                fw.mm(psO, TV(wqod_b.t, f=slice(j * 128, (j + 1) * 128)), xab.v(f=sl), True, True)
                fw.tt(ALU.mult, ropes1, psE, t1b.v(f=sl))
                fw.tt(ALU.mult, ropes2, psO, t2b.v(f=sl))
                fw.tt(ALU.add, qx.v(f=slice(j * S + t * 512, j * S + (t + 1) * 512)), ropes1, ropes2)
        for kc in range(2):
            for t in range(ns):
                sl = slice(t * 512, (t + 1) * 512)
                psE = TV(ps_big.t, f=slice((t % 2) * 512, (t % 2) * 512 + 512))
                psO = TV(ps_big.t, f=slice(1024 + (t % 2) * 512, 1024 + (t % 2) * 512 + 512))
                fw.mm(psE, TV(wked_b.t, f=slice(kc * 128, (kc + 1) * 128)), xkb.v(f=sl), True, True)
                fw.mm(psO, TV(wkod_b.t, f=slice(kc * 128, (kc + 1) * 128)), xkb.v(f=sl), True, True)
                fw.tt(ALU.mult, ropes1, psE, t1b.v(f=sl))
                fw.tt(ALU.mult, ropes2, psO, t2b.v(f=sl))
                fw.tt(ALU.add, kbf.v(f=slice(kc * S + t * 512, kc * S + (t + 1) * 512)), ropes1, ropes2)
        for kc in range(2):
            for t in range(ns):
                sl = slice(t * 512, (t + 1) * 512)
                pscur = TV(ps_big.t, f=slice((t % 4) * 512, (t % 4) * 512 + 512))
                fw.mm(pscur, TV(wv_b.t, f=slice(kc * 128, (kc + 1) * 128)), xkb.v(f=sl), True, True)
                fw.copy(pbf.v(f=sl), pscur)
            for b in range(2):
                kh = 2 * kc + b
                for tt_ in range(nt):
                    fw.mm(TV(pt.t, f=slice((tt_ % 8) * 64, (tt_ % 8) * 64 + 64)),
                          pbf.v(p=slice(b * 64, b * 64 + 64), f=slice(tt_ * 128, (tt_ + 1) * 128)),
                          TV(idb.t, p=slice(b * 64, b * 64 + 64), f=slice(b * 64, b * 64 + 64)),
                          True, True)
                    if tt_ % 8 == 7 or tt_ == nt - 1:
                        lo = (tt_ // 8) * 8
                        fw.copy(vtm.v(f=slice(kh * (nt * 64) + lo * 64, kh * (nt * 64) + (tt_ + 1) * 64)),
                                TV(pt.t, f=slice(0, (tt_ + 1 - lo) * 64)))

        negm2 = sb("negm2", [128, 1], F32)
        rs1b = sb("rs1b", [128, 1], F32)
        rs2b = sb("rs2b", [128, 1], F32)
        rsumb = sb("rsumb", [128, 1], F32)
        rcb = sb("rcb", [128, 1], F32)
        pbf2 = t1b
        pTs2 = t2b
        sdiag2 = AFa.v(f=slice(0, 128))
        drc2 = AB.v(f=slice(18 * S + 128, 18 * S + 256))
        groups = [ps_big, ps_big2]
        bufsets = [
            dict(pbf=pbf, pTs=pTs, sdiag=sdiag, negm=negm, rs1=rs1, rs2=rs2,
                 rsum=rsum, rc=rc, drc=diag_rc),
            dict(pbf=pbf2, pTs=pTs2, sdiag=sdiag2, negm=negm2, rs1=rs1b,
                 rs2=rs2b, rsum=rsumb, rc=rcb, drc=drc2),
        ]

        for j in range(8):
            for b_, hl in ((0, HA), (1, HB)):
                h = hl[j]
                rb = b_ * 64
                kh = h // 4
                kc = kh // 2

                def sc_stage(qt):
                    G = groups[qt % 2]
                    kspan = (qt + 1) * 128
                    qsl = qx.v(p=slice(rb, rb + 64), f=slice(j * S + qt * 128, j * S + (qt + 1) * 128))
                    for si in range((kspan + 511) // 512):
                        sl0 = si * 512
                        N = min(512, kspan - sl0)
                        fw.mm(G.v(f=slice(sl0, sl0 + N)), qsl,
                              kbf.v(p=slice(rb, rb + 64), f=slice(kc * S + sl0, kc * S + sl0 + N)),
                              True, True)

                def red_stage(qt):
                    G, B = groups[qt % 2], bufsets[qt % 2]
                    kspan = (qt + 1) * 128
                    nfull = qt * 128
                    fw.red(ALU.max, B["negm"], G.v(f=slice(0, kspan)), negate=True)
                    fw.tt(ALU.add, B["sdiag"], G.v(f=slice(nfull, kspan)), trineg)

                def exp_stage(qt):
                    G, B = groups[qt % 2], bufsets[qt % 2]
                    kspan = (qt + 1) * 128
                    nfull = qt * 128
                    if nfull > 0:
                        fw.act(B["pbf"].v(f=slice(0, nfull)), G.v(f=slice(0, nfull)),
                               AF.Exp, bias=B["negm"], accum=B["rs1"])
                    fw.act(B["pbf"].v(f=slice(nfull, kspan)), B["sdiag"], AF.Exp,
                           bias=B["negm"], accum=B["rs2"])

                def post_stage(qt):
                    B = bufsets[qt % 2]
                    if qt > 0:
                        fw.tt(ALU.add, B["rsum"], B["rs1"], B["rs2"])
                    else:
                        fw.copy(B["rsum"], B["rs2"])
                    fw.recip(B["rc"], B["rsum"])
                    fw.ts(B["drc"], idb, B["rc"], ALU.mult)

                def trpv_stage(qt):
                    G, B = groups[qt % 2], bufsets[qt % 2]
                    if qt <= 10:
                        tbs = [groups[(qt + 1) % 2].v(f=slice(1536, 2048)),
                               G.v(f=slice(1536, 2048))]
                        pocur = groups[(qt + 1) % 2].v(p=slice(0, 64), f=slice(1536, 1664))
                    else:
                        tbs = [G.v(f=slice(0, 512)), G.v(f=slice(512, 1024))]
                        pocur = G.v(p=slice(0, 64), f=slice(1024, 1152))
                    for g in range(0, qt + 1, 4):
                        ntile = min(4, qt + 1 - g)
                        tb = tbs[(g // 4) % 2]
                        for kt in range(g, g + ntile):
                            fw.mm(tb.v(f=slice((kt - g) * 128, (kt - g + 1) * 128)),
                                  B["pbf"].v(f=slice(kt * 128, (kt + 1) * 128)), B["drc"],
                                  kt == g, kt == g + ntile - 1)
                        fw.copy(B["pTs"].v(f=slice(g * 128, (g + ntile) * 128)),
                                tb.v(f=slice(0, ntile * 128)), eng="act")
                    for kt in range(qt + 1):
                        fw.mm(pocur, vtm.v(f=slice(kh * (nt * 64) + kt * 64, kh * (nt * 64) + (kt + 1) * 64)),
                              B["pTs"].v(f=slice(kt * 128, (kt + 1) * 128)),
                              start=(kt == 0), stop=(kt == qt))
                    fw.copy(qx.v(p=slice(rb, rb + 64), f=slice(j * S + qt * 128, j * S + (qt + 1) * 128)),
                            pocur, eng="dve")

                for qt in range(nt + 1):
                    if qt < nt:
                        sc_stage(qt)
                    if qt >= 1:
                        post_stage(qt - 1)
                    if qt < nt:
                        red_stage(qt)
                    if qt >= 1:
                        trpv_stage(qt - 1)
                    if qt < nt:
                        exp_stage(qt)

        for c in range(8):
            wsl = woutsl[c % 2]
            fw.load("pool", wsl, wout_t[:, c * 1024:(c + 1) * 1024])
            for t in range(ns):
                sl = slice(t * 512, (t + 1) * 512)
                pscur = TV(ps_big.t, f=sl)
                for k in range(8):
                    fw.mm(pscur, wsl.v(f=slice(k * 128, (k + 1) * 128)),
                          qx.v(f=slice(k * S + t * 512, k * S + (t + 1) * 512)),
                          start=(k == 0), stop=(k == 7))
                xfs = xf.v(f=slice(c * S + t * 512, c * S + (t + 1) * 512))
                fw.stt(xfs, pscur, TV(bout_s.t, f=slice(c, c + 1)), xfs, ALU.add, ALU.add)
        for t in range(ns):
            ss_row = st_sum.v(f=slice(t * 512, (t + 1) * 512))
            sq_row = st_sq.v(f=slice(t * 512, (t + 1) * 512))
            for c in range(8):
                pss = TV(ps_big.t, p=slice(0, 1), f=slice((c % 2) * 512, (c % 2) * 512 + 512))
                psq = TV(ps_big.t, p=slice(0, 1), f=slice(1024 + (c % 2) * 512, 1024 + (c % 2) * 512 + 512))
                xfs = xf.v(f=slice(c * S + t * 512, c * S + (t + 1) * 512))
                fw.copy(sbf, xfs, eng="act")
                fw.act(sqb, xfs, AF.Square)
                fw.mm(pss, onescol, sbf, True, True)
                fw.mm(psq, onescol, sqb, True, True)
                if c == 0:
                    fw.copy(ss_row, pss)
                    fw.copy(sq_row, psq)
                else:
                    fw.tt(ALU.add, ss_row, ss_row, pss)
                    fw.tt(ALU.add, sq_row, sq_row, psq)
        fw.ts(st_sum, st_sum, 1.0 / EMB, ALU.mult)
        fw.ts(st_sq, st_sq, 1.0 / EMB, ALU.mult)
        for hf in range(0, S, 512):
            w_ = min(512, S - hf)
            su = st_sum.v(f=slice(hf, hf + w_))
            sq = st_sq.v(f=slice(hf, hf + w_))
            tp = st_tmp.v(f=slice(0, w_))
            fw.tt(ALU.mult, tp, su, su)
            fw.tt(ALU.subtract, sq, sq, tp)
            fw.act(tp, sq, AF.Sqrt, bias=eps_c.v(p=slice(0, 1)))
            fw.recip(sq, tp)
            fw.stt(su, su, -1.0, sq, ALU.mult, ALU.mult)
        for t in range(ns):
            fw.mm(pb_a, onesrow_f,
                  TV(st_sq.t, p=slice(0, 1), f=slice(st_sq.f0 + t * 512, st_sq.f0 + (t + 1) * 512)), True, True)
            fw.mm(pb_b, onesrow_f,
                  TV(st_sum.t, p=slice(0, 1), f=slice(st_sum.f0 + t * 512, st_sum.f0 + (t + 1) * 512)), True, True)
            for c in range(8):
                xfs = xf.v(f=slice(c * S + t * 512, c * S + (t + 1) * 512))
                fw.tt(ALU.mult, scrf, xfs, pb_a)
                fw.tt(ALU.add, xb.v(f=slice(c * S + t * 512, c * S + (t + 1) * 512)), scrf, pb_b)

        for t in range(ns):
            psl = TV(ps_big.t, p=slice(0, NEXP), f=slice(t * 512, (t + 1) * 512))
            for k in range(8):
                fw.mm(psl, TV(wr_b.t, f=slice(k * NEXP, (k + 1) * NEXP)),
                      xb.v(f=slice(k * S + t * 512, k * S + (t + 1) * 512)),
                      start=(k == 0), stop=(k == 7))
            fw.copy(TV(lg_sb.t, p=slice(0, NEXP), f=slice(lg_sb.f0 + t * 512, lg_sb.f0 + (t + 1) * 512)), psl)
        for tt_ in range(nt):
            fw.mm(pb_a.v(f=slice((tt_ % 16) * 8, (tt_ % 16) * 8 + 8)),
                  lg_sb.v(f=slice(tt_ * 128, (tt_ + 1) * 128)),
                  TV(idf.t, p=slice(0, NEXP), f=slice(0, NEXP)), True, True)
            if tt_ % 16 == 15 or tt_ == nt - 1:
                lo = (tt_ // 16) * 16
                fw.copy(lgtm.v(f=slice(lo * 8, (tt_ + 1) * 8)),
                        pb_a.v(f=slice(0, (tt_ + 1 - lo) * 8)))
        for tt_ in range(nt):
            lgt = TV(lgtm.t, f=slice(lgtm.f0 + tt_ * 8, lgtm.f0 + (tt_ + 1) * 8))
            fw.red(ALU.max, m1_8, lgt)
            fw.ts(negm1, m1_8, -1.0, ALU.mult)
            fw.ts(eq8, lgt, m1_8, ALU.is_equal)
            fw.stt(msk8, eq8, -1e30, lgt, ALU.mult, ALU.add)
            fw.red(ALU.max, m2_8, msk8)
            fw.ts(sel8, lgt, m2_8, ALU.is_ge)
            fw.act(en8, lgt, AF.Exp, bias=negm1)
            fw.act(e2b, m2_8, AF.Exp, bias=negm1)
            fw.ts(e2b, e2b, 1.0, ALU.add)
            fw.recip(rc2, e2b)
            fw.tt(ALU.mult, en8, en8, sel8)
            fw.ts(TV(cw_tm.t, f=slice(cw_tm.f0 + tt_ * 8, cw_tm.f0 + (tt_ + 1) * 8)), en8, rc2, ALU.mult)
            fw.copy(sel_b.v(f=slice(tt_ * 8, (tt_ + 1) * 8)), sel8)
            fw.copy(eq_b.v(f=slice(tt_ * 8, (tt_ + 1) * 8)), eq8)

        def p5_slots():
            for tt_ in range(nt):
                selt = sel_b.v(f=slice(tt_ * 8, (tt_ + 1) * 8))
                fw.mm(pb_a.v(f=slice(0, 8)), triu_b, selt, True, True)
                fw.mm(pb_b.v(p=slice(0, 1), f=slice(0, 8)), onescol, selt, True, True)
                fw.copy(csum_all.v(f=slice(tt_ * 8, (tt_ + 1) * 8)), pb_a.v(f=slice(0, 8)))
                fw.copy(totals_sb.v(f=slice(tt_ * 8, (tt_ + 1) * 8)), pb_b.v(p=slice(0, 1), f=slice(0, 8)))
            fw.memset(offs_sb.v(f=slice(0, 8)), 0.0)
            for tt_ in range(1, nt):
                fw.tt(ALU.add, offs_sb.v(f=slice(tt_ * 8, (tt_ + 1) * 8)),
                      offs_sb.v(f=slice((tt_ - 1) * 8, tt_ * 8)),
                      totals_sb.v(f=slice((tt_ - 1) * 8, tt_ * 8)))
            DUM = float(NEXP * CAP)
            LDUM = 4096.0
            for tt_ in range(nt):
                fw.mm(pb_a.v(f=slice(0, 8)), onesrow_f, offs_sb.v(f=slice(tt_ * 8, (tt_ + 1) * 8)), True, True)
                fw.tt(ALU.add, slotw, csum_all.v(f=slice(tt_ * 8, (tt_ + 1) * 8)), pb_a.v(f=slice(0, 8)))
                fw.ts(slotw, slotw, -1.0, ALU.add)
                fw.ts(slotm, slotw, float(CAP - 1) + 0.5, ALU.is_le)
                fw.tt(ALU.mult, slotm, slotm, sel_b.v(f=slice(tt_ * 8, (tt_ + 1) * 8)))
                lf = slotf_all.v(f=slice(tt_ * 8, (tt_ + 1) * 8))
                fw.ts(lf, slotw, -LDUM, ALU.add)
                fw.tt(ALU.mult, lf, lf, slotm)
                fw.ts(lf, lf, LDUM, ALU.add)
                fw.tt(ALU.add, slotw, slotw, ebase_s)
                fw.ts(slotw, slotw, -DUM, ALU.add)
                fw.tt(ALU.mult, slotw, slotw, slotm)
                fw.ts(slotw, slotw, DUM, ALU.add)
                fw.tt(ALU.mult, slotm, slotw, eq_b.v(f=slice(tt_ * 8, (tt_ + 1) * 8)))
                fw.red(ALU.add, TV(sif.t, f=slice(tt_, tt_ + 1)), slotm)
                fw.tt(ALU.subtract, slotm, sel_b.v(f=slice(tt_ * 8, (tt_ + 1) * 8)),
                      eq_b.v(f=slice(tt_ * 8, (tt_ + 1) * 8)))
                fw.tt(ALU.mult, slotm, slotm, slotw)
                fw.red(ALU.add, TV(sif.t, f=slice(nt + tt_, nt + tt_ + 1)), slotm)
            fw.copy(TV(sidx.t, f=slice(0, 2 * nt)), TV(sif.t, f=slice(0, 2 * nt)))

        def p5b(tt_):
            x1row = xgt[tt_ % 2]
            for c in range(8):
                bank = pb_a if c < 4 else pb_b
                fw.mm(bank.v(f=slice((c % 4) * 128, (c % 4) * 128 + 128)),
                      xb.v(f=slice(c * S + tt_ * 128, c * S + (tt_ + 1) * 128)), idb,
                      c % 4 == 0, c % 4 == 3)
                if c == 3:
                    fw.copy(x1row.v(f=slice(0, 512)), pb_a, eng="act")
                if c == 7:
                    fw.copy(x1row.v(f=slice(512, 1024)), pb_b, eng="act")
            if DBG_STAGE >= 1:
                fw.iscatter(xg_d[:, :], "xg", TV(sidx.t, f=slice(tt_, tt_ + 1)),
                            x1row.v(f=slice(0, RS)), NEXP * CAP, ALU.bypass)
                fw.iscatter(xg_d[:, :], "xg", TV(sidx.t, f=slice(nt + tt_, nt + tt_ + 1)),
                            x1row.v(f=slice(0, RS)), NEXP * CAP, ALU.bypass)

        wrot = [0]

        def wslot():
            s_ = wst[wrot[0] % 2]
            wrot[0] += 1
            return s_

        psum6 = [TV(ps_big.t, f=slice(0, 512)), TV(ps_big.t, f=slice(512, 1024)),
                 TV(ps_big.t, f=slice(1024, 1536)), TV(ps_big.t, f=slice(1536, 2048)),
                 pb_a, pb_b]
        ACCSH = [S, EMB]

        def shared_f1(t):
            for bi in range(16):
                wsl = wslot()
                fw.dram_load("pool", wsl, w1s_b[bi * 128:(bi + 1) * 128, :], "w1sb",
                             region=dreg("w1sb", WSH, slice(bi * 128, (bi + 1) * 128),
                                         slice(0, 2048)))
                for s_ in range(2):
                    m1 = bi * 2 + s_
                    pscur = TV(ps_big.t, f=slice((m1 % 4) * 512, (m1 % 4) * 512 + 512))
                    for k in range(8):
                        fw.mm(pscur, wsl.v(f=slice(s_ * 1024 + k * 128, s_ * 1024 + (k + 1) * 128)),
                              xb.v(f=slice(k * S + t * 512, k * S + (t + 1) * 512)),
                              start=(k == 0), stop=(k == 7))
                    fw.act(y1e.v(f=slice(m1 * 512, (m1 + 1) * 512)), pscur, AF.Silu)

        def shared_f2(t):
            for fh in range(2):
                for bi in range(8):
                    wsl = wslot()
                    blk_ = fh * 8 + bi
                    fw.dram_load("pool", wsl, w2s_b[blk_ * 128:(blk_ + 1) * 128, :], "w2sb",
                                 region=dreg("w2sb", WSH, slice(blk_ * 128, (blk_ + 1) * 128),
                                             slice(0, 2048)))
                    for kk in range(4):
                        k2 = bi * 4 + kk
                        for tile in range(4):
                            fw.mm(psum6[tile], y1e.v(f=slice(k2 * 512 + tile * 128, k2 * 512 + (tile + 1) * 128)),
                                  wsl.v(f=slice(kk * 512, (kk + 1) * 512)),
                                  start=(k2 == 0), stop=(kk == 3), skip_check=(0 < k2 < 31))
                for tile in range(4):
                    fw.copy(y2h[tile % 2], psum6[tile])
                    r0 = t * 512 + tile * 128
                    fw.dram_store("act", acc_d[r0:r0 + 128, fh * 512:(fh + 1) * 512],
                                  "acc", y2h[tile % 2],
                                  region=dreg("acc", ACCSH, slice(r0, r0 + 128),
                                              slice(fh * 512, (fh + 1) * 512)))

        shared_f1(0)
        p5_slots()
        shared_f2(0)
        if ns > 1:
            for x in range(nt // 2):
                p5b(x)
            shared_f1(1)
            for x in range(nt // 2, nt):
                p5b(x)
            shared_f2(1)
            for t in range(2, ns):
                shared_f1(t)
                shared_f2(t)
        else:
            for x in range(nt):
                p5b(x)

        NTE = CAP // 128
        pre_w1 = [None, None]

        def wload(wsl_, src_slice):
            fw.load("pool", wsl_, src_slice)

        def prefetch_w1(e_):
            for b2 in range(2):
                wsl_ = wslot()
                r0_ = (e_ * 16 + b2) * 128
                wload(wsl_, w1r_t[r0_:r0_ + 128, :])
                pre_w1[b2] = wsl_

        if DBG_STAGE >= 0 and DBG_R >= 9:
            prefetch_w1(0)
        for e in range(NEXP if DBG_STAGE >= 0 else 0):
            for tile in range(NTE if DBG_R >= 1 else 0):
                xslot = xgt[tile % 2]
                fw.dram_load("sp", xslot.v(f=slice(0, RS)),
                             xg_d[e * CAP + tile * 128:e * CAP + (tile + 1) * 128, :], "xg")
                for c in range(8 if DBG_R >= 6 else 0):
                    bank = pb_a if c < 4 else pb_b
                    fw.mm(bank.v(f=slice((c % 4) * 128, (c % 4) * 128 + 128)),
                          xslot.v(f=slice(c * 128, (c + 1) * 128)), idb,
                          c % 4 == 0, c % 4 == 3)
                    if c == 3 and DBG_R >= 7:
                        for cc in range(4):
                            fw.copy(x1e.v(f=slice(cc * CAP + tile * 128, cc * CAP + (tile + 1) * 128)),
                                    pb_a.v(f=slice(cc * 128, (cc + 1) * 128)))
                    if c == 7 and DBG_R >= 7:
                        for cc in range(4):
                            fw.copy(x1e.v(f=slice((cc + 4) * CAP + tile * 128, (cc + 4) * CAP + (tile + 1) * 128)),
                                    pb_b.v(f=slice(cc * 128, (cc + 1) * 128)))
            for bi in range(16 if DBG_R >= 9 else 0):
                if bi < 2:
                    wsl = pre_w1[bi]
                else:
                    wsl = wslot()
                    r0_ = (e * 16 + bi) * 128
                    wload(wsl, w1r_t[r0_:r0_ + 128, :])
                for s_ in range(2):
                    m1 = bi * 2 + s_
                    ps0 = TV(ps_big.t, f=slice((m1 % 2) * 1024, (m1 % 2) * 1024 + 512))
                    ps1 = TV(ps_big.t, f=slice((m1 % 2) * 1024 + 512, (m1 % 2) * 1024 + 512 + 256))
                    for k in range(8):
                        lh = wsl.v(f=slice(s_ * 1024 + k * 128, s_ * 1024 + (k + 1) * 128))
                        fw.mm(ps0, lh, x1e.v(f=slice(k * CAP, k * CAP + 512)),
                              start=(k == 0), stop=(k == 7))
                        fw.mm(ps1, lh, x1e.v(f=slice(k * CAP + 512, k * CAP + 768)),
                              start=(k == 0), stop=(k == 7))
                    fw.act(y1e.v(f=slice(m1 * CAP, m1 * CAP + 512)), ps0, AF.Silu)
                    fw.act(y1e.v(f=slice(m1 * CAP + 512, m1 * CAP + 768)), ps1, AF.Silu)
            for fh in range(2 if DBG_R >= 9 else 0):
                for bi in range(8):
                    wsl = wslot()
                    r0_ = (e * 16 + fh * 8 + bi) * 128
                    wload(wsl, w2r_t[r0_:r0_ + 128, :])
                    for kk in range(4):
                        k2 = bi * 4 + kk
                        for tile in range(NTE):
                            fw.mm(psum6[tile], y1e.v(f=slice(k2 * CAP + tile * 128, k2 * CAP + (tile + 1) * 128)),
                                  wsl.v(f=slice(kk * 512, (kk + 1) * 512)),
                                  start=(k2 == 0), stop=(kk == 3), skip_check=(0 < k2 < 31))
                for tile in range(NTE):
                    fw.copy(y2sb.v(f=slice(tile * 1024 + fh * 512, tile * 1024 + (fh + 1) * 512)),
                            psum6[tile], eng="act")
            if e + 1 < NEXP and DBG_R >= 9:
                prefetch_w1(e + 1)
            ptws = [ptw, AB.v(f=slice(30 * CAP, 31 * CAP))]
            rwsbs = [rwsb, AB.v(f=slice(31 * CAP, 32 * CAP)),
                     AB.v(f=slice(29 * CAP, 30 * CAP))]
            trbanks = [(pb_a, pb_b), (TV(ps_big.t, f=slice(0, 512)), TV(ps_big.t, f=slice(512, 1024)))]
            prs2 = [(po, pt),
                    (TV(ps_big.t, f=slice(1024, 1536)), TV(ps_big.t, f=slice(1536, 2048)))]
            y2cs = [TV(y2h[0].t, f=slice(y2h[0].f0, y2h[0].f0 + 1024)),
                    AFa.v(f=slice(0, 1024))]

            def ret_build(tt_):
                p_ = ptws[tt_ % 2]
                slotc = TV(slotf_all.t, f=slice(tt_ * 8 + e, tt_ * 8 + e + 1))
                fw.ts(p_.v(f=slice(0, CAP // 2)), srange_s, slotc, ALU.is_equal)
                fw.ts(sshift, slotc, -float(CAP // 2), ALU.add)
                fw.ts(p_.v(f=slice(CAP // 2, CAP)), srange_s, sshift, ALU.is_equal)
                fw.ts(p_.v(f=slice(0, CAP)), p_.v(f=slice(0, CAP)),
                      TV(cw_tm.t, f=slice(cw_tm.f0 + tt_ * 8 + e, cw_tm.f0 + tt_ * 8 + e + 1)), ALU.mult)

            def ret_tr(tt_):
                p_ = ptws[tt_ % 2]
                r_ = rwsbs[tt_ % 3]
                bA, bB = trbanks[tt_ % 2]
                for st in range(NTE):
                    bank = bA if st < 4 else bB
                    fw.mm(bank.v(f=slice((st % 4) * 128, (st % 4) * 128 + 128)),
                          p_.v(f=slice(st * 128, (st + 1) * 128)), idb,
                          st % 4 == 0, st % 4 == 3 or st == NTE - 1)
                    if st == 3:
                        fw.copy(r_.v(f=slice(0, 512)), bA)
                    if st == 5:
                        fw.copy(r_.v(f=slice(512, 768)), bB.v(f=slice(0, 256)), eng="act")

            def ret_mm(tt_):
                r_ = rwsbs[tt_ % 3]
                y2c = y2cs[tt_ % 2]
                for fh in range(2):
                    pr = prs2[tt_ % 2][fh]
                    for st in range(NTE):
                        fw.mm(pr, r_.v(f=slice(st * 128, (st + 1) * 128)),
                              y2sb.v(f=slice(st * 1024 + fh * 512, st * 1024 + (fh + 1) * 512)),
                              start=(st == 0), stop=(st == NTE - 1))
                    fw.copy(TV(y2c.t, f=slice(y2c.f0 + fh * 512, y2c.f0 + (fh + 1) * 512)),
                            pr, eng=("dve" if fh == 0 else "act"))
                r0 = tt_ * 128
                fw.dram_store("pool", acc_d[r0:r0 + 128, 0:EMB], "acc", y2c,
                              region=dreg("acc", ACCSH, slice(r0, r0 + 128), slice(0, EMB)),
                              accum=ALU.add)

            for tt_ in range(nt + 2):
                if tt_ < nt:
                    ret_build(tt_)
                if 1 <= tt_ <= nt:
                    ret_tr(tt_ - 1)
                if tt_ >= 2:
                    ret_mm(tt_ - 2)

        ssum2 = sb("ssum2", [128, 1], F32)
        ssq2 = sb("ssq2", [128, 1], F32)
        mmt2 = negm2
        rstd2b = rs1b
        mneg2b = rs2b
        accp = [[xf.v(f=slice(0, 512)), xf.v(f=slice(512, 1024))],
                [xf.v(f=slice(1024, 1536)), xf.v(f=slice(1536, 2048))]]

        def p7_load(tt_):
            a0, a1 = accp[tt_ % 2]
            fw.dram_load("sp", a0, acc_d[tt_ * 128:(tt_ + 1) * 128, 0:512], "acc",
                         region=dreg("acc", ACCSH, slice(tt_ * 128, (tt_ + 1) * 128), slice(0, 512)))
            fw.dram_load("sp", a1, acc_d[tt_ * 128:(tt_ + 1) * 128, 512:1024], "acc",
                         region=dreg("acc", ACCSH, slice(tt_ * 128, (tt_ + 1) * 128), slice(512, 1024)))

        p7_load(0)
        for tt_ in range(nt):
            if tt_ + 1 < nt:
                p7_load(tt_ + 1)
            buf = s2tt[tt_ % 2]
            ob = outtm[tt_ % 2]
            a0, a1 = accp[tt_ % 2]
            ss_ = ssum if tt_ % 2 == 0 else ssum2
            sq_ = ssq if tt_ % 2 == 0 else ssq2
            mm_ = mmt if tt_ % 2 == 0 else mmt2
            rs_ = rstd2 if tt_ % 2 == 0 else rstd2b
            mn_ = mneg2 if tt_ % 2 == 0 else mneg2b
            for c in range(8):
                bank = pb_a if c < 4 else pb_b
                fw.mm(bank.v(f=slice((c % 4) * 128, (c % 4) * 128 + 128)),
                      xb.v(f=slice(c * S + tt_ * 128, c * S + (tt_ + 1) * 128)), idb,
                      c % 4 == 0, c % 4 == 3)
                if c == 3:
                    fw.tt(ALU.add, buf.v(f=slice(0, 512)), pb_a, a0)
                if c == 7:
                    fw.tt(ALU.add, buf.v(f=slice(512, 1024)), pb_b, a1)
            fw.act(sqtt, buf, AF.Square, accum=sq_)
            fw.red(ALU.add, ss_, buf)
            fw.ts(ss_, ss_, 1.0 / EMB, ALU.mult)
            fw.ts(sq_, sq_, 1.0 / EMB, ALU.mult)
            fw.tt(ALU.mult, mm_, ss_, ss_)
            fw.tt(ALU.subtract, sq_, sq_, mm_)
            fw.act(mm_, sq_, AF.Sqrt, bias=eps_c)
            fw.recip(rs_, mm_)
            fw.stt(mn_, ss_, -1.0, rs_, ALU.mult, ALU.mult)
            fw.act(ob, buf, AF.Identity, bias=mn_, scale=rs_)
            fw.store("act", out_d[tt_ * 128:(tt_ + 1) * 128, :], ob)

        fw.finish()
        fw.build()
    return nc


def kernel(**inputs):
    shared, xts = host_prep(inputs)
    nc = build_core(2048)
    in_maps = [dict(shared, xT=xts[b]) for b in range(8)]
    res = run_bass_kernel_spmd(nc, in_maps, core_ids=list(range(8)))
    return np.stack([np.asarray(r["out"], np.float32) for r in res.results], 0)


# version 31 (final)
# speedup vs baseline: 1.0267x; 1.0026x over previous
import contextlib
import numpy as np
import concourse.bass as bass
import concourse.mybir as mybir
from concourse.bass_utils import run_bass_kernel_spmd

F32 = mybir.dt.float32
BF16 = mybir.dt.bfloat16
AF = mybir.ActivationFunctionType
ALU = mybir.AluOpType
AX = mybir.AxisListType

ENGINES = ("pe", "act", "dve", "pool", "sp")
DMA_QUEUES = ("sp", "pool", "act")


class TV:
    __slots__ = ("t", "p0", "p1", "f0", "f1")

    def __init__(self, t, p=None, f=None):
        self.t = t
        shape = t.shape
        self.p0, self.p1 = (p.start or 0, p.stop if p.stop is not None else shape[0]) if p else (0, shape[0])
        self.f0, self.f1 = (f.start or 0, f.stop if f.stop is not None else shape[1]) if f else (0, shape[1])

    @property
    def ap(self):
        return self.t[self.p0:self.p1, self.f0:self.f1]

    @property
    def name(self):
        return self.t.name

    def v(self, p=None, f=None):
        pp = slice(self.p0 + (p.start or 0), self.p0 + p.stop) if p else slice(self.p0, self.p1)
        ff = slice(self.f0 + (f.start or 0), self.f0 + f.stop) if f else slice(self.f0, self.f1)
        return TV(self.t, pp, ff)


def _overlap(a, b):
    return not (a[1] <= b[0] or b[1] <= a[0] or a[3] <= b[2] or b[3] <= a[2])


def _covers(a, b):
    return a[0] <= b[0] and a[1] >= b[1] and a[2] <= b[2] and a[3] >= b[3]


def _reg(tv):
    return (tv.p0, tv.p1, tv.f0, tv.f1)


class _DramReg:
    def __init__(self, name, shape=(1, 1)):
        self.name = name
        self.shape = list(shape)

    def __getitem__(self, _):
        return None


def dreg(name, shape, rows=None, cols=None):
    return TV(_DramReg(name, shape), rows, cols)


class FW:
    def __init__(self, nc, stack, n_dma_slots=4):
        self.nc = nc
        self.stack = stack
        self.q = {e: [] for e in ENGINES}
        self.sems = {}
        self.cnt = {}
        self.waited = {e: {} for e in ENGINES}
        for e in ENGINES:
            self._mksem("s_" + e)
        self.dma_slots = {}
        self.dma_rr = {}
        self.slot_uses = {}
        for qn in DMA_QUEUES:
            names = []
            for i in range(n_dma_slots):
                sn = f"dq_{qn}_{i}"
                self._mksem(sn)
                names.append(sn)
                self.slot_uses[sn] = 0
            self.dma_slots[qn] = names
            self.dma_rr[qn] = 0
        self.writes = {}
        self.reads = {}
        self.pe_pend = []
        self.psum_bank_state = {}
        self.psum_width = {}
        self.n_instr = 0

    def reg_psum(self, t, dtype):
        self.psum_width[t.name] = 512 if dtype == F32 else 1024

    def _mksem(self, name):
        self.sems[name] = self.stack.enter_context(self.nc.semaphore(name))
        self.cnt[name] = 0

    def _edge(self, eng, sem_name, val):
        if val <= 0:
            return
        if self.waited[eng].get(sem_name, 0) >= val:
            return
        self.waited[eng][sem_name] = val
        s = self.sems[sem_name]
        self.q[eng].append(lambda E, s=s, v=val: E.wait_ge(s, v))

    def barrier(self):
        for e in ENGINES:
            for sn, v in self.cnt.items():
                self._edge(e, sn, v)

    def _pend_guard(self, eng, tv, writing):
        if eng == "pe" or not self.pe_pend:
            return
        reg = _reg(tv)
        for kind, nm, r in self.pe_pend:
            if nm == tv.name and (writing or kind == "w") and _overlap(reg, r):
                raise RuntimeError(
                    f"{eng} touches {tv.name}{reg} while an open PE accumulation "
                    f"group still {'reads' if kind == 'r' else 'writes'} {r} — close "
                    f"the group (stop=True) first")

    def _breg(self, tv):
        w = self.psum_width.get(tv.name)
        if w is None:
            return _reg(tv)
        return (tv.p0, tv.p1, (tv.f0 // w) * w, ((tv.f1 + w - 1) // w) * w)

    def _haz_read(self, eng, tv):
        self._pend_guard(eng, tv, False)
        reg = self._breg(tv)
        for r in self.writes.get(tv.name, ()):
            if _overlap(reg, r):
                self._edge(eng, r[4], r[5])

    def _haz_write(self, eng, tv):
        self._pend_guard(eng, tv, True)
        reg = self._breg(tv)
        for r in self.writes.get(tv.name, ()):
            if _overlap(reg, r):
                self._edge(eng, r[4], r[5])
        for r in self.reads.get(tv.name, ()):
            if _overlap(reg, r):
                self._edge(eng, r[4], r[5])

    def _note_read(self, name, reg, sem, val):
        lst = self.reads.setdefault(name, [])
        for r in lst:
            if r[4] == sem and r[0] == reg[0] and r[1] == reg[1] and r[2] == reg[2] and r[3] == reg[3]:
                r[5] = max(r[5], val)
                return
        lst.append([reg[0], reg[1], reg[2], reg[3], sem, val])

    def _note_write(self, name, reg, sem, val):
        wl = self.writes.setdefault(name, [])
        wl[:] = [r for r in wl if not _covers(reg, r)]
        wl.append([reg[0], reg[1], reg[2], reg[3], sem, val])
        rl = self.reads.get(name)
        if rl:
            rl[:] = [r for r in rl if not _covers(reg, r)]

    def _emit(self, eng, fn, reads, writes):
        for tv in reads:
            self._haz_read(eng, tv)
        for tv in writes:
            self._haz_write(eng, tv)
        sem = "s_" + eng
        self.cnt[sem] += 1
        val = self.cnt[sem]
        s = self.sems[sem]
        self.q[eng].append(lambda E, fn=fn, s=s: fn(E).then_inc(s, 1))
        for tv in reads:
            self._note_read(tv.name, self._breg(tv), sem, val)
        for tv in writes:
            self._note_write(tv.name, self._breg(tv), sem, val)
        self.n_instr += 1
        return val

    def _pe_flush(self, val):
        for kind, nm, reg in self.pe_pend:
            if kind == "r":
                self._note_read(nm, reg, "s_pe", val)
            else:
                self._note_write(nm, reg, "s_pe", val)
        self.pe_pend.clear()

    def _psum_base_guard(self, out_tv, base, is_group_start):
        w = self.psum_width.get(out_tv.name)
        assert w is not None, f"psum tensor {out_tv.name} not registered"
        for bank in range(out_tv.f0 // w, (out_tv.f1 + w - 1) // w):
            key = (out_tv.name, bank)
            st = self.psum_bank_state.get(key)
            if st is not None and st[0] != base:
                if st[1] is None:
                    raise RuntimeError("mixed-base write into bank of an open PE group")
                self._edge("pe", "s_pe", st[1])
            self.psum_bank_state[key] = (base, None)

    def _psum_close(self, out_tvs, val):
        for out_tv in out_tvs:
            w = self.psum_width[out_tv.name]
            for bank in range(out_tv.f0 // w, (out_tv.f1 + w - 1) // w):
                key = (out_tv.name, bank)
                base = self.psum_bank_state[key][0]
                self.psum_bank_state[key] = (base, val)

    def mm(self, out_tv, lhsT_tv, rhs_tv, start, stop, skip_check=False):
        assert lhsT_tv.p0 == rhs_tv.p0, "lhsT/rhs partition bases must match"
        assert lhsT_tv.p1 - lhsT_tv.p0 == rhs_tv.p1 - rhs_tv.p0
        self._psum_base_guard(out_tv, lhsT_tv.p0, start)
        self._haz_read("pe", lhsT_tv)
        self._haz_read("pe", rhs_tv)
        self._haz_write("pe", out_tv)
        self.pe_pend.append(("r", lhsT_tv.name, self._breg(lhsT_tv)))
        self.pe_pend.append(("r", rhs_tv.name, self._breg(rhs_tv)))
        self.pe_pend.append(("w", out_tv.name, self._breg(out_tv)))
        o, l, r = out_tv.ap, lhsT_tv.ap, rhs_tv.ap
        if stop:
            sem = "s_pe"
            self.cnt[sem] += 1
            val = self.cnt[sem]
            s = self.sems[sem]
            self.q["pe"].append(lambda E, o=o, l=l, r=r, st=start, s=s, sk=skip_check:
                                E.matmul(o, l, r, start=st, stop=True, skip_group_check=sk).then_inc(s, 1))
            self._pe_flush(val)
            self._psum_close([out_tv], val)
        else:
            self.q["pe"].append(lambda E, o=o, l=l, r=r, st=start, sk=skip_check:
                                E.matmul(o, l, r, start=st, stop=False, skip_group_check=sk))
        self.n_instr += 1

    def transpose(self, out_tv, in_tv, ident_tv):
        self._psum_base_guard(out_tv, in_tv.p0, True)
        self._haz_read("pe", in_tv)
        self._haz_read("pe", ident_tv)
        self._haz_write("pe", out_tv)
        self.pe_pend.append(("r", in_tv.name, self._breg(in_tv)))
        self.pe_pend.append(("r", ident_tv.name, self._breg(ident_tv)))
        self.pe_pend.append(("w", out_tv.name, self._breg(out_tv)))
        o, i, idn = out_tv.ap, in_tv.ap, ident_tv.ap
        sem = "s_pe"
        self.cnt[sem] += 1
        val = self.cnt[sem]
        s = self.sems[sem]
        self.q["pe"].append(lambda E, o=o, i=i, idn=idn, s=s:
                            E.transpose(o, i, idn).then_inc(s, 1))
        self._pe_flush(val)
        self._psum_close([out_tv], val)
        self.n_instr += 1

    def act(self, out_tv, in_tv, func, bias=0.0, scale=1.0, accum=None):
        reads = [in_tv]
        writes = [out_tv]
        b = bias.ap if isinstance(bias, TV) else bias
        sc = scale.ap if isinstance(scale, TV) else scale
        if isinstance(bias, TV):
            reads.append(bias)
        if isinstance(scale, TV):
            reads.append(scale)
        acc_ap = None
        if accum is not None:
            writes.append(accum)
            acc_ap = accum.ap
        o, i = out_tv.ap, in_tv.ap
        self._emit("act", lambda E, o=o, i=i, f=func, b=b, sc=sc, a=acc_ap:
                   E.activation(o, i, f, bias=b, scale=sc, accum_out=a), reads, writes)

    def tt(self, op, out_tv, a_tv, b_tv, eng="dve"):
        o, a, b = out_tv.ap, a_tv.ap, b_tv.ap
        self._emit(eng, lambda E, o=o, a=a, b=b, op=op: E.tensor_tensor(o, a, b, op),
                   [a_tv, b_tv], [out_tv])

    def ts(self, out_tv, in_tv, s1, op0, s2=None, op1=None, eng="dve"):
        reads = [in_tv]
        a1 = s1.ap if isinstance(s1, TV) else s1
        if isinstance(s1, TV):
            reads.append(s1)
        a2 = s2.ap if isinstance(s2, TV) else s2
        if isinstance(s2, TV):
            reads.append(s2)
        kw = {}
        if op1 is not None:
            kw["op1"] = op1
        o, i = out_tv.ap, in_tv.ap
        self._emit(eng, lambda E, o=o, i=i, a1=a1, a2=a2, op0=op0, kw=kw:
                   E.tensor_scalar(o, i, a1, a2, op0, **kw), reads, [out_tv])

    def stt(self, out_tv, in0_tv, scalar, in1_tv, op0, op1, eng="dve"):
        reads = [in0_tv, in1_tv]
        sc = scalar.ap if isinstance(scalar, TV) else scalar
        if isinstance(scalar, TV):
            reads.append(scalar)
        o, i0, i1 = out_tv.ap, in0_tv.ap, in1_tv.ap
        self._emit(eng, lambda E, o=o, i0=i0, sc=sc, i1=i1, op0=op0, op1=op1:
                   E.scalar_tensor_tensor(o, i0, sc, i1, op0, op1), reads, [out_tv])

    def red(self, op, out_tv, in_tv, negate=False, eng="dve"):
        o, i = out_tv.ap, in_tv.ap
        self._emit(eng, lambda E, o=o, i=i, op=op, n=negate:
                   E.tensor_reduce(o, i, AX.X, op, negate=n), [in_tv], [out_tv])

    def copy(self, out_tv, in_tv, eng="dve"):
        o, i = out_tv.ap, in_tv.ap
        if eng == "act":
            self._emit(eng, lambda E, o=o, i=i: E.copy(o, i), [in_tv], [out_tv])
        else:
            self._emit(eng, lambda E, o=o, i=i: E.tensor_copy(o, i), [in_tv], [out_tv])

    def memset(self, tv, c, eng="dve"):
        a = tv.ap
        self._emit(eng, lambda E, a=a, c=c: E.memset(a, c), [], [tv])

    def recip(self, out_tv, in_tv):
        o, i = out_tv.ap, in_tv.ap
        self._emit("dve", lambda E, o=o, i=i: E.reciprocal(o, i), [in_tv], [out_tv])

    def _dma(self, queue, fn, read_tvs, write_tvs):
        slot_i = self.dma_rr[queue]
        self.dma_rr[queue] = (slot_i + 1) % len(self.dma_slots[queue])
        sn = self.dma_slots[queue][slot_i]
        self._edge(queue, sn, 16 * self.slot_uses[sn])
        for tv in read_tvs:
            self._haz_read(queue, tv)
        for tv in write_tvs:
            self._haz_write(queue, tv)
        self.slot_uses[sn] += 1
        self.cnt[sn] += 16
        val = self.cnt[sn]
        s = self.sems[sn]
        self.q[queue].append(lambda E, fn=fn, s=s: fn(E).then_inc(s, 16))
        for tv in read_tvs:
            self._note_read(tv.name, self._breg(tv), sn, val)
        for tv in write_tvs:
            self._note_write(tv.name, self._breg(tv), sn, val)
        self.n_instr += 1

    def load(self, queue, dst_tv, src_ap):
        d = dst_tv.ap
        self._dma(queue, lambda E, d=d, s=src_ap: E.dma_start(d, s), [], [dst_tv])

    def store(self, queue, dst_ap, src_tv):
        s = src_tv.ap
        self._dma(queue, lambda E, d=dst_ap, s=s: E.dma_start(d, s), [src_tv], [])

    def sb2sb(self, queue, dst_tv, src_tv):
        d, s = dst_tv.ap, src_tv.ap
        self._dma(queue, lambda E, d=d, s=s: E.dma_start(d, s), [src_tv], [dst_tv])

    def iscatter(self, out_ap, out_name, idx_tv, in_tv, bound, op, in_dram=None):
        from concourse.bass import IndirectOffsetOnAxis
        i = idx_tv.ap
        if in_dram is not None:
            s, s_tv = in_dram[0], TV(_DramReg(in_dram[1]))
        else:
            s, s_tv = in_tv.ap, in_tv
        fake = TV(_DramReg(out_name))

        def _go(E, o=out_ap, i=i, s=s, b=bound, op=op):
            try:
                return E.indirect_dma_start(o, IndirectOffsetOnAxis(i, 0), s, None,
                                            bounds_check=b, oob_is_err=False, compute_op=op)
            except Exception:
                print("ISCATTER-FAIL out=", o, "idx=", i, "in=", s, "bound=", b, "op=", op)
                raise
        self._dma("pool", _go,
                  [idx_tv, s_tv, fake] if op != ALU.bypass else [idx_tv, s_tv],
                  [fake])

    def igather(self, out_tv, in_ap, in_name, idx_tv, bound):
        from concourse.bass import IndirectOffsetOnAxis
        o, i = out_tv.ap, idx_tv.ap
        fake = TV(_DramReg(in_name))
        self._dma("pool", lambda E, o=o, i=i, s=in_ap, b=bound:
                  E.indirect_dma_start(o, None, s, IndirectOffsetOnAxis(i, 0),
                                       bounds_check=b, oob_is_err=False, compute_op=ALU.bypass),
                  [idx_tv, fake], [out_tv])

    def dram_store(self, queue, dst_ap, dst_name, src_tv, region=None, accum=None):
        s = src_tv.ap
        fake = region if region is not None else TV(_DramReg(dst_name))
        kw = {} if accum is None else {"accum_op": accum}
        reads = [src_tv] + ([fake] if accum is not None else [])
        self._dma(queue, lambda E, d=dst_ap, s=s, kw=kw: E.dma_start(d, s, **kw),
                  reads, [fake])

    def dram_load(self, queue, dst_tv, src_ap, src_name, region=None):
        d = dst_tv.ap
        fake = region if region is not None else TV(_DramReg(src_name))
        self._dma(queue, lambda E, d=d, s=src_ap: E.dma_start(d, s), [fake], [dst_tv])

    def finish(self):
        for qn in DMA_QUEUES:
            for sn in self.dma_slots[qn]:
                self._edge(qn, sn, self.cnt[sn])

    def build(self):
        with self.nc.Block() as blk:
            @blk.tensor
            def _(E):
                for f in self.q["pe"]:
                    f(E)

            @blk.scalar
            def _(E):
                for f in self.q["act"]:
                    f(E)

            @blk.vector
            def _(E):
                for f in self.q["dve"]:
                    f(E)

            @blk.gpsimd
            def _(E):
                for f in self.q["pool"]:
                    f(E)

            @blk.sync
            def _(E):
                for f in self.q["sp"]:
                    f(E)


EMB = 1024
NH = 16
KVH = 4
HD = 64
QL = 64
KVL = 32
FFI = 4096
NEXP = 8
EPS = 1e-5
ROPE_BASE = 10000.0
HA = [0, 1, 2, 3, 8, 9, 10, 11]
HB = [4, 5, 6, 7, 12, 13, 14, 15]
CAP = 768
RS = 1024
I32 = mybir.dt.int32
DBG_STAGE = 3
DBG_R = 9


def host_prep(inputs, S=2048):
    f32 = np.float32
    d = {}
    Wqa = np.asarray(inputs["L0_attn_Wqa"], f32)
    Wqb = np.asarray(inputs["L0_attn_Wqb"], f32)
    Wkva = np.asarray(inputs["L0_attn_Wkva"], f32)
    Wkvb = np.asarray(inputs["L0_attn_Wkvb"], f32)
    Wout = np.asarray(inputs["L0_attn_Wout"], f32)
    bout = np.asarray(inputs["L0_attn_bout"], f32)
    W1s = np.asarray(inputs["L0_ffn_w1_shared"], f32)
    W2s = np.asarray(inputs["L0_ffn_w2_shared"], f32)
    W1r = np.asarray(inputs["L0_ffn_w1_routed"], f32)
    W2r = np.asarray(inputs["L0_ffn_w2_routed"], f32)
    Wr = np.asarray(inputs["L0_ffn_w_router"], f32)

    d["wqa_t"] = np.ascontiguousarray(Wqa.reshape(8, 128, QL).transpose(1, 0, 2).reshape(128, 8 * QL))
    d["wkva_t"] = np.ascontiguousarray(Wkva.reshape(8, 128, KVL).transpose(1, 0, 2).reshape(128, 8 * KVL))

    qe_idx = np.zeros(EMB, np.int64)
    qo_idx = np.zeros(EMB, np.int64)
    for j in range(8):
        for b, hl in ((0, HA), (1, HB)):
            h = hl[j]
            for r in range(64):
                qe_idx[j * 128 + b * 64 + r] = h * 64 + 2 * (r % 32)
                qo_idx[j * 128 + b * 64 + r] = h * 64 + 2 * (r % 32) + 1
    d["wqed"] = np.ascontiguousarray(Wqb[:, qe_idx])
    d["wqod"] = np.ascontiguousarray(Wqb[:, qo_idx])

    ke_idx = np.zeros(KVH * HD, np.int64)
    ko_idx = np.zeros(KVH * HD, np.int64)
    v_idx = np.zeros(KVH * HD, np.int64)
    for kh in range(KVH):
        for r in range(64):
            ke_idx[kh * 64 + r] = kh * 128 + 4 * (r % 32)
            ko_idx[kh * 64 + r] = kh * 128 + 4 * (r % 32) + 2
            v_idx[kh * 64 + r] = kh * 128 + 2 * r + 1
    d["wked"] = np.ascontiguousarray(Wkvb[:, ke_idx])
    d["wkod"] = np.ascontiguousarray(Wkvb[:, ko_idx])
    d["wv"] = np.ascontiguousarray(Wkvb[:, v_idx])

    rowperm = np.zeros(EMB, np.int64)
    for j in range(8):
        for b, hl in ((0, HA), (1, HB)):
            h = hl[j]
            for dd in range(64):
                rowperm[j * 128 + b * 64 + dd] = h * 64 + dd
    Wout_p = Wout[rowperm, :]
    d["wout_t"] = np.ascontiguousarray(
        Wout_p.reshape(8, 128, 8, 128).transpose(1, 2, 0, 3).reshape(128, 8192))
    d["bout_t"] = np.ascontiguousarray(bout.reshape(8, 128).T)

    inv = ROPE_BASE ** (-(np.arange(32, dtype=np.float64) / 32.0))
    ang = np.outer(inv, np.arange(S, dtype=np.float64))
    cos = np.cos(ang).astype(f32)
    sin = np.sin(ang).astype(f32)
    t1 = np.concatenate([cos, sin, cos, sin], 0)
    t2 = np.concatenate([-sin, cos, -sin, cos], 0)
    d["t1"] = np.ascontiguousarray(t1)
    d["t2"] = np.ascontiguousarray(t2)
    tri = np.zeros((128, 128), f32)
    iu = np.triu_indices(128, 1)
    tri[iu] = -1e30
    d["trineg"] = tri
    d["identf"] = np.eye(128, dtype=f32)

    def tile_w1(w):
        return np.ascontiguousarray(w.reshape(8, 128, 32, 128).transpose(1, 2, 0, 3).reshape(128, 32768))

    def tile_w2(w):
        return np.ascontiguousarray(w.reshape(32, 128, 2, 512).transpose(1, 2, 0, 3).reshape(128, 32768))

    def blockify(w):
        return np.ascontiguousarray(w.reshape(128, 16, 2048).transpose(1, 0, 2).reshape(16 * 128, 2048))

    d["w1s_t"] = tile_w1(W1s)
    d["w2s_t"] = tile_w2(W2s)
    d["w1r_t"] = np.ascontiguousarray(np.stack([blockify(tile_w1(W1r[e])) for e in range(NEXP)], 0).reshape(NEXP * 16 * 128, 2048))
    d["w2r_t"] = np.ascontiguousarray(np.stack([blockify(tile_w2(W2r[e])) for e in range(NEXP)], 0).reshape(NEXP * 16 * 128, 2048))
    d["wr_t"] = np.ascontiguousarray(Wr.reshape(8, 128, NEXP).transpose(1, 0, 2).reshape(128, 8 * NEXP))

    d["triu_t"] = np.triu(np.ones((128, 128), f32))
    d["srange"] = np.ascontiguousarray(
        np.broadcast_to(np.arange(CAP // 2, dtype=f32)[None, :], (128, CAP // 2)))
    d["ebase"] = np.ascontiguousarray(
        np.broadcast_to(np.arange(NEXP, dtype=f32)[None, :] * CAP, (128, NEXP)))

    x = np.asarray(inputs["L0_attn_x"], f32)
    xts = [np.ascontiguousarray(x[b].T) for b in range(x.shape[0])]
    return d, xts


def build_core(S=2048):
    nt = S // 128
    ns = S // 512
    nc = bass.Bass("TRN2", target_bir_lowering=False)

    xT = nc.dram_tensor("xT", [EMB, S], F32, kind="ExternalInput").ap()
    wqa_t = nc.dram_tensor("wqa_t", [128, 8 * QL], F32, kind="ExternalInput").ap()
    wqed = nc.dram_tensor("wqed", [QL, EMB], F32, kind="ExternalInput").ap()
    wqod = nc.dram_tensor("wqod", [QL, EMB], F32, kind="ExternalInput").ap()
    wkva_t = nc.dram_tensor("wkva_t", [128, 8 * KVL], F32, kind="ExternalInput").ap()
    wked = nc.dram_tensor("wked", [KVL, 256], F32, kind="ExternalInput").ap()
    wkod = nc.dram_tensor("wkod", [KVL, 256], F32, kind="ExternalInput").ap()
    wv = nc.dram_tensor("wv", [KVL, 256], F32, kind="ExternalInput").ap()
    wout_t = nc.dram_tensor("wout_t", [128, 8192], F32, kind="ExternalInput").ap()
    bout_t = nc.dram_tensor("bout_t", [128, 8], F32, kind="ExternalInput").ap()
    t1d = nc.dram_tensor("t1", [128, S], F32, kind="ExternalInput").ap()
    t2d = nc.dram_tensor("t2", [128, S], F32, kind="ExternalInput").ap()
    trineg_d = nc.dram_tensor("trineg", [128, 128], F32, kind="ExternalInput").ap()
    identf_d = nc.dram_tensor("identf", [128, 128], F32, kind="ExternalInput").ap()
    w1s_t = nc.dram_tensor("w1s_t", [128, 32768], F32, kind="ExternalInput").ap()
    w2s_t = nc.dram_tensor("w2s_t", [128, 32768], F32, kind="ExternalInput").ap()
    w1r_t = nc.dram_tensor("w1r_t", [NEXP * 16 * 128, 2048], F32, kind="ExternalInput").ap()
    w2r_t = nc.dram_tensor("w2r_t", [NEXP * 16 * 128, 2048], F32, kind="ExternalInput").ap()
    wr_t = nc.dram_tensor("wr_t", [128, 8 * NEXP], F32, kind="ExternalInput").ap()
    out_d = nc.dram_tensor("out", [S, EMB], F32, kind="ExternalOutput").ap()
    triu_d = nc.dram_tensor("triu_t", [128, 128], F32, kind="ExternalInput").ap()
    srange_d = nc.dram_tensor("srange", [128, CAP // 2], F32, kind="ExternalInput").ap()
    ebase_d = nc.dram_tensor("ebase", [128, NEXP], F32, kind="ExternalInput").ap()
    xg_d = nc.dram_tensor("xg", [NEXP * CAP + 128, RS], BF16).ap()
    acc_d = nc.dram_tensor("acc", [S, EMB], F32).ap()
    w1s_b = nc.dram_tensor("w1s_b", [2048, 2048], BF16).ap()
    w2s_b = nc.dram_tensor("w2s_b", [2048, 2048], BF16).ap()

    with contextlib.ExitStack() as ctx:
        E = ctx.enter_context
        fw = FW(nc, ctx)

        def sb(name, shape, dt):
            return TV(E(nc.sbuf_tensor(name, shape, dt)))

        def psum(name, shape, dt):
            t = TV(E(nc.psum_tensor(name, shape, dt)))
            fw.reg_psum(t.t, dt)
            return t

        xf = sb("xf", [128, 8 * S], F32)
        xb = sb("xb", [128, 8 * S], BF16)
        wqa_b = sb("wqa_b", [128, 8 * QL], BF16)
        wqed_b = sb("wqed_b", [QL, EMB], BF16)
        wqod_b = sb("wqod_b", [QL, EMB], BF16)
        wkva_b = sb("wkva_b", [128, 8 * KVL], BF16)
        wked_b = sb("wked_b", [KVL, 256], BF16)
        wkod_b = sb("wkod_b", [KVL, 256], BF16)
        wv_b = sb("wv_b", [KVL, 256], BF16)
        wr_b = sb("wr_b", [128, 8 * NEXP], BF16)
        idf = sb("idf", [128, 128], F32)
        idb = sb("idb", [128, 128], BF16)
        trineg = sb("trineg_s", [128, 128], F32)
        bout_s = sb("bout_s", [128, 8], F32)
        onescol = sb("onescol", [128, 1], BF16)
        onesrow_f = sb("onesrow_f", [1, 128], F32)
        onesrow_b = sb("onesrow_b", [1, 128], BF16)
        negm = sb("negm", [128, 1], F32)
        rs1 = sb("rs1", [128, 1], F32)
        rs2 = sb("rs2", [128, 1], F32)
        rsum = sb("rsum", [128, 1], F32)
        rc = sb("rc", [128, 1], F32)
        m1_8 = sb("m1_8", [128, 1], F32)
        negm1 = sb("negm1", [128, 1], F32)
        m2_8 = sb("m2_8", [128, 1], F32)
        e2b = sb("e2b", [128, 1], F32)
        rc2 = sb("rc2", [128, 1], F32)
        eq8 = sb("eq8", [128, NEXP], F32)
        msk8 = sb("msk8", [128, NEXP], F32)
        sel8 = sb("sel8", [128, NEXP], F32)
        en8 = sb("en8", [128, NEXP], F32)
        ssum = sb("ssum", [128, 1], F32)
        ssq = sb("ssq", [128, 1], F32)
        mmt = sb("mmt", [128, 1], F32)
        rstd2 = sb("rstd2", [128, 1], F32)
        mneg2 = sb("mneg2", [128, 1], F32)
        eps_c = sb("eps_c", [128, 1], F32)
        triu_b = sb("triu_b", [128, 128], BF16)
        srange_s = sb("srange_s", [128, CAP // 2], F32)
        sshift = sb("sshift", [128, 1], F32)
        ebase_s = sb("ebase_s", [128, NEXP], F32)
        sel_b = sb("sel_b", [128, (S // 128) * NEXP], BF16)
        eq_b = sb("eq_b", [128, (S // 128) * NEXP], BF16)
        slotf_all = sb("slotf_all", [128, (S // 128) * NEXP], F32)
        sif = sb("sif", [128, 2 * (S // 128)], F32)
        sidx = sb("sidx", [128, 2 * (S // 128)], I32)


        AB = sb("AB", [128, 39040], BF16)
        AFa = sb("AFa", [128, 5760], F32)
        qx = AB.v(f=slice(0, 8 * S))
        kbf = AB.v(f=slice(8 * S, 10 * S))
        vtm = AB.v(f=slice(10 * S, 10 * S + 4 * (S // 128) * 64))
        xab = AB.v(p=slice(0, 64), f=slice(12 * S, 13 * S))
        xkb = AB.v(p=slice(0, 32), f=slice(13 * S, 14 * S))
        pbf = AB.v(f=slice(14 * S, 15 * S))
        pTs = AB.v(f=slice(15 * S, 16 * S))
        t1b = AB.v(f=slice(16 * S, 17 * S))
        t2b = AB.v(f=slice(17 * S, 18 * S))
        diag_rc = AB.v(f=slice(18 * S, 18 * S + 128))
        sbf = AB.v(f=slice(18 * S + 128, 18 * S + 640))
        sqb = AB.v(f=slice(18 * S + 640, 18 * S + 1152))
        woutsl = [AB.v(f=slice(18 * S + 1152, 18 * S + 1152 + 1024)),
                  AB.v(f=slice(15 * S, 15 * S + 1024))]
        y1e = AB.v(f=slice(0, 24576))
        x1e = AB.v(f=slice(24576, 24576 + 8 * CAP))
        y2sb = AB.v(f=slice(24576, 24576 + 8 * CAP))
        wst = [AB.v(f=slice(30720 + i * 2048, 30720 + (i + 1) * 2048)) for i in range(2)]
        xgt = [AB.v(f=slice(34816 + i * 1024, 34816 + (i + 1) * 1024)) for i in range(2)]
        ptw = AB.v(f=slice(36864, 36864 + CAP))
        rwsb = AB.v(f=slice(36864 + CAP, 36864 + 2 * CAP))

        ropes1 = AFa.v(f=slice(0, 512))
        ropes2 = AFa.v(f=slice(512, 1024))
        sdiag = AFa.v(f=slice(1024, 1152))
        st_sum = AFa.v(p=slice(0, 1), f=slice(1152, 1152 + S))
        st_sq = AFa.v(p=slice(0, 1), f=slice(1152 + S, 1152 + 2 * S))
        st_tmp = AFa.v(p=slice(0, 1), f=slice(1152 + 2 * S, 1152 + 2 * S + 512))
        scrf = AFa.v(f=slice(0, 512))
        cwb = AFa.v(f=slice(512, 1024))
        lg_sb = AFa.v(p=slice(0, NEXP), f=slice(1152, 1152 + S))
        lgtm = AFa.v(f=slice(1152 + S, 1152 + S + 8 * nt))
        cw_tm = AFa.v(f=slice(1152 + S + 8 * nt, 1152 + S + 16 * nt))
        ffn_tmp = AFa.v(f=slice(1152 + S + 16 * nt, 1664 + S + 16 * nt))
        totals_sb = AFa.v(p=slice(0, 1), f=slice(3968, 3968 + nt * 8))
        offs_sb = AFa.v(p=slice(0, 1), f=slice(4096, 4096 + nt * 8))
        csum_all = AFa.v(f=slice(4224, 4224 + nt * 8))
        slotw = AFa.v(f=slice(4352, 4360))
        slotm = AFa.v(f=slice(4360, 4368))
        y2h = [AFa.v(f=slice(4368 + i * 512, 4368 + (i + 1) * 512)) for i in range(2)]
        s2tt = [AFa.v(f=slice(0, 1024)), AFa.v(f=slice(1024, 2048))]
        sqtt = AFa.v(f=slice(3072, 4096))
        outtm = [AFa.v(f=slice(2048, 3072)), AFa.v(f=slice(4096, 5120))]
        accl0 = AFa.v(f=slice(3072, 3584))
        accl1 = AFa.v(f=slice(3584, 4096))

        ps_big = psum("ps_big", [128, 2048], F32)
        ps_big2 = psum("ps_big2", [128, 2048], F32)
        pb_a = TV(ps_big2.t, f=slice(0, 512))
        pb_b = TV(ps_big2.t, f=slice(512, 1024))
        po = TV(ps_big2.t, f=slice(1024, 1536))
        pt = TV(ps_big2.t, f=slice(1536, 2048))

        for c in range(8):
            fw.load("sp", xf.v(f=slice(c * S, (c + 1) * S)), xT[c * 128:(c + 1) * 128, :])
        fw.load("sp", idf, identf_d)
        fw.load("sp", trineg, trineg_d)
        fw.load("sp", bout_s, bout_t)
        fw.load("pool", idb, identf_d)
        fw.load("pool", t1b, t1d)
        fw.load("pool", t2b, t2d)
        fw.load("pool", wqa_b, wqa_t)
        fw.load("pool", wqed_b, wqed)
        fw.load("pool", wqod_b, wqod)
        fw.load("pool", wkva_b, wkva_t)
        fw.load("pool", wked_b, wked)
        fw.load("pool", wkod_b, wkod)
        fw.load("pool", wv_b, wv)
        fw.load("pool", wr_b, wr_t)
        fw.memset(onescol, 1.0)
        fw.memset(eps_c, EPS)
        fw.load("pool", triu_b, triu_d)
        fw.load("sp", srange_s, srange_d)
        fw.load("sp", ebase_s, ebase_d)
        fw.memset(xgt[0].v(f=slice(0, RS)), 0.0)
        for r0 in range(0, NEXP * CAP + 128, 128):
            fw.dram_store("act", xg_d[r0:r0 + 128, :], "xg", xgt[0].v(f=slice(0, RS)))
        fw.memset(onesrow_f, 1.0)
        fw.memset(onesrow_b, 1.0)
        fw.ts(wqa_b, wqa_b, 0.125, ALU.mult)
        for c in range(8):
            fw.copy(xb.v(f=slice(c * S, (c + 1) * S)), xf.v(f=slice(c * S, (c + 1) * S)))

        bnc = AFa.t.bitcast(BF16)
        WSH = [2048, 2048]
        cb_i = [0]

        def cast_block(src_ap, dst_ap, name, c0):
            blk = c0 // 2048
            i = cb_i[0] % 4
            cb_i[0] += 1
            trk = AFa.v(f=slice(1152 + i * 1024, 1152 + (i + 1) * 1024))
            bap = bnc[:, 2304 + i * 2048:2304 + (i + 1) * 2048]
            fw._dma("pool",
                    lambda E, d=bap, s=src_ap[0:128, c0:c0 + 2048]: E.dma_start(d, s),
                    [], [trk])
            reg = dreg(name, WSH, slice(blk * 128, (blk + 1) * 128), slice(0, 2048))
            fw._dma("sp",
                    lambda E, d=dst_ap[blk * 128:(blk + 1) * 128, :], s=bap: E.dma_start(d, s),
                    [trk], [reg])

        for c0 in range(0, 32768, 2048):
            cast_block(w1s_t, w1s_b, "w1sb", c0)
        for c0 in range(0, 32768, 2048):
            cast_block(w2s_t, w2s_b, "w2sb", c0)

        for t in range(ns):
            sl = slice(t * 512, (t + 1) * 512)
            pscur = TV(ps_big.t, p=slice(0, 64), f=sl)
            for k in range(8):
                fw.mm(pscur, TV(wqa_b.t, f=slice(k * 64, (k + 1) * 64)),
                      xb.v(f=slice(k * S + t * 512, k * S + (t + 1) * 512)),
                      start=(k == 0), stop=(k == 7))
            fw.copy(xab.v(f=sl), pscur)
        for t in range(ns):
            sl = slice(t * 512, (t + 1) * 512)
            pscur = TV(ps_big.t, p=slice(0, 32), f=sl)
            for k in range(8):
                fw.mm(pscur, TV(wkva_b.t, f=slice(k * 32, (k + 1) * 32)),
                      xb.v(f=slice(k * S + t * 512, k * S + (t + 1) * 512)),
                      start=(k == 0), stop=(k == 7))
            fw.copy(xkb.v(f=sl), pscur)

        for j in range(8):
            for t in range(ns):
                sl = slice(t * 512, (t + 1) * 512)
                psE = TV(ps_big.t, f=slice((t % 2) * 512, (t % 2) * 512 + 512))
                psO = TV(ps_big.t, f=slice(1024 + (t % 2) * 512, 1024 + (t % 2) * 512 + 512))
                fw.mm(psE, TV(wqed_b.t, f=slice(j * 128, (j + 1) * 128)), xab.v(f=sl), True, True)
                fw.mm(psO, TV(wqod_b.t, f=slice(j * 128, (j + 1) * 128)), xab.v(f=sl), True, True)
                fw.tt(ALU.mult, ropes1, psE, t1b.v(f=sl))
                fw.tt(ALU.mult, ropes2, psO, t2b.v(f=sl))
                fw.tt(ALU.add, qx.v(f=slice(j * S + t * 512, j * S + (t + 1) * 512)), ropes1, ropes2)
        for kc in range(2):
            for t in range(ns):
                sl = slice(t * 512, (t + 1) * 512)
                psE = TV(ps_big.t, f=slice((t % 2) * 512, (t % 2) * 512 + 512))
                psO = TV(ps_big.t, f=slice(1024 + (t % 2) * 512, 1024 + (t % 2) * 512 + 512))
                fw.mm(psE, TV(wked_b.t, f=slice(kc * 128, (kc + 1) * 128)), xkb.v(f=sl), True, True)
                fw.mm(psO, TV(wkod_b.t, f=slice(kc * 128, (kc + 1) * 128)), xkb.v(f=sl), True, True)
                fw.tt(ALU.mult, ropes1, psE, t1b.v(f=sl))
                fw.tt(ALU.mult, ropes2, psO, t2b.v(f=sl))
                fw.tt(ALU.add, kbf.v(f=slice(kc * S + t * 512, kc * S + (t + 1) * 512)), ropes1, ropes2)
        for kc in range(2):
            for t in range(ns):
                sl = slice(t * 512, (t + 1) * 512)
                pscur = TV(ps_big.t, f=slice((t % 4) * 512, (t % 4) * 512 + 512))
                fw.mm(pscur, TV(wv_b.t, f=slice(kc * 128, (kc + 1) * 128)), xkb.v(f=sl), True, True)
                fw.copy(pbf.v(f=sl), pscur)
            for b in range(2):
                kh = 2 * kc + b
                for tt_ in range(nt):
                    fw.mm(TV(pt.t, f=slice((tt_ % 8) * 64, (tt_ % 8) * 64 + 64)),
                          pbf.v(p=slice(b * 64, b * 64 + 64), f=slice(tt_ * 128, (tt_ + 1) * 128)),
                          TV(idb.t, p=slice(b * 64, b * 64 + 64), f=slice(b * 64, b * 64 + 64)),
                          True, True)
                    if tt_ % 8 == 7 or tt_ == nt - 1:
                        lo = (tt_ // 8) * 8
                        fw.copy(vtm.v(f=slice(kh * (nt * 64) + lo * 64, kh * (nt * 64) + (tt_ + 1) * 64)),
                                TV(pt.t, f=slice(0, (tt_ + 1 - lo) * 64)))

        negm2 = sb("negm2", [128, 1], F32)
        rs1b = sb("rs1b", [128, 1], F32)
        rs2b = sb("rs2b", [128, 1], F32)
        rsumb = sb("rsumb", [128, 1], F32)
        rcb = sb("rcb", [128, 1], F32)
        pbf2 = t1b
        pTs2 = t2b
        sdiag2 = AFa.v(f=slice(0, 128))
        drc2 = AB.v(f=slice(18 * S + 128, 18 * S + 256))
        groups = [ps_big, ps_big2]
        bufsets = [
            dict(pbf=pbf, pTs=pTs, sdiag=sdiag, negm=negm, rs1=rs1, rs2=rs2,
                 rsum=rsum, rc=rc, drc=diag_rc),
            dict(pbf=pbf2, pTs=pTs2, sdiag=sdiag2, negm=negm2, rs1=rs1b,
                 rs2=rs2b, rsum=rsumb, rc=rcb, drc=drc2),
        ]

        for j in range(8):
            for b_, hl in ((0, HA), (1, HB)):
                h = hl[j]
                rb = b_ * 64
                kh = h // 4
                kc = kh // 2

                def sc_stage(qt):
                    G = groups[qt % 2]
                    kspan = (qt + 1) * 128
                    qsl = qx.v(p=slice(rb, rb + 64), f=slice(j * S + qt * 128, j * S + (qt + 1) * 128))
                    for si in range((kspan + 511) // 512):
                        sl0 = si * 512
                        N = min(512, kspan - sl0)
                        fw.mm(G.v(f=slice(sl0, sl0 + N)), qsl,
                              kbf.v(p=slice(rb, rb + 64), f=slice(kc * S + sl0, kc * S + sl0 + N)),
                              True, True)

                def red_stage(qt):
                    G, B = groups[qt % 2], bufsets[qt % 2]
                    kspan = (qt + 1) * 128
                    nfull = qt * 128
                    fw.red(ALU.max, B["negm"], G.v(f=slice(0, kspan)), negate=True)
                    fw.tt(ALU.add, B["sdiag"], G.v(f=slice(nfull, kspan)), trineg)

                def exp_stage(qt):
                    G, B = groups[qt % 2], bufsets[qt % 2]
                    kspan = (qt + 1) * 128
                    nfull = qt * 128
                    if nfull > 0:
                        fw.act(B["pbf"].v(f=slice(0, nfull)), G.v(f=slice(0, nfull)),
                               AF.Exp, bias=B["negm"], accum=B["rs1"])
                    fw.act(B["pbf"].v(f=slice(nfull, kspan)), B["sdiag"], AF.Exp,
                           bias=B["negm"], accum=B["rs2"])

                def post_stage(qt):
                    B = bufsets[qt % 2]
                    if qt > 0:
                        fw.tt(ALU.add, B["rsum"], B["rs1"], B["rs2"])
                    else:
                        fw.copy(B["rsum"], B["rs2"])
                    fw.recip(B["rc"], B["rsum"])
                    fw.ts(B["drc"], idb, B["rc"], ALU.mult)

                def trpv_stage(qt):
                    G, B = groups[qt % 2], bufsets[qt % 2]
                    if qt <= 10:
                        tbs = [groups[(qt + 1) % 2].v(f=slice(1536, 2048)),
                               G.v(f=slice(1536, 2048))]
                        pocur = groups[(qt + 1) % 2].v(p=slice(0, 64), f=slice(1536, 1664))
                    else:
                        tbs = [G.v(f=slice(0, 512)), G.v(f=slice(512, 1024))]
                        pocur = G.v(p=slice(0, 64), f=slice(1024, 1152))
                    for g in range(0, qt + 1, 4):
                        ntile = min(4, qt + 1 - g)
                        tb = tbs[(g // 4) % 2]
                        for kt in range(g, g + ntile):
                            fw.mm(tb.v(f=slice((kt - g) * 128, (kt - g + 1) * 128)),
                                  B["pbf"].v(f=slice(kt * 128, (kt + 1) * 128)), B["drc"],
                                  kt == g, kt == g + ntile - 1)
                        fw.copy(B["pTs"].v(f=slice(g * 128, (g + ntile) * 128)),
                                tb.v(f=slice(0, ntile * 128)), eng="act")
                    for kt in range(qt + 1):
                        fw.mm(pocur, vtm.v(f=slice(kh * (nt * 64) + kt * 64, kh * (nt * 64) + (kt + 1) * 64)),
                              B["pTs"].v(f=slice(kt * 128, (kt + 1) * 128)),
                              start=(kt == 0), stop=(kt == qt))
                    fw.copy(qx.v(p=slice(rb, rb + 64), f=slice(j * S + qt * 128, j * S + (qt + 1) * 128)),
                            pocur, eng="dve")

                for qt in range(nt + 1):
                    if qt < nt:
                        sc_stage(qt)
                    if qt >= 1:
                        post_stage(qt - 1)
                    if qt < nt:
                        red_stage(qt)
                    if qt >= 1:
                        trpv_stage(qt - 1)
                    if qt < nt:
                        exp_stage(qt)

        for c in range(8):
            wsl = woutsl[c % 2]
            fw.load("pool", wsl, wout_t[:, c * 1024:(c + 1) * 1024])
            for t in range(ns):
                sl = slice(t * 512, (t + 1) * 512)
                pscur = TV(ps_big.t, f=sl)
                for k in range(8):
                    fw.mm(pscur, wsl.v(f=slice(k * 128, (k + 1) * 128)),
                          qx.v(f=slice(k * S + t * 512, k * S + (t + 1) * 512)),
                          start=(k == 0), stop=(k == 7))
                xfs = xf.v(f=slice(c * S + t * 512, c * S + (t + 1) * 512))
                fw.stt(xfs, pscur, TV(bout_s.t, f=slice(c, c + 1)), xfs, ALU.add, ALU.add)
        for t in range(ns):
            ss_row = st_sum.v(f=slice(t * 512, (t + 1) * 512))
            sq_row = st_sq.v(f=slice(t * 512, (t + 1) * 512))
            for c in range(8):
                pss = TV(ps_big.t, p=slice(0, 1), f=slice((c % 2) * 512, (c % 2) * 512 + 512))
                psq = TV(ps_big.t, p=slice(0, 1), f=slice(1024 + (c % 2) * 512, 1024 + (c % 2) * 512 + 512))
                xfs = xf.v(f=slice(c * S + t * 512, c * S + (t + 1) * 512))
                fw.copy(sbf, xfs, eng="act")
                fw.act(sqb, xfs, AF.Square)
                fw.mm(pss, onescol, sbf, True, True)
                fw.mm(psq, onescol, sqb, True, True)
                if c == 0:
                    fw.copy(ss_row, pss)
                    fw.copy(sq_row, psq)
                else:
                    fw.tt(ALU.add, ss_row, ss_row, pss)
                    fw.tt(ALU.add, sq_row, sq_row, psq)
        fw.ts(st_sum, st_sum, 1.0 / EMB, ALU.mult)
        fw.ts(st_sq, st_sq, 1.0 / EMB, ALU.mult)
        for hf in range(0, S, 512):
            w_ = min(512, S - hf)
            su = st_sum.v(f=slice(hf, hf + w_))
            sq = st_sq.v(f=slice(hf, hf + w_))
            tp = st_tmp.v(f=slice(0, w_))
            fw.tt(ALU.mult, tp, su, su)
            fw.tt(ALU.subtract, sq, sq, tp)
            fw.act(tp, sq, AF.Sqrt, bias=eps_c.v(p=slice(0, 1)))
            fw.recip(sq, tp)
            fw.stt(su, su, -1.0, sq, ALU.mult, ALU.mult)
        for t in range(ns):
            fw.mm(pb_a, onesrow_f,
                  TV(st_sq.t, p=slice(0, 1), f=slice(st_sq.f0 + t * 512, st_sq.f0 + (t + 1) * 512)), True, True)
            fw.mm(pb_b, onesrow_f,
                  TV(st_sum.t, p=slice(0, 1), f=slice(st_sum.f0 + t * 512, st_sum.f0 + (t + 1) * 512)), True, True)
            for c in range(8):
                xfs = xf.v(f=slice(c * S + t * 512, c * S + (t + 1) * 512))
                fw.tt(ALU.mult, scrf, xfs, pb_a)
                fw.tt(ALU.add, xb.v(f=slice(c * S + t * 512, c * S + (t + 1) * 512)), scrf, pb_b)

        for t in range(ns):
            psl = TV(ps_big.t, p=slice(0, NEXP), f=slice(t * 512, (t + 1) * 512))
            for k in range(8):
                fw.mm(psl, TV(wr_b.t, f=slice(k * NEXP, (k + 1) * NEXP)),
                      xb.v(f=slice(k * S + t * 512, k * S + (t + 1) * 512)),
                      start=(k == 0), stop=(k == 7))
            fw.copy(TV(lg_sb.t, p=slice(0, NEXP), f=slice(lg_sb.f0 + t * 512, lg_sb.f0 + (t + 1) * 512)), psl)
        for tt_ in range(nt):
            fw.mm(pb_a.v(f=slice((tt_ % 16) * 8, (tt_ % 16) * 8 + 8)),
                  lg_sb.v(f=slice(tt_ * 128, (tt_ + 1) * 128)),
                  TV(idf.t, p=slice(0, NEXP), f=slice(0, NEXP)), True, True)
            if tt_ % 16 == 15 or tt_ == nt - 1:
                lo = (tt_ // 16) * 16
                fw.copy(lgtm.v(f=slice(lo * 8, (tt_ + 1) * 8)),
                        pb_a.v(f=slice(0, (tt_ + 1 - lo) * 8)))
        for tt_ in range(nt):
            lgt = TV(lgtm.t, f=slice(lgtm.f0 + tt_ * 8, lgtm.f0 + (tt_ + 1) * 8))
            fw.red(ALU.max, m1_8, lgt)
            fw.ts(negm1, m1_8, -1.0, ALU.mult)
            fw.ts(eq8, lgt, m1_8, ALU.is_equal)
            fw.stt(msk8, eq8, -1e30, lgt, ALU.mult, ALU.add)
            fw.red(ALU.max, m2_8, msk8)
            fw.ts(sel8, lgt, m2_8, ALU.is_ge)
            fw.act(en8, lgt, AF.Exp, bias=negm1)
            fw.act(e2b, m2_8, AF.Exp, bias=negm1)
            fw.ts(e2b, e2b, 1.0, ALU.add)
            fw.recip(rc2, e2b)
            fw.tt(ALU.mult, en8, en8, sel8)
            fw.ts(TV(cw_tm.t, f=slice(cw_tm.f0 + tt_ * 8, cw_tm.f0 + (tt_ + 1) * 8)), en8, rc2, ALU.mult)
            fw.copy(sel_b.v(f=slice(tt_ * 8, (tt_ + 1) * 8)), sel8)
            fw.copy(eq_b.v(f=slice(tt_ * 8, (tt_ + 1) * 8)), eq8)

        def p5_slots():
            for tt_ in range(nt):
                selt = sel_b.v(f=slice(tt_ * 8, (tt_ + 1) * 8))
                fw.mm(pb_a.v(f=slice(0, 8)), triu_b, selt, True, True)
                fw.mm(pb_b.v(p=slice(0, 1), f=slice(0, 8)), onescol, selt, True, True)
                fw.copy(csum_all.v(f=slice(tt_ * 8, (tt_ + 1) * 8)), pb_a.v(f=slice(0, 8)))
                fw.copy(totals_sb.v(f=slice(tt_ * 8, (tt_ + 1) * 8)), pb_b.v(p=slice(0, 1), f=slice(0, 8)))
            fw.memset(offs_sb.v(f=slice(0, 8)), 0.0)
            for tt_ in range(1, nt):
                fw.tt(ALU.add, offs_sb.v(f=slice(tt_ * 8, (tt_ + 1) * 8)),
                      offs_sb.v(f=slice((tt_ - 1) * 8, tt_ * 8)),
                      totals_sb.v(f=slice((tt_ - 1) * 8, tt_ * 8)))
            DUM = float(NEXP * CAP)
            LDUM = 4096.0
            for tt_ in range(nt):
                fw.mm(pb_a.v(f=slice(0, 8)), onesrow_f, offs_sb.v(f=slice(tt_ * 8, (tt_ + 1) * 8)), True, True)
                fw.tt(ALU.add, slotw, csum_all.v(f=slice(tt_ * 8, (tt_ + 1) * 8)), pb_a.v(f=slice(0, 8)))
                fw.ts(slotw, slotw, -1.0, ALU.add)
                fw.ts(slotm, slotw, float(CAP - 1) + 0.5, ALU.is_le)
                fw.tt(ALU.mult, slotm, slotm, sel_b.v(f=slice(tt_ * 8, (tt_ + 1) * 8)))
                lf = slotf_all.v(f=slice(tt_ * 8, (tt_ + 1) * 8))
                fw.ts(lf, slotw, -LDUM, ALU.add)
                fw.tt(ALU.mult, lf, lf, slotm)
                fw.ts(lf, lf, LDUM, ALU.add)
                fw.tt(ALU.add, slotw, slotw, ebase_s)
                fw.ts(slotw, slotw, -DUM, ALU.add)
                fw.tt(ALU.mult, slotw, slotw, slotm)
                fw.ts(slotw, slotw, DUM, ALU.add)
                fw.tt(ALU.mult, slotm, slotw, eq_b.v(f=slice(tt_ * 8, (tt_ + 1) * 8)))
                fw.red(ALU.add, TV(sif.t, f=slice(tt_, tt_ + 1)), slotm)
                fw.tt(ALU.subtract, slotm, sel_b.v(f=slice(tt_ * 8, (tt_ + 1) * 8)),
                      eq_b.v(f=slice(tt_ * 8, (tt_ + 1) * 8)))
                fw.tt(ALU.mult, slotm, slotm, slotw)
                fw.red(ALU.add, TV(sif.t, f=slice(nt + tt_, nt + tt_ + 1)), slotm)
            fw.copy(TV(sidx.t, f=slice(0, 2 * nt)), TV(sif.t, f=slice(0, 2 * nt)))

        def p5b(tt_):
            x1row = xgt[tt_ % 2]
            for c in range(8):
                bank = pb_a if c < 4 else pb_b
                fw.mm(bank.v(f=slice((c % 4) * 128, (c % 4) * 128 + 128)),
                      xb.v(f=slice(c * S + tt_ * 128, c * S + (tt_ + 1) * 128)), idb,
                      c % 4 == 0, c % 4 == 3)
                if c == 3:
                    fw.copy(x1row.v(f=slice(0, 512)), pb_a, eng="act")
                if c == 7:
                    fw.copy(x1row.v(f=slice(512, 1024)), pb_b, eng="act")
            if DBG_STAGE >= 1:
                fw.iscatter(xg_d[:, :], "xg", TV(sidx.t, f=slice(tt_, tt_ + 1)),
                            x1row.v(f=slice(0, RS)), NEXP * CAP, ALU.bypass)
                fw.iscatter(xg_d[:, :], "xg", TV(sidx.t, f=slice(nt + tt_, nt + tt_ + 1)),
                            x1row.v(f=slice(0, RS)), NEXP * CAP, ALU.bypass)

        wrot = [0]

        def wslot():
            s_ = wst[wrot[0] % 2]
            wrot[0] += 1
            return s_

        psum6 = [TV(ps_big.t, f=slice(0, 512)), TV(ps_big.t, f=slice(512, 1024)),
                 TV(ps_big.t, f=slice(1024, 1536)), TV(ps_big.t, f=slice(1536, 2048)),
                 pb_a, pb_b]
        ACCSH = [S, EMB]

        def shared_f1(t):
            for bi in range(16):
                wsl = wslot()
                fw.dram_load("pool", wsl, w1s_b[bi * 128:(bi + 1) * 128, :], "w1sb",
                             region=dreg("w1sb", WSH, slice(bi * 128, (bi + 1) * 128),
                                         slice(0, 2048)))
                for s_ in range(2):
                    m1 = bi * 2 + s_
                    pscur = TV(ps_big.t, f=slice((m1 % 4) * 512, (m1 % 4) * 512 + 512))
                    for k in range(8):
                        fw.mm(pscur, wsl.v(f=slice(s_ * 1024 + k * 128, s_ * 1024 + (k + 1) * 128)),
                              xb.v(f=slice(k * S + t * 512, k * S + (t + 1) * 512)),
                              start=(k == 0), stop=(k == 7))
                    fw.act(y1e.v(f=slice(m1 * 512, (m1 + 1) * 512)), pscur, AF.Silu)

        def shared_f2(t):
            for fh in range(2):
                for bi in range(8):
                    wsl = wslot()
                    blk_ = fh * 8 + bi
                    fw.dram_load("pool", wsl, w2s_b[blk_ * 128:(blk_ + 1) * 128, :], "w2sb",
                                 region=dreg("w2sb", WSH, slice(blk_ * 128, (blk_ + 1) * 128),
                                             slice(0, 2048)))
                    for kk in range(4):
                        k2 = bi * 4 + kk
                        for tile in range(4):
                            fw.mm(psum6[tile], y1e.v(f=slice(k2 * 512 + tile * 128, k2 * 512 + (tile + 1) * 128)),
                                  wsl.v(f=slice(kk * 512, (kk + 1) * 512)),
                                  start=(k2 == 0), stop=(kk == 3), skip_check=(0 < k2 < 31))
                for tile in range(4):
                    fw.copy(y2h[tile % 2], psum6[tile])
                    r0 = t * 512 + tile * 128
                    fw.dram_store("act", acc_d[r0:r0 + 128, fh * 512:(fh + 1) * 512],
                                  "acc", y2h[tile % 2],
                                  region=dreg("acc", ACCSH, slice(r0, r0 + 128),
                                              slice(fh * 512, (fh + 1) * 512)))

        if ns > 1:
            shared_f1(0)
            shared_f2(0)
            shared_f1(1)
            shared_f2(1)
            p5_slots()
            for x in range(nt):
                p5b(x)
            for t in range(2, ns):
                shared_f1(t)
                shared_f2(t)
        else:
            shared_f1(0)
            p5_slots()
            shared_f2(0)
            for x in range(nt):
                p5b(x)

        NTE = CAP // 128
        pre_w1 = [None, None]

        def wload(wsl_, src_slice):
            fw.load("pool", wsl_, src_slice)

        def prefetch_w1(e_):
            for b2 in range(2):
                wsl_ = wslot()
                r0_ = (e_ * 16 + b2) * 128
                wload(wsl_, w1r_t[r0_:r0_ + 128, :])
                pre_w1[b2] = wsl_

        if DBG_STAGE >= 0 and DBG_R >= 9:
            prefetch_w1(0)
        for e in range(NEXP if DBG_STAGE >= 0 else 0):
            for tile in range(NTE if DBG_R >= 1 else 0):
                xslot = xgt[tile % 2]
                fw.dram_load("sp", xslot.v(f=slice(0, RS)),
                             xg_d[e * CAP + tile * 128:e * CAP + (tile + 1) * 128, :], "xg")
                for c in range(8 if DBG_R >= 6 else 0):
                    bank = pb_a if c < 4 else pb_b
                    fw.mm(bank.v(f=slice((c % 4) * 128, (c % 4) * 128 + 128)),
                          xslot.v(f=slice(c * 128, (c + 1) * 128)), idb,
                          c % 4 == 0, c % 4 == 3)
                    if c == 3 and DBG_R >= 7:
                        for cc in range(4):
                            fw.copy(x1e.v(f=slice(cc * CAP + tile * 128, cc * CAP + (tile + 1) * 128)),
                                    pb_a.v(f=slice(cc * 128, (cc + 1) * 128)))
                    if c == 7 and DBG_R >= 7:
                        for cc in range(4):
                            fw.copy(x1e.v(f=slice((cc + 4) * CAP + tile * 128, (cc + 4) * CAP + (tile + 1) * 128)),
                                    pb_b.v(f=slice(cc * 128, (cc + 1) * 128)))
            for bi in range(16 if DBG_R >= 9 else 0):
                if bi < 2:
                    wsl = pre_w1[bi]
                else:
                    wsl = wslot()
                    r0_ = (e * 16 + bi) * 128
                    wload(wsl, w1r_t[r0_:r0_ + 128, :])
                for s_ in range(2):
                    m1 = bi * 2 + s_
                    ps0 = TV(ps_big.t, f=slice((m1 % 2) * 1024, (m1 % 2) * 1024 + 512))
                    ps1 = TV(ps_big.t, f=slice((m1 % 2) * 1024 + 512, (m1 % 2) * 1024 + 512 + 256))
                    for k in range(8):
                        lh = wsl.v(f=slice(s_ * 1024 + k * 128, s_ * 1024 + (k + 1) * 128))
                        fw.mm(ps0, lh, x1e.v(f=slice(k * CAP, k * CAP + 512)),
                              start=(k == 0), stop=(k == 7))
                        fw.mm(ps1, lh, x1e.v(f=slice(k * CAP + 512, k * CAP + 768)),
                              start=(k == 0), stop=(k == 7))
                    fw.act(y1e.v(f=slice(m1 * CAP, m1 * CAP + 512)), ps0, AF.Silu)
                    fw.act(y1e.v(f=slice(m1 * CAP + 512, m1 * CAP + 768)), ps1, AF.Silu)
            for fh in range(2 if DBG_R >= 9 else 0):
                for bi in range(8):
                    wsl = wslot()
                    r0_ = (e * 16 + fh * 8 + bi) * 128
                    wload(wsl, w2r_t[r0_:r0_ + 128, :])
                    for kk in range(4):
                        k2 = bi * 4 + kk
                        for tile in range(NTE):
                            fw.mm(psum6[tile], y1e.v(f=slice(k2 * CAP + tile * 128, k2 * CAP + (tile + 1) * 128)),
                                  wsl.v(f=slice(kk * 512, (kk + 1) * 512)),
                                  start=(k2 == 0), stop=(kk == 3), skip_check=(0 < k2 < 31))
                for tile in range(NTE):
                    fw.copy(y2sb.v(f=slice(tile * 1024 + fh * 512, tile * 1024 + (fh + 1) * 512)),
                            psum6[tile], eng="act")
            if e + 1 < NEXP and DBG_R >= 9:
                prefetch_w1(e + 1)
            ptws = [ptw, AB.v(f=slice(30 * CAP, 31 * CAP))]
            rwsbs = [rwsb, AB.v(f=slice(31 * CAP, 32 * CAP)),
                     AB.v(f=slice(29 * CAP, 30 * CAP))]
            trbanks = [(pb_a, pb_b), (TV(ps_big.t, f=slice(0, 512)), TV(ps_big.t, f=slice(512, 1024)))]
            prs2 = [(po, pt),
                    (TV(ps_big.t, f=slice(1024, 1536)), TV(ps_big.t, f=slice(1536, 2048)))]
            y2cs = [TV(y2h[0].t, f=slice(y2h[0].f0, y2h[0].f0 + 1024)),
                    AFa.v(f=slice(0, 1024))]

            def ret_build(tt_):
                p_ = ptws[tt_ % 2]
                slotc = TV(slotf_all.t, f=slice(tt_ * 8 + e, tt_ * 8 + e + 1))
                fw.ts(p_.v(f=slice(0, CAP // 2)), srange_s, slotc, ALU.is_equal)
                fw.ts(sshift, slotc, -float(CAP // 2), ALU.add)
                fw.ts(p_.v(f=slice(CAP // 2, CAP)), srange_s, sshift, ALU.is_equal)
                fw.ts(p_.v(f=slice(0, CAP)), p_.v(f=slice(0, CAP)),
                      TV(cw_tm.t, f=slice(cw_tm.f0 + tt_ * 8 + e, cw_tm.f0 + tt_ * 8 + e + 1)), ALU.mult)

            def ret_tr(tt_):
                p_ = ptws[tt_ % 2]
                r_ = rwsbs[tt_ % 3]
                bA, bB = trbanks[tt_ % 2]
                for st in range(NTE):
                    bank = bA if st < 4 else bB
                    fw.mm(bank.v(f=slice((st % 4) * 128, (st % 4) * 128 + 128)),
                          p_.v(f=slice(st * 128, (st + 1) * 128)), idb,
                          st % 4 == 0, st % 4 == 3 or st == NTE - 1)
                    if st == 3:
                        fw.copy(r_.v(f=slice(0, 512)), bA)
                    if st == 5:
                        fw.copy(r_.v(f=slice(512, 768)), bB.v(f=slice(0, 256)), eng="act")

            def ret_mm(tt_):
                r_ = rwsbs[tt_ % 3]
                y2c = y2cs[tt_ % 2]
                for fh in range(2):
                    pr = prs2[tt_ % 2][fh]
                    for st in range(NTE):
                        fw.mm(pr, r_.v(f=slice(st * 128, (st + 1) * 128)),
                              y2sb.v(f=slice(st * 1024 + fh * 512, st * 1024 + (fh + 1) * 512)),
                              start=(st == 0), stop=(st == NTE - 1))
                    fw.copy(TV(y2c.t, f=slice(y2c.f0 + fh * 512, y2c.f0 + (fh + 1) * 512)),
                            pr, eng=("dve" if fh == 0 else "act"))
                r0 = tt_ * 128
                fw.dram_store("pool", acc_d[r0:r0 + 128, 0:EMB], "acc", y2c,
                              region=dreg("acc", ACCSH, slice(r0, r0 + 128), slice(0, EMB)),
                              accum=ALU.add)

            for tt_ in range(nt + 2):
                if tt_ < nt:
                    ret_build(tt_)
                if 1 <= tt_ <= nt:
                    ret_tr(tt_ - 1)
                if tt_ >= 2:
                    ret_mm(tt_ - 2)

        ssum2 = sb("ssum2", [128, 1], F32)
        ssq2 = sb("ssq2", [128, 1], F32)
        mmt2 = negm2
        rstd2b = rs1b
        mneg2b = rs2b
        accp = [[xf.v(f=slice(0, 512)), xf.v(f=slice(512, 1024))],
                [xf.v(f=slice(1024, 1536)), xf.v(f=slice(1536, 2048))]]

        def p7_load(tt_):
            a0, a1 = accp[tt_ % 2]
            fw.dram_load("sp", a0, acc_d[tt_ * 128:(tt_ + 1) * 128, 0:512], "acc",
                         region=dreg("acc", ACCSH, slice(tt_ * 128, (tt_ + 1) * 128), slice(0, 512)))
            fw.dram_load("sp", a1, acc_d[tt_ * 128:(tt_ + 1) * 128, 512:1024], "acc",
                         region=dreg("acc", ACCSH, slice(tt_ * 128, (tt_ + 1) * 128), slice(512, 1024)))

        p7_load(0)
        for tt_ in range(nt):
            if tt_ + 1 < nt:
                p7_load(tt_ + 1)
            buf = s2tt[tt_ % 2]
            ob = outtm[tt_ % 2]
            a0, a1 = accp[tt_ % 2]
            ss_ = ssum if tt_ % 2 == 0 else ssum2
            sq_ = ssq if tt_ % 2 == 0 else ssq2
            mm_ = mmt if tt_ % 2 == 0 else mmt2
            rs_ = rstd2 if tt_ % 2 == 0 else rstd2b
            mn_ = mneg2 if tt_ % 2 == 0 else mneg2b
            for c in range(8):
                bank = pb_a if c < 4 else pb_b
                fw.mm(bank.v(f=slice((c % 4) * 128, (c % 4) * 128 + 128)),
                      xb.v(f=slice(c * S + tt_ * 128, c * S + (tt_ + 1) * 128)), idb,
                      c % 4 == 0, c % 4 == 3)
                if c == 3:
                    fw.tt(ALU.add, buf.v(f=slice(0, 512)), pb_a, a0)
                if c == 7:
                    fw.tt(ALU.add, buf.v(f=slice(512, 1024)), pb_b, a1)
            fw.act(sqtt, buf, AF.Square, accum=sq_)
            fw.red(ALU.add, ss_, buf)
            fw.ts(ss_, ss_, 1.0 / EMB, ALU.mult)
            fw.ts(sq_, sq_, 1.0 / EMB, ALU.mult)
            fw.tt(ALU.mult, mm_, ss_, ss_)
            fw.tt(ALU.subtract, sq_, sq_, mm_)
            fw.act(mm_, sq_, AF.Sqrt, bias=eps_c)
            fw.recip(rs_, mm_)
            fw.stt(mn_, ss_, -1.0, rs_, ALU.mult, ALU.mult)
            fw.act(ob, buf, AF.Identity, bias=mn_, scale=rs_)
            fw.store("act", out_d[tt_ * 128:(tt_ + 1) * 128, :], ob)

        fw.finish()
        fw.build()
    return nc


def kernel(**inputs):
    shared, xts = host_prep(inputs)
    nc = build_core(2048)
    in_maps = [dict(shared, xT=xts[b]) for b in range(8)]
    res = run_bass_kernel_spmd(nc, in_maps, core_ids=list(range(8)))
    return np.stack([np.asarray(r["out"], np.float32) for r in res.results], 0)
